# version 06
# baseline (speedup 1.0000x reference)
import contextlib
import numpy as np
import concourse.bass as bass
import concourse.mybir as mybir
from concourse.bass_utils import run_bass_kernel_spmd

F32 = mybir.dt.float32
F16 = mybir.dt.float16
I32 = mybir.dt.int32
BF16 = mybir.dt.bfloat16
AF = mybir.ActivationFunctionType
ALU = mybir.AluOpType
AX = mybir.AxisListType

L, D, H, DK, MAXF, DH, SD, PD = 2048, 1024, 16, 64, 10, 4096, 16, 1072
NT, NC, TF = L // 128, D // 128, 1024
NTF = L // TF
NEG = -3.0e38
SENT = 2.5e38

_DEBUG = ()
_STAGE = 99


_NAME_N = [0]


def _un(n):
    _NAME_N[0] += 1
    return f"{n}_{_NAME_N[0]}"


class KB:
    def __init__(self, nc, block, es, dbg_out):
        self.nc, self.block, self.es, self.dbg_out = nc, block, es, dbg_out
        self.sems, self.cnt = {}, {}
        for name in ["sdw", "sdx", "st", "sv", "ss", "sg", "bar"]:
            self.sems[name] = es.enter_context(nc.semaphore(name))
            self.cnt[name] = 0
        from concourse.hw_specs import get_activation_tables
        self.tables = {}
        for idx, funcs in enumerate(get_activation_tables(nc.m.arch).values()):
            for f in funcs:
                self.tables.setdefault(f, idx)
        self.cur_table = None
        self.wq = []
        self._slot_rel = {}
        self._bar_snapshot = []

    def inc(self, name, n=1):
        self.cnt[name] += n
        return self.sems[name], n

    def tag(self, name):
        return self.cnt[name]

    def wait(self, eng, name, value=None):
        eng.wait_ge(self.sems[name], self.cnt[name] if value is None else value)

    def barrier(self, prev=None,
                nxt=("vector", "scalar", "tensor", "gpsimd")):
        targets = [(self.sems[n], self.cnt[n]) for n in ["sv", "ss", "st", "sg"]
                   if self.cnt[n] > 0]
        self._bar_snapshot = targets
        self._slot_rel = {}
        for e in nxt:
            @getattr(self.block, e)
            def _(eng):
                for sem, val in targets:
                    eng.wait_ge(sem, val)

    def act(self, scalar, out, in_, func, **kw):
        tid = self.tables[func]
        if tid != self.cur_table:
            scalar.preload_act_table(func)
            self.cur_table = tid
        return scalar.activation(out=out, in_=in_, func=func, **kw)

    def vgap(self, vector, n=1):
        for _ in range(n):
            vector.tensor_copy(out=self.JNK[:, 0:128], in_=self.JNK[:, 128:256])

    def sgap(self, scalar, n=1):
        for _ in range(n):
            scalar.activation(out=self.JNK[:, 0:128], in_=self.JNK[:, 128:256],
                              func=mybir.ActivationFunctionType.Copy)

    def stream(self, src_ap, dst_ap, slot):
        rel = self._slot_rel.get(slot)
        rels = [rel] if rel is not None else list(self._bar_snapshot)
        self.wq.append((src_ap, dst_ap, rels))
        self.cnt["sdw"] += 16
        return self.cnt["sdw"]

    def release(self, slot, sem_name):
        self._slot_rel[slot] = (self.sems[sem_name], self.cnt[sem_name])

    def new_phase_slots(self):
        self._slot_rel = {}

    def flush_stream(self):
        entries = list(self.wq)
        @self.block.sync
        def _(sync):
            for src, dst, rels in entries:
                for sem, val in rels:
                    sync.wait_ge(sem, val)
                sync.dma_start(out=dst, in_=src).then_inc(self.sems["sdw"], 16)
        self.wq = []

    def dump(self, name, src_ap):
        if name not in self.dbg_out:
            return
        dst = self.dbg_out[name]
        @self.block.scalar
        def _(scalar):
            self.wait(scalar, "sv")
            self.wait(scalar, "ss")
            self.wait(scalar, "st")
            p, f = src_ap.shape[0], src_ap.shape[-1]
            scalar.dma_start(out=dst[0:p, 0:f] if len(src_ap.shape) == 2 else dst,
                             in_=src_ap).then_inc(*self.inc("sdx", 16))
            self.wait(scalar, "sdx")


def build(debug=(), stage=99):
    nc = bass.Bass("TRN2", target_bir_lowering=False,
                   detect_race_conditions=False)
    es = contextlib.ExitStack()

    def din(name, shape):
        return nc.dram_tensor(name, shape, F32, kind="ExternalInput").ap()

    W = {"x": din("x", [L, D])}
    for n, sh in [("wqT", [D, D]), ("bq", [D]), ("wkT", [D, D]), ("bk", [D]),
                  ("wvT", [D, D]), ("bv", [D]), ("woT", [D, D]), ("bo", [D]),
                  ("winS", [D, PD]), ("binS", [PD]), ("alog", [SD]),
                  ("woutN", [D, D]), ("bout", [D])]:
        W[n] = din(n, sh)
    for l in range(2):
        W[f"w1T{l}"] = din(f"w1T{l}", [D, DH])
        W[f"cw{l}"] = din(f"cw{l}", [DH, 3])
        W[f"cb{l}"] = din(f"cb{l}", [DH])
        W[f"w2T{l}"] = din(f"w2T{l}", [DH, D])
    y_hbm = nc.dram_tensor("y", [L, D], F32, kind="ExternalOutput").ap()

    dbg_shapes = {
        "xn_fm": ([128, NC * L], BF16), "xq_fm": ([128, NC * L], BF16),
        "qs": ([16, L], F32), "sel": ([128, 2 * L], BF16),
        "selt": ([128, NT * 160], F32), "xng": ([128, NC * 160], BF16),
        "q8": ([128, 8 * 80], BF16), "kfm": ([128, NC * L], BF16),
        "vtm": ([128, NT * D], BF16), "attn_out": ([128, NT * D], BF16),
        "x_l0": ([128, NT * D], F32), "gz": ([128, NT * D], BF16),
        "attb": ([128, 2048], BF16), "outg": ([128, 8 * 128], BF16),
        "vmk": ([16, 10], F32), "ucol": ([16, 1], F32),
        "outall": ([128, 512], F32),
        "selfs": ([80, 2048], BF16), "full": ([128, NC * L], BF16),
        "hsil": ([128, 32 * 1024], BF16), "scb": ([128, 2 * 1024], F32),
        "b_t": ([16, L], F32), "c_t": ([16, L], F32), "st_t": ([16, L], F32),
        "ycol": ([128, 16], F32), "a_t": ([16, L], F32),
        "gzc": ([128, 16 * 1024], BF16), "y2c": ([128, 16 * 1024], BF16),
        "bcds": ([128, L], F32), "hq": ([128, 32 * TF], BF16),
        "stat": ([16, L], F32),
    }
    dbg_out = {}
    for name in debug:
        sh, dt = dbg_shapes[name]
        dbg_out[name] = nc.dram_tensor("dbg_" + name, sh, dt,
                                       kind="ExternalOutput").ap()

    XRES = es.enter_context(nc.sbuf_tensor(_un("XRES"), [128, NT * D], F32))
    IDENT = es.enter_context(nc.sbuf_tensor(_un("IDENT"), [128, 128], BF16))
    IDENTF = es.enter_context(nc.sbuf_tensor(_un("IDENTF"), [128, 128], F32))
    ONESF = es.enter_context(nc.sbuf_tensor(_un("ONESF"), [128, 128], F32))
    IOTA = es.enter_context(nc.sbuf_tensor(_un("IOTA"), [128, 128], F32))
    B16W = es.enter_context(nc.sbuf_tensor(_un("B16W"), [128, 48], BF16))
    MASKP = es.enter_context(nc.sbuf_tensor(_un("MASKP"), [128, 8], F32))
    E80W = es.enter_context(nc.sbuf_tensor(_un("E80W"), [16, 160], F32))
    ETMP = es.enter_context(nc.sbuf_tensor(_un("ETMP"), [16, 160], F32))
    COLS = es.enter_context(nc.sbuf_tensor(_un("COLS"), [128, 64], F32))
    BIASC = es.enter_context(nc.sbuf_tensor(_un("BIASC"), [128, 280], F32))
    CONSTV = es.enter_context(nc.sbuf_tensor(_un("CONSTV"), [128, 4], F32))
    JNK = es.enter_context(nc.sbuf_tensor(_un("JNK"), [128, 256], F32))
    _cvals = [1e-5, 1.0 / D, 1.0]
    for _i, _v in enumerate(_cvals):
        nc.const_aps.register(F32, _v, CONSTV[:, _i:_i + 1])

    ctx = {}
    with es, nc.Block() as block:
        kb = KB(nc, block, es, dbg_out)
        kb.JNK = JNK
        ctx.update(nc=nc, block=block, kb=kb, W=W, XRES=XRES, IDENT=IDENT,
                   IDENTF=IDENTF, ONESF=ONESF, IOTA=IOTA, B16W=B16W, E80W=E80W,
                   MASKP=MASKP, COLS=COLS, BIASC=BIASC, stage=stage)
        ctx["M_COL"] = lambda tc: COLS[:, tc:tc + 1]
        ctx["R_COL"] = lambda tc: COLS[:, 16 + tc:17 + tc]
        ctx["S_COL"] = lambda i: COLS[:, 32 + i:33 + i]
        ctx["bq_col"] = lambda c: BIASC[:, c:c + 1]
        ctx["bk_col"] = lambda c: BIASC[:, 8 + c:9 + c]
        ctx["cb_col"] = lambda l, c: BIASC[:, 16 + l * 32 + c:17 + l * 32 + c]
        ctx["cw_col"] = lambda l, k, c: BIASC[:, 80 + l * 96 + k * 32 + c:81 + l * 96 + k * 32 + c]
        ctx["ALOG_COL"] = BIASC[0:SD, 272:273]

        @block.scalar
        def _(scalar):
            for tc in range(NT):
                scalar.dma_start(out=XRES[:, tc * D:(tc + 1) * D],
                                 in_=W["x"][tc * 128:(tc + 1) * 128, :]
                                 ).then_inc(*kb.inc("sdx", 16))
            for vec, off in [(W["bq"], 0), (W["bk"], 8)]:
                scalar.dma_start(out=BIASC[:, off:off + 8],
                                 in_=vec.rearrange("(c p) -> p c", p=128),
                                 allow_slow_non_contiguous=True,
                                 ).then_inc(*kb.inc("sdx", 16))
            for l in range(2):
                scalar.dma_start(out=BIASC[:, 16 + l * 32:16 + (l + 1) * 32],
                                 in_=W[f"cb{l}"].rearrange("(c p) -> p c", p=128),
                                 allow_slow_non_contiguous=True,
                                 ).then_inc(*kb.inc("sdx", 16))
                scalar.dma_start(
                    out=BIASC[:, 80 + l * 96:80 + (l + 1) * 96].rearrange(
                        "p (k c) -> p k c", k=3),
                    in_=W[f"cw{l}"].rearrange("(c p) k -> p k c", p=128),
                    allow_slow_non_contiguous=True,
                ).then_inc(*kb.inc("sdx", 16))
            scalar.dma_start(out=BIASC[0:SD, 272:273], in_=W["alog"][:, None]
                             ).then_inc(*kb.inc("sdx", 16))
        x_dma_target = kb.tag("sdx")

        @block.gpsimd
        def _(gpsimd):
            gpsimd.iota(IOTA[:], pattern=[[-1, 128]], base=0, channel_multiplier=1,
                        allow_small_or_imprecise_dtypes=True
                        ).then_inc(*kb.inc("sg"))

        @block.vector
        def _(vector):
            kb.wait(vector, "sg")
            vector.tensor_single_scalar(out=IDENT[:], in_=IOTA[:], scalar=0.0,
                                        op=ALU.is_equal)
            vector.tensor_single_scalar(out=IDENTF[:], in_=IOTA[:], scalar=0.0,
                                        op=ALU.is_equal)
            vector.memset(ONESF[:], 1.0)
            vector.memset(JNK[:], 0.0)
            for _i, _v in enumerate(_cvals):
                vector.memset(CONSTV[:, _i:_i + 1], _v)
            vector.tensor_single_scalar(out=COLS[:, 32:33], in_=IOTA[:, 0:1],
                                        scalar=64.0, op=ALU.is_ge)
            kb.vgap(vector)
            _li = vector.tensor_scalar(out=COLS[:, 32:33], in0=COLS[:, 32:33],
                                 scalar1=1.0, scalar2=16.0, op0=ALU.mult,
                                 op1=ALU.add)
            _li.then_inc(*kb.inc("sv"))

        @block.gpsimd
        def _(gpsimd):
            kb.wait(gpsimd, "sv")
            gpsimd.iota(IOTA[:, 0:48], pattern=[[1, 48]], base=0,
                        channel_multiplier=0, allow_small_or_imprecise_dtypes=True
                        ).then_inc(*kb.inc("sg"))

        @block.vector
        def _(vector):
            kb.wait(vector, "sg")
            _li = vector.tensor_single_scalar(out=B16W[:], in_=IOTA[:, 0:48],
                                        scalar=COLS[:, 32:33], op=ALU.is_equal)
            _li.then_inc(*kb.inc("sv"))

        @block.gpsimd
        def _(gpsimd):
            kb.wait(gpsimd, "sv")
            gpsimd.iota(ETMP[:], pattern=[[1, 160]], base=0, channel_multiplier=-10,
                        allow_small_or_imprecise_dtypes=True)
            gpsimd.iota(IOTA[0:16, 100:110], pattern=[[1, 10]], base=0,
                        channel_multiplier=0, allow_small_or_imprecise_dtypes=True
                        ).then_inc(*kb.inc("sg"))

        @block.vector
        def _(vector):
            kb.wait(vector, "sg")
            vector.tensor_single_scalar(out=E80W[:], in_=ETMP[:], scalar=0.0,
                                        op=ALU.is_ge)
            vector.tensor_single_scalar(out=ETMP[:], in_=ETMP[:], scalar=10.0,
                                        op=ALU.is_lt)
            vector.tensor_mul(out=E80W[:], in0=E80W[:], in1=ETMP[:])
            for hl in range(8):
                vector.tensor_single_scalar(out=MASKP[:, hl:hl + 1],
                                            in_=IOTA[:, 127:128],
                                            scalar=float(10 * hl - 127), op=ALU.is_ge)
            for hl in range(8):
                vector.tensor_single_scalar(out=COLS[:, 56 + hl:57 + hl],
                                            in_=IOTA[:, 127:128],
                                            scalar=float(10 * hl + 10 - 127),
                                            op=ALU.is_lt)
            for hl in range(8):
                vector.tensor_mul(out=MASKP[:, hl:hl + 1],
                                  in0=MASKP[:, hl:hl + 1],
                                  in1=COLS[:, 56 + hl:57 + hl])
            _li = vector.memset(COLS[:, 56:64], 0.0)
            _li.then_inc(*kb.inc("sv"))

        sv0, sg0, ss0 = kb.tag("sv"), kb.tag("sg"), kb.tag("ss")
        for ename in ["vector", "scalar", "tensor", "gpsimd"]:
            @getattr(block, ename)
            def _(eng):
                eng.wait_ge(kb.sems["sdx"], x_dma_target)
                eng.wait_ge(kb.sems["sv"], sv0)
                eng.wait_ge(kb.sems["sg"], sg0)

        if stage >= 1:
            emit_layer(ctx, 0)
        elif "qs" in dbg_out:
            pass
        if stage >= 20:
            emit_layer(ctx, 1)

        kb.barrier(nxt=("scalar",))
        @block.scalar
        def _(scalar):
            for tc in range(NT):
                scalar.dma_start(out=y_hbm[tc * 128:(tc + 1) * 128, :],
                                 in_=XRES[:, tc * D:(tc + 1) * D]
                                 ).then_inc(*kb.inc("sdx", 16))
        out_target = kb.tag("sdx")
        kb.flush_stream()
        @block.sync
        def _(sync):
            sync.wait_ge(kb.sems["sdx"], out_target)

    return nc


def emit_layer(ctx, layer):
    nc, block, kb, stage = ctx["nc"], ctx["block"], ctx["kb"], ctx["stage"]
    XRES = ctx["XRES"]

    les = contextlib.ExitStack()
    C_AR = les.enter_context(nc.sbuf_tensor(_un(f"C{layer}"), [128, NC * L], BF16))
    D_AR = les.enter_context(nc.sbuf_tensor(_un(f"D{layer}"), [128, NT * D], BF16))
    with les:
        abes = contextlib.ExitStack()
        AB = abes.enter_context(nc.sbuf_tensor(_un(f"AB{layer}"), [128, NC * L], BF16))
        with abes:
            emit_ln(ctx, AB, layer)
            if layer == 0:
                kb.dump("xn_fm", AB[:, :])
                if stage < 2:
                    return
                emit_l0_attn(ctx, AB, C_AR, D_AR)
            else:
                emit_l1_attn(ctx, AB, C_AR, D_AR)
        if (layer == 0 and stage < 6) or (layer == 1 and stage < 26):
            return
        kb.dump("attn_out" if layer == 0 else "gz", D_AR[:, :])
        kb.barrier(nxt=("vector",))
        @block.vector
        def _(vector):
            for tc in range(NT):
                _li = vector.tensor_add(out=XRES[:, tc * D:(tc + 1) * D],
                                  in0=XRES[:, tc * D:(tc + 1) * D],
                                  in1=D_AR[:, tc * D:(tc + 1) * D])
            _li.then_inc(*kb.inc("sv"))
        if (layer == 0 and stage < 7) or (layer == 1 and stage < 27):
            kb.dump("x_l0", XRES[:, :])
            return
        emit_quant_ffn(ctx, layer, D_AR, C_AR)
        if layer == 0:
            kb.dump("x_l0", XRES[:, :])


def emit_ln(ctx, AB, layer):
    nc, block, kb = ctx["nc"], ctx["block"], ctx["kb"]
    XRES, IDENT = ctx["XRES"], ctx["IDENT"]
    COLS = ctx["COLS"]
    M_COL, R_COL, S_COL = ctx["M_COL"], ctx["R_COL"], ctx["S_COL"]
    NM_COL = lambda tc: COLS[:, 48 + tc:49 + tc]

    ln_es = contextlib.ExitStack()
    QWK = ln_es.enter_context(nc.sbuf_tensor(_un("QWK"), [128, D], F32))
    XNT = [ln_es.enter_context(nc.sbuf_tensor(_un(f"XNT{i}"), [128, D], BF16))
           for i in range(2)]
    TPB = ln_es.enter_context(nc.psum_tensor(_un("TPB"), [128, 1024], BF16))
    with ln_es:
        kb.barrier()
        @block.vector
        def _(vector):
            for tc in range(NT):
                vector.reduce_sum(out=NM_COL(tc), in_=XRES[:, tc * D:(tc + 1) * D],
                                  axis=AX.X)
            kb.vgap(vector)
            vector.tensor_scalar(out=COLS[:, 0:16], in0=COLS[:, 48:64],
                                 scalar1=1.0 / D, scalar2=None, op0=ALU.mult)
            kb.vgap(vector)
            _li = vector.tensor_scalar(out=COLS[:, 48:64], in0=COLS[:, 48:64],
                                       scalar1=-1.0 / D, scalar2=None,
                                       op0=ALU.mult)
            _li.then_inc(*kb.inc("sv"))
        t_means = kb.tag("sv")
        @block.scalar
        def _(scalar, t=t_means):
            scalar.wait_ge(kb.sems["sv"], t)
            for tc in range(NT):
                _li = kb.act(scalar, out=QWK[:], in_=XRES[:, tc * D:(tc + 1) * D],
                             func=AF.Square, bias=NM_COL(tc),
                             accum_out=R_COL(tc))
            _li.then_inc(*kb.inc("ss"))
        t_ssq = kb.tag("ss")
        @block.vector
        def _(vector, t=t_ssq):
            vector.wait_ge(kb.sems["ss"], t)
            _li = vector.tensor_scalar(out=COLS[:, 16:32], in0=COLS[:, 16:32],
                                       scalar1=1.0 / D, scalar2=1e-5,
                                       op0=ALU.mult, op1=ALU.add)
            _li.then_inc(*kb.inc("sv"))
        t_c1 = kb.tag("sv")
        @block.scalar
        def _(scalar, t=t_c1):
            scalar.wait_ge(kb.sems["sv"], t)
            _li = kb.act(scalar, out=COLS[:, 16:32], in_=COLS[:, 16:32],
                         func=AF.Sqrt)
            _li.then_inc(*kb.inc("ss"))
        t_c2 = kb.tag("ss")
        @block.vector
        def _(vector, t=t_c2):
            vector.wait_ge(kb.sems["ss"], t)
            vector.reciprocal(out=COLS[:, 16:32], in_=COLS[:, 16:32])
            kb.vgap(vector)
            _li = vector.tensor_copy(out=kb.JNK[:, 0:1], in_=COLS[:, 16:17])
            _li.then_inc(*kb.inc("sv"))
        t_rstd = kb.tag("sv")
        last_ev = [None]
        for tc in range(NT):
            xnb = XNT[tc % 2]
            @block.vector
            def _(vector, tc=tc, xnb=xnb, t=t_rstd, lt=last_ev[0]):
                vector.wait_ge(kb.sems["sv"], t)
                if lt is not None:
                    vector.wait_ge(kb.sems["st"], lt)
                _li = vector.tensor_scalar(out=xnb[:],
                                           in0=XRES[:, tc * D:(tc + 1) * D],
                                           scalar1=M_COL(tc), scalar2=R_COL(tc),
                                           op0=ALU.subtract, op1=ALU.mult)
                _li.then_inc(*kb.inc("sv"))
            t_xn = kb.tag("sv")
            @block.tensor
            def _(tensor, xnb=xnb, t=t_xn):
                tensor.wait_ge(kb.sems["sv"], t)
                for j in range(8):
                    _li = tensor.transpose(TPB[:, j * 128:(j + 1) * 128],
                                     xnb[:, j * 128:(j + 1) * 128], IDENT[:])
                _li.then_inc(*kb.inc("st"))
            last_ev[0] = kb.tag("st")
            t_tp = kb.tag("st")
            @block.vector
            def _(vector, tc=tc, t=t_tp):
                vector.wait_ge(kb.sems["st"], t)
                _li = vector.tensor_copy(
                    out=AB[:, :].rearrange("p (c t) -> p c t", t=L)[
                        :, :, tc * 128:(tc + 1) * 128],
                    in_=TPB[:].rearrange("p (c t) -> p c t", c=8))
                _li.then_inc(*kb.inc("sv"))


def emit_l0_attn(ctx, AB, C_AR, D_AR):
    nc, block, kb, W = ctx["nc"], ctx["block"], ctx["kb"], ctx["W"]
    XRES, IDENT, IDENTF = ctx["XRES"], ctx["IDENT"], ctx["IDENTF"]
    ONESF, IOTA, B16W, E80W = ctx["ONESF"], ctx["IOTA"], ctx["B16W"], ctx["E80W"]
    COLS, MASKP = ctx["COLS"], ctx["MASKP"]
    M_COL, R_COL, S_COL = ctx["M_COL"], ctx["R_COL"], ctx["S_COL"]
    bq_col, bk_col = ctx["bq_col"], ctx["bk_col"]
    stage = ctx["stage"]

    aes = contextlib.ExitStack()
    QS = aes.enter_context(nc.sbuf_tensor(_un("QS"), [16, L], F32))
    QSTM = aes.enter_context(nc.sbuf_tensor(_un("QSTM"), [128, NT * 16], F32))
    M8 = aes.enter_context(nc.sbuf_tensor(_un("M8"), [16, 16], F32))
    VMK = aes.enter_context(nc.sbuf_tensor(_un("VMK"), [16, 10], F32))
    VROW = aes.enter_context(nc.sbuf_tensor(_un("VROW"), [1, 160], F32))
    VMC = aes.enter_context(nc.sbuf_tensor(_un("VMC"), [128, 2], F32))
    UCOL = aes.enter_context(nc.sbuf_tensor(_un("UCOL"), [16, 1], F32))
    OUTG = aes.enter_context(nc.sbuf_tensor(_un("OUTG"), [128, 8 * 128], BF16))
    with aes:
        s1 = contextlib.ExitStack()
        WQF = s1.enter_context(nc.sbuf_tensor(_un("WQF"), [128, NC * 1024], BF16))
        STG = s1.enter_context(nc.sbuf_tensor(_un("STGq"), [128, 1024], F32))
        QBF = s1.enter_context(nc.sbuf_tensor(_un("QBF"), [128, 512], F32))
        QSB = s1.enter_context(nc.sbuf_tensor(_un("QSB"), [128, 2048], BF16))
        CMB = s1.enter_context(nc.sbuf_tensor(_un("CMB"), [16, 2 * 512], F32))
        PQ = [s1.enter_context(nc.psum_tensor(_un(f"PQ{i}"), [128, 512], F32))
              for i in range(2)]
        PSTAT = s1.enter_context(nc.psum_tensor(_un("PSTAT"), [16, 2048], F32))
        with s1:
            kb.barrier()
            for c in range(NC):
                tgt = kb.stream(
                    W["wqT"].rearrange("(g p) d -> p g d", p=128)[:, :, c * 128:(c + 1) * 128],
                    STG[:].rearrange("p (g d) -> p g d", g=8), "stgq")
                @block.vector
                def _(vector, c=c, tgt=tgt):
                    vector.wait_ge(kb.sems["sdw"], tgt)
                    _li = vector.tensor_copy(out=WQF[:, c * 1024:(c + 1) * 1024],
                                             in_=STG[:])
                    _li.then_inc(*kb.inc("sv"))
                kb.release("stgq", "sv")
            wq_ready = kb.tag("sv")
            pq_free = [None, None]
            for tb in range(4):
                comb_done = kb.tag("sv")
                for c in range(NC):
                    pq = PQ[c % 2]
                    @block.tensor
                    def _(tensor, c=c, tb=tb, pq=pq, tf=pq_free[c % 2],
                          cd=comb_done):
                        if c == 0 and tb == 0:
                            tensor.wait_ge(kb.sems["sv"], wq_ready)
                        if tf is not None:
                            tensor.wait_ge(kb.sems["sv"], tf)
                        if c == 0 and tb > 0:
                            tensor.wait_ge(kb.sems["sv"], cd)
                        for g in range(8):
                            _li = tensor.matmul(
                                pq[:],
                                WQF[:, c * 1024 + g * 128:c * 1024 + (g + 1) * 128],
                                AB[:, g * L + tb * 512:g * L + (tb + 1) * 512],
                                start=(g == 0), stop=(g == 7))
                        _li.then_inc(*kb.inc("st"))
                    t_q = kb.tag("st")
                    @block.vector
                    def _(vector, c=c, pq=pq, t=t_q):
                        vector.wait_ge(kb.sems["st"], t)
                        _li = vector.tensor_scalar(out=QBF[:], in0=pq[:],
                                                   scalar1=bq_col(c), scalar2=None,
                                                   op0=ALU.add)
                        _li.then_inc(*kb.inc("sv"))
                    pq_free[c % 2] = kb.tag("sv")
                    t_qbf = kb.tag("sv")
                    @block.scalar
                    def _(scalar, t=t_qbf):
                        scalar.wait_ge(kb.sems["sv"], t)
                        kb.act(scalar, out=QSB[:, 0:512], in_=QBF[:], func=AF.Square)
                        _li = kb.act(scalar, out=QSB[:, 512:1024], in_=QBF[:],
                                     func=AF.Exp)
                        _li.then_inc(*kb.inc("ss"))
                    t_se = kb.tag("ss")
                    @block.vector
                    def _(vector, t=t_se):
                        vector.wait_ge(kb.sems["ss"], t)
                        vector.tensor_copy(out=QSB[:, 1024:1536], in_=QBF[:])
                        _li = vector.tensor_mul(out=QSB[:, 1536:2048],
                                                in0=QSB[:, 512:1024], in1=QBF[:])
                        _li.then_inc(*kb.inc("sv"))
                    t_el = kb.tag("sv")
                    @block.tensor
                    def _(tensor, c=c, t=t_el):
                        tensor.wait_ge(kb.sems["sv"], t)
                        for s in range(4):
                            _li = tensor.matmul(PSTAT[:, s * 512:(s + 1) * 512],
                                                B16W[:, 16 - 2 * c:32 - 2 * c],
                                                QSB[:, s * 512:(s + 1) * 512],
                                                start=(c == 0), stop=(c == NC - 1))
                        _li.then_inc(*kb.inc("st"))
                t_stat = kb.tag("st")
                S2 = PSTAT[:, 0:512]; Se = PSTAT[:, 512:1024]
                S1 = PSTAT[:, 1024:1536]; Sg = PSTAT[:, 1536:2048]
                T0 = CMB[:, 0:512]; T1 = CMB[:, 512:1024]; T2 = QBF[0:16, 0:512]
                qsl = QS[:, tb * 512:(tb + 1) * 512]
                @block.vector
                def _(vector, t=t_stat, qsl=qsl, S1=S1, S2=S2, Se=Se, Sg=Sg):
                    vector.wait_ge(kb.sems["st"], t)
                    vector.tensor_scalar(out=T0, in0=S1, scalar1=1.0 / DK,
                                         scalar2=None, op0=ALU.mult)
                    vector.tensor_mul(out=T0, in0=S1, in1=T0)
                    vector.tensor_sub(out=T0, in0=S2, in1=T0)
                    vector.tensor_scalar(out=qsl, in0=T0, scalar1=0.2 / (DK - 1),
                                         scalar2=None, op0=ALU.mult)
                    vector.reciprocal(out=T1, in_=Se)
                    _li = vector.tensor_mul(out=T1, in0=Sg, in1=T1)
                    _li.then_inc(*kb.inc("sv"))
                t_v1 = kb.tag("sv")
                @block.scalar
                def _(scalar, t=t_v1, S2=S2, Se=Se):
                    scalar.wait_ge(kb.sems["sv"], t)
                    kb.act(scalar, out=T0, in_=S2, func=AF.Sqrt)
                    _li = kb.act(scalar, out=T2, in_=Se, func=AF.Ln)
                    _li.then_inc(*kb.inc("ss"))
                t_s1 = kb.tag("ss")
                @block.vector
                def _(vector, t=t_s1, qsl=qsl):
                    vector.wait_ge(kb.sems["ss"], t)
                    vector.scalar_tensor_tensor(out=qsl, in0=T0, scalar=0.5,
                                                in1=qsl, op0=ALU.mult, op1=ALU.add)
                    vector.tensor_sub(out=T2, in0=T2, in1=T1)
                    vector.tensor_scalar(out=T2, in0=T2, scalar1=0.3,
                                         scalar2=-1.92e-8, op0=ALU.mult,
                                         op1=ALU.add)
                    _li = vector.tensor_add(out=qsl, in0=qsl, in1=T2)
                    _li.then_inc(*kb.inc("sv"))
        kb.dump("qs", QS[:, :])
        if stage < 3:
            return

        s2 = contextlib.ExitStack()
        SCR = s2.enter_context(nc.sbuf_tensor(_un("SCR"), [16, 512], F32))
        QSC = s2.enter_context(nc.sbuf_tensor(_un("QSC"), [16, L], F32))
        SENTT = s2.enter_context(nc.sbuf_tensor(_un("SENTT"), [16, 10], F32))
        VMM = s2.enter_context(nc.sbuf_tensor(_un("VMM"), [16, 10], mybir.dt.uint8))
        FI32 = s2.enter_context(nc.sbuf_tensor(_un("FI32"), [16, 1], I32))
        PT = s2.enter_context(nc.psum_tensor(_un("PT"), [128, 16], F32))
        PU = s2.enter_context(nc.psum_tensor(_un("PU"), [16, 1], F32))
        with s2:
            kb.barrier()
            FC = lambda i: COLS[0:16, 40 + i:41 + i]
            @block.scalar
            def _(scalar):
                for q in range(4):
                    _li = kb.act(scalar, out=SCR[:], in_=QS[:, q * 512:(q + 1) * 512],
                                 func=AF.Square, accum_out=FC(4 + q))
                _li.then_inc(*kb.inc("ss"))
            t0 = kb.tag("ss")
            @block.vector
            def _(vector, t=t0):
                vector.wait_ge(kb.sems["ss"], t)
                vector.reduce_sum(out=FC(0), in_=QS[:, :], axis=AX.X)
                vector.tensor_add(out=FC(4), in0=FC(4), in1=FC(5))
                vector.tensor_add(out=FC(6), in0=FC(6), in1=FC(7))
                kb.vgap(vector)
                vector.tensor_add(out=FC(4), in0=FC(4), in1=FC(6))
                vector.tensor_scalar(out=FC(1), in0=FC(0), scalar1=1.0 / L,
                                     scalar2=None, op0=ALU.mult)
                kb.vgap(vector)
                vector.tensor_mul(out=FC(2), in0=FC(0), in1=FC(1))
                kb.vgap(vector)
                vector.tensor_sub(out=FC(2), in0=FC(4), in1=FC(2))
                kb.vgap(vector)
                _li = vector.tensor_scalar(out=FC(2), in0=FC(2),
                                           scalar1=1.0 / (L - 1), scalar2=None,
                                           op0=ALU.mult)
                _li.then_inc(*kb.inc("sv"))
            t1 = kb.tag("sv")
            @block.scalar
            def _(scalar, t=t1):
                scalar.wait_ge(kb.sems["sv"], t)
                _li = kb.act(scalar, out=FC(3), in_=FC(2), func=AF.Sqrt)
                _li.then_inc(*kb.inc("ss"))
            t2 = kb.tag("ss")
            @block.vector
            def _(vector, t=t2):
                vector.wait_ge(kb.sems["ss"], t)
                vector.tensor_scalar(out=FC(1), in0=FC(1), scalar1=1e-6,
                                     scalar2=None, op0=ALU.add)
                kb.vgap(vector)
                vector.reciprocal(out=FC(1), in_=FC(1))
                kb.vgap(vector)
                vector.tensor_mul(out=FC(0), in0=FC(3), in1=FC(1))
                kb.vgap(vector)
                _li = vector.tensor_scalar(out=FC(0), in0=FC(0), scalar1=10.0,
                                           scalar2=None, op0=ALU.mult)
                _li.then_inc(*kb.inc("sv"))
            t3 = kb.tag("sv")
            @block.scalar
            def _(scalar, t=t3):
                scalar.wait_ge(kb.sems["sv"], t)
                scalar.copy(out=FI32[:], in_=FC(0))
                kb.sgap(scalar)
                _li = scalar.copy(out=FC(0), in_=FI32[:])
                _li.then_inc(*kb.inc("ss"))
            t4 = kb.tag("ss")
            @block.vector
            def _(vector, t=t4):
                vector.wait_ge(kb.sems["ss"], t)
                _li = vector.tensor_scalar(out=FC(0), in0=FC(0), scalar1=3.0,
                                           scalar2=10.0, op0=ALU.max, op1=ALU.min)
                _li.then_inc(*kb.inc("sv"))
            t5 = kb.tag("sv")
            @block.tensor
            def _(tensor, t=t5):
                tensor.wait_ge(kb.sems["sv"], t)
                _li = tensor.matmul(PU[:], ONESF[0:1, 0:16], FC(0)[0:1, 0:1],
                                    start=True, stop=True)
                _li.then_inc(*kb.inc("st"))
            tu = kb.tag("st")
            @block.vector
            def _(vector, t=tu):
                vector.wait_ge(kb.sems["st"], t)
                _li = vector.tensor_copy(out=UCOL[:], in_=PU[:])
                _li.then_inc(*kb.inc("sv"))
            for tc in range(NT):
                @block.tensor
                def _(tensor, tc=tc):
                    kb.wait(tensor, "sv")
                    _li = tensor.transpose(PT[:], QS[:, tc * 128:(tc + 1) * 128],
                                           IDENTF[0:16, 0:16])
                    _li.then_inc(*kb.inc("st"))
                tt = kb.tag("st")
                @block.vector
                def _(vector, tc=tc, t=tt):
                    vector.wait_ge(kb.sems["st"], t)
                    _li = vector.tensor_copy(out=QSTM[:, tc * 16:(tc + 1) * 16],
                                             in_=PT[:])
                    _li.then_inc(*kb.inc("sv"))
            qstm_done = kb.tag("sv")
            @block.vector
            def _(vector, t=qstm_done):
                vector.wait_ge(kb.sems["sv"], t)
                vector.tensor_copy(out=QSC[:], in_=QS[:, :])
                vector.max(out=M8[:, 0:8], in_=QSC[:])
                kb.vgap(vector)
                vector.match_replace(out=QSC[:], in_to_replace=M8[:, 0:8],
                                     in_values=QSC[:], imm_value=NEG)
                vector.max(out=M8[:, 8:16], in_=QSC[:])
                vector.memset(SENTT[:], SENT)
                vector.tensor_single_scalar(out=VMM[:], in_=IOTA[0:16, 100:110],
                                            scalar=UCOL[:], op=ALU.is_lt)
                kb.vgap(vector)
                _li = vector.select(out=VMK[:], mask=VMM[:], on_true=M8[:, 0:10],
                                    on_false=SENTT[:])
                _li.then_inc(*kb.inc("sv"))
            t_v = kb.tag("sv")
            @block.scalar
            def _(scalar, t=t_v):
                scalar.wait_ge(kb.sems["sv"], t)
                scalar.dma_start(
                    out=VROW[0:1, :].rearrange("o (h j) -> o h j", h=16),
                    in_=M8[:, 0:10]).then_inc(*kb.inc("sdx", 16))
                scalar.dma_start(out=VMC[0:80, 0:1], in_=VMK[0:8, :]
                                 ).then_inc(*kb.inc("sdx", 16))
                scalar.dma_start(out=VMC[0:80, 1:2], in_=VMK[8:16, :]
                                 ).then_inc(*kb.inc("sdx", 16))
                scalar.wait_ge(kb.sems["sdx"], kb.tag("sdx"))
            t_vrow = kb.tag("sdx")
            ctx["t_vrow"] = t_vrow
            kb.dump("vmk", VMK[:, :])
            kb.dump("ucol", UCOL[:, :])
        if stage < 4:
            return

        s46 = contextlib.ExitStack()
        Q8 = s46.enter_context(nc.sbuf_tensor(_un("Q8"), [128, 8 * 80], BF16))
        with s46:
            s4 = contextlib.ExitStack()
            XNG = s4.enter_context(nc.sbuf_tensor(_un("XNG"), [128, NC * 160], BF16))
            SELC = s4.enter_context(nc.sbuf_tensor(_un("SELC"), [128, 160], F32))
            MRROW = s4.enter_context(nc.sbuf_tensor(_un("MRROW"), [2, 160], F32))
            MR2 = s4.enter_context(nc.sbuf_tensor(_un("MR2"), [1, 160], F32))
            SCR1 = s4.enter_context(nc.sbuf_tensor(_un("SCR1"), [128, 160], F32))
            MBB = s4.enter_context(nc.sbuf_tensor(_un("MBB"), [128, 160], F32))
            RBB = s4.enter_context(nc.sbuf_tensor(_un("RBB"), [128, 160], F32))
            STG4 = s4.enter_context(nc.sbuf_tensor(_un("STG4"), [128, 1024], F32))
            WB4 = [s4.enter_context(nc.sbuf_tensor(_un(f"WB4{i}"), [128, 1024], BF16))
                   for i in range(2)]
            with s4:
                g_es = contextlib.ExitStack()
                PVB = g_es.enter_context(nc.psum_tensor(_un("PVB"), [128, 160], F32))
                PM = g_es.enter_context(nc.psum_tensor(_un("PM"), [2, 160], F32))
                PXG = [g_es.enter_context(nc.psum_tensor(_un(f"PXG{i}"), [128, 160], F32))
                       for i in range(4)]
                PMB = g_es.enter_context(nc.psum_tensor(_un("PMB"), [128, 160], F32))
                PRB = g_es.enter_context(nc.psum_tensor(_un("PRB"), [128, 160], F32))
                g_es.__enter__()
                kb.barrier()
                @block.vector
                def _(vector):
                    _li = vector.memset(Q8[:], 0.0)
                    _li.then_inc(*kb.inc("sv"))
                @block.tensor
                def _(tensor, tvr=ctx.get("t_vrow", 0)):
                    kb.wait(tensor, "sv")
                    tensor.wait_ge(kb.sems["sdx"], tvr)
                    _li = tensor.matmul(PVB[:], ONESF[0:1, 0:128], VROW[0:1, :],
                                        start=True, stop=True)
                    _li.then_inc(*kb.inc("st"))
                t_vb = kb.tag("st")
                for half in range(2):
                    for tc in range(NT):
                        @block.vector
                        def _(vector, tc=tc, half=half, t=t_vb):
                            vector.wait_ge(kb.sems["st"], t)
                            if not (half == 0 and tc == 0):
                                kb.wait(vector, "st")
                            _li = vector.tensor_tensor(
                                out=SELC[:].rearrange("p (h j) -> p h j", h=16),
                                in0=QSTM[:, tc * 16:(tc + 1) * 16].to_broadcast(
                                    [128, 16, 10]),
                                in1=PVB[:].rearrange("p (h j) -> p h j", h=16),
                                op=ALU.is_equal)
                            _li.then_inc(*kb.inc("sv"))
                        t_sel = kb.tag("sv")
                        @block.tensor
                        def _(tensor, tc=tc, half=half, t=t_sel):
                            tensor.wait_ge(kb.sems["sv"], t)
                            st_ = (tc == 0)
                            sp_ = (tc == NT - 1)
                            for ci in range(4):
                                c = half * 4 + ci
                                _li = tensor.matmul(
                                    PXG[ci][:],
                                    XRES[:, tc * 1024 + c * 128:tc * 1024 + (c + 1) * 128],
                                    SELC[:], start=st_, stop=sp_)
                            if half == 0:
                                _li = tensor.matmul(
                                    PM[:], COLS[:, tc:tc + 17:16], SELC[:],
                                    start=st_, stop=sp_)
                            _li.then_inc(*kb.inc("st"))
                    t_g = kb.tag("st")
                    if half == 0:
                        @block.vector
                        def _(vector, t=t_g):
                            vector.wait_ge(kb.sems["st"], t)
                            _li = vector.tensor_copy(out=MRROW[:], in_=PM[:])
                            _li.then_inc(*kb.inc("sv"))
                        t_mrr = kb.tag("sv")
                        @block.scalar
                        def _(scalar, t=t_mrr):
                            scalar.wait_ge(kb.sems["sv"], t)
                            scalar.dma_start(out=MR2[:], in_=MRROW[1:2, :]
                                             ).then_inc(*kb.inc("sdx", 16))
                        t_mr2 = kb.tag("sdx")
                        @block.tensor
                        def _(tensor, t=t_mrr, t2=t_mr2):
                            tensor.wait_ge(kb.sems["sv"], t)
                            tensor.wait_ge(kb.sems["sdx"], t2)
                            tensor.matmul(PMB[:], ONESF[0:1, 0:128], MRROW[0:1, :],
                                          start=True, stop=True)
                            _li = tensor.matmul(PRB[:], ONESF[0:1, 0:128],
                                                MR2[0:1, :], start=True, stop=True)
                            _li.then_inc(*kb.inc("st"))
                    t_bc = kb.tag("st")
                    if half == 0:
                        @block.vector
                        def _(vector, t=t_bc):
                            vector.wait_ge(kb.sems["st"], t)
                            vector.tensor_copy(out=MBB[:], in_=PMB[:])
                            _li = vector.tensor_copy(out=RBB[:], in_=PRB[:])
                            _li.then_inc(*kb.inc("sv"))
                    t_bc2 = kb.tag("sv")
                    for ci in range(4):
                        c = half * 4 + ci
                        @block.vector
                        def _(vector, c=c, ci=ci, t=t_bc2, tg=t_g):
                            vector.wait_ge(kb.sems["sv"], t)
                            vector.wait_ge(kb.sems["st"], tg)
                            vector.tensor_sub(out=SCR1[:], in0=PXG[ci][:], in1=MBB[:])
                            _li = vector.tensor_mul(out=XNG[:, c * 160:(c + 1) * 160],
                                                    in0=SCR1[:], in1=RBB[:])
                            _li.then_inc(*kb.inc("sv"))
                kb.dump("xng", XNG[:, :])
                g_es.close()
                pqs_es = contextlib.ExitStack()
                PQS = pqs_es.enter_context(nc.psum_tensor(_un("PQS"), [128, 20], F32))
                with pqs_es:
                    for cs in range(NC):
                        tgt = kb.stream(
                            W["wqT"].rearrange("(g p) d -> p g d", p=128)[
                                :, :, cs * 128:(cs + 1) * 128],
                            STG4[:].rearrange("p (g d) -> p g d", g=8), "stg4")
                        @block.vector
                        def _(vector, cs=cs, tgt=tgt):
                            vector.wait_ge(kb.sems["sdw"], tgt)
                            _li = vector.tensor_copy(out=WB4[cs % 2][:], in_=STG4[:])
                            _li.then_inc(*kb.inc("sv"))
                        kb.release("stg4", "sv")
                        t_w = kb.tag("sv")
                        @block.tensor
                        def _(tensor, cs=cs, t=t_w):
                            tensor.wait_ge(kb.sems["sv"], t)
                            if cs > 0:
                                tensor.wait_ge(kb.sems["sv"], kb.tag("sv"))
                            for g in range(8):
                                _li = tensor.matmul(
                                    PQS[:],
                                    WB4[cs % 2][:, g * 128:(g + 1) * 128],
                                    XNG[:, g * 160 + 20 * cs:g * 160 + 20 * cs + 20],
                                    start=(g == 0), stop=(g == 7))
                            _li.then_inc(*kb.inc("st"))
                        t_qs = kb.tag("st")
                        for hh in range(2):
                            h = 2 * cs + hh
                            s = h // 8; cc = (h % 8) // 2
                            blk = (s * 4 + cc) * 80 + (h % 8) * 10
                            pb = (h % 2) * 64
                            @block.vector
                            def _(vector, h=h, blk=blk, pb=pb, cs=cs, hh=hh, t=t_qs):
                                vector.wait_ge(kb.sems["st"], t)
                                _li = vector.tensor_scalar(
                                    out=Q8[pb:pb + 64, blk:blk + 10],
                                    in0=PQS[pb:pb + 64, hh * 10:(hh + 1) * 10],
                                    scalar1=bq_col(cs)[pb:pb + 64, :],
                                    scalar2=0.125, op0=ALU.add, op1=ALU.mult)
                                _li.then_inc(*kb.inc("sv"))
            kb.dump("q8", Q8[:, :])
            if stage < 5:
                return

            s5 = contextlib.ExitStack()
            STG5 = s5.enter_context(nc.sbuf_tensor(_un("STG5"), [128, 1024], F32))
            KWB = [s5.enter_context(nc.sbuf_tensor(_un(f"KWB{i}"), [128, 1024], BF16))
                   for i in range(2)]
            WVF = s5.enter_context(nc.sbuf_tensor(_un("WVF"), [128, NC * 1024], BF16))
            PKV = [s5.enter_context(nc.psum_tensor(_un(f"PKV{i}"), [128, 512], F32))
                   for i in range(3)]
            with s5:
                kb.barrier()
                kv_free = [None, None, None]
                ki = 0
                for c in range(NC):
                    tgt = kb.stream(
                        W["wkT"].rearrange("(g p) d -> p g d", p=128)[
                            :, :, c * 128:(c + 1) * 128],
                        STG5[:].rearrange("p (g d) -> p g d", g=8), "stg5")
                    @block.vector
                    def _(vector, c=c, tgt=tgt):
                        vector.wait_ge(kb.sems["sdw"], tgt)
                        _li = vector.tensor_copy(out=KWB[c % 2][:], in_=STG5[:])
                        _li.then_inc(*kb.inc("sv"))
                    kb.release("stg5", "sv")
                    t_w = kb.tag("sv")
                    for tb in range(4):
                        pk = PKV[ki % 3]
                        @block.tensor
                        def _(tensor, c=c, tb=tb, pk=pk, t=t_w, kf=kv_free[ki % 3]):
                            tensor.wait_ge(kb.sems["sv"], t)
                            if kf is not None:
                                tensor.wait_ge(kb.sems["sv"], kf)
                            for g in range(8):
                                _li = tensor.matmul(
                                    pk[:],
                                    KWB[c % 2][:, g * 128:(g + 1) * 128],
                                    AB[:, g * L + tb * 512:g * L + (tb + 1) * 512],
                                    start=(g == 0), stop=(g == 7))
                            _li.then_inc(*kb.inc("st"))
                        t_k = kb.tag("st")
                        @block.vector
                        def _(vector, c=c, tb=tb, pk=pk, t=t_k):
                            vector.wait_ge(kb.sems["st"], t)
                            _li = vector.tensor_scalar(
                                out=C_AR[:, c * L + tb * 512:c * L + (tb + 1) * 512],
                                in0=pk[:], scalar1=bk_col(c), scalar2=None,
                                op0=ALU.add)
                            _li.then_inc(*kb.inc("sv"))
                        kv_free[ki % 3] = kb.tag("sv")
                        ki += 1
                for g in range(NC):
                    tgt = kb.stream(W["wvT"][g * 128:(g + 1) * 128, :],
                                    STG5[:], "stg5")
                    @block.vector
                    def _(vector, g=g, tgt=tgt):
                        vector.wait_ge(kb.sems["sdw"], tgt)
                        _li = vector.tensor_copy(out=WVF[:, g * 1024:(g + 1) * 1024],
                                                 in_=STG5[:])
                        _li.then_inc(*kb.inc("sv"))
                    kb.release("stg5", "sv")
                t_wv = kb.tag("sv")
                for tc in range(NT):
                    for half in range(2):
                        pk = PKV[ki % 3]
                        @block.tensor
                        def _(tensor, tc=tc, half=half, pk=pk, t=t_wv,
                              kf=kv_free[ki % 3]):
                            tensor.wait_ge(kb.sems["sv"], t)
                            if kf is not None:
                                tensor.wait_ge(kb.sems["sv"], kf)
                            for g in range(8):
                                _li = tensor.matmul(
                                    pk[:],
                                    AB[:, g * L + tc * 128:g * L + (tc + 1) * 128],
                                    WVF[:, g * 1024 + half * 512:g * 1024 + (half + 1) * 512],
                                    start=(g == 0), stop=(g == 7))
                            _li.then_inc(*kb.inc("st"))
                        t_v = kb.tag("st")
                        @block.vector
                        def _(vector, tc=tc, half=half, pk=pk, t=t_v):
                            vector.wait_ge(kb.sems["st"], t)
                            _li = vector.tensor_copy(
                                out=D_AR[:, tc * 1024 + half * 512:tc * 1024 + (half + 1) * 512],
                                in_=pk[:])
                            _li.then_inc(*kb.inc("sv"))
                        kv_free[ki % 3] = kb.tag("sv")
                        ki += 1
            kb.dump("kfm", C_AR[:, :])
            kb.dump("vtm", D_AR[:, :])

            s6 = contextlib.ExitStack()
            ATTB = s6.enter_context(nc.sbuf_tensor(_un("ATTB"), [128, 2048], BF16))
            ATTT = s6.enter_context(nc.sbuf_tensor(_un("ATTT"), [128, NT * 80], BF16))
            BVBC = s6.enter_context(nc.sbuf_tensor(_un("BVBC"), [128, 1024], F32))
            OUTALL = s6.enter_context(nc.sbuf_tensor(_un("OUTALL"), [128, 512], F32))
            BROW = s6.enter_context(nc.sbuf_tensor(_un("BROW"), [1, 1024], F32))
            SMX = s6.enter_context(nc.sbuf_tensor(_un("SMX"), [128, 8], F32))
            with s6:
                kb.barrier()
                bv_es = contextlib.ExitStack()
                PB1 = bv_es.enter_context(nc.psum_tensor(_un("PB1"), [128, 1024], F32))
                with bv_es:
                    @block.scalar
                    def _(scalar):
                        scalar.dma_start(out=BROW[:], in_=W["bv"][None, :]
                                         ).then_inc(*kb.inc("sdx", 16))
                    t_br = kb.tag("sdx")
                    @block.tensor
                    def _(tensor, t=t_br):
                        tensor.wait_ge(kb.sems["sdx"], t)
                        for off in (0, 512):
                            _li = tensor.matmul(PB1[:, off:off + 512],
                                                ONESF[0:1, 0:128],
                                                BROW[0:1, off:off + 512],
                                                start=True, stop=True)
                        _li.then_inc(*kb.inc("st"))
                    t_bv = kb.tag("st")
                    @block.vector
                    def _(vector, t=t_bv):
                        vector.wait_ge(kb.sems["st"], t)
                        vector.tensor_copy(out=BVBC[:], in_=PB1[:])
                        _li = vector.memset(OUTG[:], 0.0)
                        _li.then_inc(*kb.inc("sv"))
                for s in range(2):
                    h6 = contextlib.ExitStack()
                    PSC = h6.enter_context(nc.psum_tensor(_un("PSC"), [80, 2048], F32))
                    PE = h6.enter_context(nc.psum_tensor(_un("PE"), [80, 2048], F32))
                    with h6:
                        @block.tensor
                        def _(tensor, s=s):
                            kb.wait(tensor, "sv")
                            for tp in range(4):
                                for cc in range(4):
                                    _li = tensor.matmul(
                                        PSC[:, tp * 512:(tp + 1) * 512],
                                        Q8[:, (s * 4 + cc) * 80:(s * 4 + cc + 1) * 80],
                                        C_AR[:, (4 * s + cc) * L + tp * 512:(4 * s + cc) * L + (tp + 1) * 512],
                                        start=(cc == 0), stop=(cc == 3))
                            _li.then_inc(*kb.inc("st"))
                        t_sc = kb.tag("st")
                        @block.vector
                        def _(vector, t=t_sc):
                            vector.wait_ge(kb.sems["st"], t)
                            _li = vector.reduce_max(out=SMX[0:80, 0:1], in_=PSC[:],
                                                    axis=AX.X, negate=True)
                            _li.then_inc(*kb.inc("sv"))
                        t_mx = kb.tag("sv")
                        @block.scalar
                        def _(scalar, t=t_mx):
                            scalar.wait_ge(kb.sems["sv"], t)
                            for q in range(4):
                                _li = kb.act(scalar, out=PE[:, q * 512:(q + 1) * 512],
                                             in_=PSC[:, q * 512:(q + 1) * 512],
                                             func=AF.Exp, bias=SMX[0:80, 0:1],
                                             accum_out=SMX[0:80, 1 + q:2 + q])
                            _li.then_inc(*kb.inc("ss"))
                        t_e = kb.tag("ss")
                        @block.vector
                        def _(vector, t=t_e):
                            vector.wait_ge(kb.sems["ss"], t)
                            vector.tensor_add(out=SMX[0:80, 1:2], in0=SMX[0:80, 1:2],
                                              in1=SMX[0:80, 2:3])
                            vector.tensor_add(out=SMX[0:80, 3:4], in0=SMX[0:80, 3:4],
                                              in1=SMX[0:80, 4:5])
                            kb.vgap(vector)
                            vector.tensor_add(out=SMX[0:80, 1:2], in0=SMX[0:80, 1:2],
                                              in1=SMX[0:80, 3:4])
                            kb.vgap(vector)
                            vector.reciprocal(out=SMX[0:80, 5:6], in_=SMX[0:80, 1:2])
                            kb.vgap(vector)
                            _li = vector.tensor_scalar(out=ATTB[0:80, :], in0=PE[:],
                                                       scalar1=SMX[0:80, 5:6],
                                                       scalar2=None, op0=ALU.mult)
                            _li.then_inc(*kb.inc("sv"))
                        t_at = kb.tag("sv")
                    h7 = contextlib.ExitStack()
                    PTT = h7.enter_context(nc.psum_tensor(_un("PTT"), [128, 80], BF16))
                    PO = h7.enter_context(nc.psum_tensor(_un("PO"), [80, 512], F32))
                    with h7:
                        for tc in range(NT):
                            @block.tensor
                            def _(tensor, tc=tc, t=t_at):
                                tensor.wait_ge(kb.sems["sv"], t)
                                if tc > 0:
                                    kb.wait(tensor, "sv")
                                _li = tensor.transpose(
                                    PTT[:, 0:80],
                                    ATTB[0:80, tc * 128:(tc + 1) * 128],
                                    IDENT[0:80, 0:80])
                                _li.then_inc(*kb.inc("st"))
                            t_tt = kb.tag("st")
                            @block.vector
                            def _(vector, tc=tc, t=t_tt):
                                vector.wait_ge(kb.sems["st"], t)
                                _li = vector.tensor_copy(
                                    out=ATTT[:, tc * 80:(tc + 1) * 80],
                                    in_=PTT[:, 0:80])
                                _li.then_inc(*kb.inc("sv"))
                        t_attt = kb.tag("sv")
                        @block.tensor
                        def _(tensor, s=s, t=t_attt):
                            tensor.wait_ge(kb.sems["sv"], t)
                            for tc in range(NT):
                                _li = tensor.matmul(
                                    PO[:],
                                    ATTT[:, tc * 80:(tc + 1) * 80],
                                    D_AR[:, tc * 1024 + s * 512:tc * 1024 + (s + 1) * 512],
                                    start=(tc == 0), stop=(tc == NT - 1))
                            _li.then_inc(*kb.inc("st"))
                        t_po = kb.tag("st")
                        if s == 0:
                            kb.dump("attb", ATTB[0:80, :])
                        if s == 0:
                            pass
                        @block.vector
                        def _(vector, s=s, t=t_po):
                            vector.wait_ge(kb.sems["st"], t)
                            vector.scalar_tensor_tensor(
                                out=OUTALL[0:80, :], in0=PO[:], scalar=1.0,
                                in1=BVBC[0:80, s * 512:(s + 1) * 512],
                                op0=ALU.mult, op1=ALU.add)
                            for pp in range(4):
                                pg = s * 4 + pp
                                for hh in range(2):
                                    hl = 2 * pp + hh
                                    _li = vector.tensor_scalar(
                                        out=OUTG[0:80, pg * 128 + hh * 64:pg * 128 + (hh + 1) * 64],
                                        in0=OUTALL[0:80, hl * 64:(hl + 1) * 64],
                                        scalar1=MASKP[0:80, hl:hl + 1], scalar2=None,
                                        op0=ALU.mult)
                            _li.then_inc(*kb.inc("sv"))
                        if s == 0:
                            kb.dump("outall", OUTALL[0:80, :])
        kb.dump("outg", OUTG[:, :])
        s7 = contextlib.ExitStack()
        SELFS = s7.enter_context(nc.sbuf_tensor(_un("SELFS"), [80, 2048], BF16))
        OUTG2 = None
        with s7:
            kb.barrier()
            for s in range(2):
                pe_es = contextlib.ExitStack()
                PEA = pe_es.enter_context(nc.psum_tensor(_un("PEA"), [80, 2048], F32))
                with pe_es:
                    @block.tensor
                    def _(tensor, s=s):
                        kb.wait(tensor, "sv")
                        for off in range(0, 2048, 512):
                            _li = tensor.matmul(PEA[:, off:off + 512],
                                                E80W[:, s * 80:(s + 1) * 80],
                                                QS[:, off:off + 512],
                                                start=True, stop=True)
                        _li.then_inc(*kb.inc("st"))
                    t_pe = kb.tag("st")
                    @block.vector
                    def _(vector, s=s, t=t_pe, tvr=ctx.get("t_vrow", 0)):
                        vector.wait_ge(kb.sems["st"], t)
                        vector.wait_ge(kb.sems["sdx"], tvr)
                        _li = vector.tensor_single_scalar(
                            out=SELFS[:], in_=PEA[:],
                            scalar=VMC[0:80, s:s + 1], op=ALU.is_equal)
                        _li.then_inc(*kb.inc("sv"))
                if s == 0:
                    kb.dump("selfs", SELFS[:, :])
                sc_es = contextlib.ExitStack()
                PF = sc_es.enter_context(nc.psum_tensor(_un("PF"), [128, 2048], F32))
                with sc_es:
                    for pp in range(4):
                        pg = s * 4 + pp
                        @block.tensor
                        def _(tensor, pg=pg):
                            kb.wait(tensor, "sv")
                            if pg % 4 > 0:
                                tensor.wait_ge(kb.sems["sv"], kb.tag("sv"))
                            for off in range(0, 2048, 512):
                                _li = tensor.matmul(PF[:, off:off + 512],
                                                    OUTG[0:80, pg * 128:(pg + 1) * 128],
                                                    SELFS[:, off:off + 512],
                                                    start=True, stop=True)
                            _li.then_inc(*kb.inc("st"))
                        t_f = kb.tag("st")
                        @block.vector
                        def _(vector, pg=pg, t=t_f):
                            vector.wait_ge(kb.sems["st"], t)
                            _li = vector.tensor_copy(out=C_AR[:, pg * L:(pg + 1) * L],
                                                     in_=PF[:])
                            _li.then_inc(*kb.inc("sv"))

        kb.dump("full", C_AR[:, :])
        s8 = contextlib.ExitStack()
        STG8 = s8.enter_context(nc.sbuf_tensor(_un("STG8"), [128, 1024], F32))
        WOF = s8.enter_context(nc.sbuf_tensor(_un("WOF"), [128, NC * 1024], BF16))
        BOBC = s8.enter_context(nc.sbuf_tensor(_un("BOBC"), [128, 1024], F32))
        BROW2 = s8.enter_context(nc.sbuf_tensor(_un("BROW2"), [1, 1024], F32))
        PB2 = s8.enter_context(nc.psum_tensor(_un("PB2"), [128, 1024], F32))
        PWO = [s8.enter_context(nc.psum_tensor(_un(f"PWO{i}"), [128, 1024], F32))
               for i in range(2)]
        with s8:
            kb.barrier()
            @block.scalar
            def _(scalar):
                scalar.dma_start(out=BROW2[:], in_=W["bo"][None, :]
                                 ).then_inc(*kb.inc("sdx", 16))
            t_br = kb.tag("sdx")
            for g in range(NC):
                tgt = kb.stream(W["woT"][g * 128:(g + 1) * 128, :], STG8[:], "stg8")
                @block.vector
                def _(vector, g=g, tgt=tgt):
                    vector.wait_ge(kb.sems["sdw"], tgt)
                    _li = vector.tensor_copy(out=WOF[:, g * 1024:(g + 1) * 1024],
                                             in_=STG8[:])
                    _li.then_inc(*kb.inc("sv"))
                kb.release("stg8", "sv")
            t_wo = kb.tag("sv")
            @block.tensor
            def _(tensor, t=t_br):
                tensor.wait_ge(kb.sems["sdx"], t)
                for off in (0, 512):
                    _li = tensor.matmul(PB2[:, off:off + 512], ONESF[0:1, 0:128],
                                        BROW2[0:1, off:off + 512],
                                        start=True, stop=True)
                _li.then_inc(*kb.inc("st"))
            t_bo = kb.tag("st")
            @block.vector
            def _(vector, t=t_bo):
                vector.wait_ge(kb.sems["st"], t)
                _li = vector.tensor_copy(out=BOBC[:], in_=PB2[:])
                _li.then_inc(*kb.inc("sv"))
            wo_free = [None, None]
            for tc in range(NT):
                pw = PWO[tc % 2]
                @block.tensor
                def _(tensor, tc=tc, pw=pw, t=t_wo, wf=wo_free[tc % 2]):
                    tensor.wait_ge(kb.sems["sv"], t)
                    if wf is not None:
                        tensor.wait_ge(kb.sems["sv"], wf)
                    for g in range(8):
                        for off in (0, 512):
                            _li = tensor.matmul(
                                pw[:, off:off + 512],
                                C_AR[:, g * L + tc * 128:g * L + (tc + 1) * 128],
                                WOF[:, g * 1024 + off:g * 1024 + off + 512],
                                start=(g == 0), stop=(g == 7))
                    _li.then_inc(*kb.inc("st"))
                t_w = kb.tag("st")
                @block.vector
                def _(vector, tc=tc, pw=pw, t=t_w):
                    vector.wait_ge(kb.sems["st"], t)
                    _li = vector.scalar_tensor_tensor(
                        out=D_AR[:, tc * 1024:(tc + 1) * 1024], in0=pw[:],
                        scalar=1.0, in1=BOBC[:], op0=ALU.mult, op1=ALU.add)
                    _li.then_inc(*kb.inc("sv"))
                wo_free[tc % 2] = kb.tag("sv")


def emit_l1_attn(ctx, AB, C_AR, D_AR):
    nc, block, kb, W = ctx["nc"], ctx["block"], ctx["kb"], ctx["W"]
    XRES, IDENT, IDENTF = ctx["XRES"], ctx["IDENT"], ctx["IDENTF"]
    ONESF = ctx["ONESF"]
    ALOG_COL = ctx["ALOG_COL"]
    S_COL = ctx["S_COL"]
    stage = ctx["stage"]

    aes = contextlib.ExitStack()
    XZ3 = aes.enter_context(nc.sbuf_tensor(_un("XZ3"), [128, NT * 48], F32))
    with aes:
        s1 = contextlib.ExitStack()
        STG = s1.enter_context(nc.sbuf_tensor(_un("STGi"), [128, 1072], F32))
        WIF = s1.enter_context(nc.sbuf_tensor(_un("WIF"), [128, NC * PD], BF16))
        BINB = s1.enter_context(nc.sbuf_tensor(_un("BINB"), [128, PD], F32))
        BIROW = s1.enter_context(nc.sbuf_tensor(_un("BIROW"), [1, PD], F32))
        ZW = [s1.enter_context(nc.sbuf_tensor(_un(f"ZW{i}"), [128, 512], F32))
              for i in range(2)]
        PXZ = [s1.enter_context(nc.psum_tensor(_un(f"PXZ{i}"), [128, 512], F32))
               for i in range(2)]
        PX3 = s1.enter_context(nc.psum_tensor(_un("PX3"), [128, 48], F32))
        PBI = s1.enter_context(nc.psum_tensor(_un("PBI"), [128, 1072], F32))

        with s1:
            kb.barrier()
            for g in range(NC):
                tgt = kb.stream(W["winS"][g * 128:(g + 1) * 128, :], STG[:], "stgi")
                @block.vector
                def _(vector, g=g, tgt=tgt):
                    vector.wait_ge(kb.sems["sdw"], tgt)
                    _li = vector.tensor_copy(out=WIF[:, g * PD:(g + 1) * PD], in_=STG[:])
                    _li.then_inc(*kb.inc("sv"))
                kb.release("stgi", "sv")
            t_wi = kb.tag("sv")
            @block.scalar
            def _(scalar):
                scalar.dma_start(out=BINB[0:1, :], in_=W["binS"][None, :]
                                 ).then_inc(*kb.inc("sdx", 16))
            t_bir = kb.tag("sdx")
            @block.tensor
            def _(tensor, t=t_bir):
                tensor.wait_ge(kb.sems["sdx"], t)
                for off, w in [(0, 512), (512, 512), (1024, 48)]:
                    _li = tensor.matmul(PBI[:, off:off + w], ONESF[0:1, 0:128],
                                  BINB[0:1, off:off + w], start=True, stop=True)
                _li.then_inc(*kb.inc("st"))
            t_bi = kb.tag("st")
            @block.vector
            def _(vector, t=t_bi):
                vector.wait_ge(kb.sems["st"], t)
                _li = vector.tensor_copy(out=BINB[:], in_=PBI[:])
                _li.then_inc(*kb.inc("sv"))
            xz_free = [None, None]
            x3_free = None
            for tc in range(NT):
                for half in range(2):
                    px = PXZ[half]
                    @block.tensor
                    def _(tensor, tc=tc, half=half, px=px, t=t_wi,
                          xf=xz_free[half]):
                        tensor.wait_ge(kb.sems["sv"], t)
                        if xf is not None:
                            tensor.wait_ge(kb.sems["ss"], xf)
                        for g in range(8):
                            _li = tensor.matmul(
                                px[:],
                                AB[:, g * L + tc * 128:g * L + (tc + 1) * 128],
                                WIF[:, g * PD + half * 512:g * PD + (half + 1) * 512],
                                start=(g == 0), stop=(g == 7))
                        _li.then_inc(*kb.inc("st"))
                    t_x = kb.tag("st")
                    zw_guard = kb.tag("ss")
                    @block.vector
                    def _(vector, half=half, px=px, t=t_x, g=zw_guard):
                        vector.wait_ge(kb.sems["st"], t)
                        vector.wait_ge(kb.sems["ss"], g)
                        _li = vector.tensor_add(out=ZW[half][:], in0=px[:],
                                          in1=BINB[:, half * 512:(half + 1) * 512])
                        _li.then_inc(*kb.inc("sv"))
                    t_z = kb.tag("sv")
                    @block.scalar
                    def _(scalar, tc=tc, half=half, t=t_z):
                        scalar.wait_ge(kb.sems["sv"], t)
                        _li = kb.act(scalar,
                               out=C_AR[:, tc * 1024 + half * 512:tc * 1024 + (half + 1) * 512],
                               in_=ZW[half][:], func=AF.Gelu)
                        _li.then_inc(*kb.inc("ss"))
                    xz_free[half] = kb.tag("ss")
                @block.tensor
                def _(tensor, tc=tc, t=t_wi):
                    tensor.wait_ge(kb.sems["sv"], t)
                    if tc > 0:
                        tensor.wait_ge(kb.sems["sv"], kb.tag("sv"))
                    for g in range(8):
                        _li = tensor.matmul(PX3[:],
                                      AB[:, g * L + tc * 128:g * L + (tc + 1) * 128],
                                      WIF[:, g * PD + 1024:g * PD + 1072],
                                      start=(g == 0), stop=(g == 7))
                    _li.then_inc(*kb.inc("st"))
                t_3 = kb.tag("st")
                @block.vector
                def _(vector, tc=tc, t=t_3):
                    vector.wait_ge(kb.sems["st"], t)
                    _li = vector.tensor_add(out=XZ3[:, tc * 48:(tc + 1) * 48],
                                            in0=PX3[:], in1=BINB[:, 1024:1072])
                    _li.then_inc(*kb.inc("sv"))

        kb.dump("gzc", C_AR[:, :])
        s2 = contextlib.ExitStack()
        AT = s2.enter_context(nc.sbuf_tensor(_un("AT"), [16, L], F32))
        PY = s2.enter_context(nc.psum_tensor(_un("PY"), [1, 2048], F32))
        PT48 = s2.enter_context(nc.psum_tensor(_un("PT48"), [16, 384], F32))
        with s2:
            sbc = contextlib.ExitStack()
            BT = sbc.enter_context(nc.sbuf_tensor(_un("BT"), [16, L], F32))
            CT = sbc.enter_context(nc.sbuf_tensor(_un("CT"), [16, L], F32))
            ST = sbc.enter_context(nc.sbuf_tensor(_un("ST"), [16, L], F32))
            sbc.__enter__()
            kb.barrier()
            for tc in range(NT):
                @block.tensor
                def _(tensor, tc=tc):
                    if tc > 0:
                        kb.wait(tensor, "sv")
                    for kk in range(3):
                        _li = tensor.transpose(PT48[:, kk * 128:(kk + 1) * 128],
                                               XZ3[:, tc * 48 + kk * 16:tc * 48 + (kk + 1) * 16],
                                               IDENTF[:])
                    _li.then_inc(*kb.inc("st"))
                t_t = kb.tag("st")
                @block.vector
                def _(vector, tc=tc, t=t_t):
                    vector.wait_ge(kb.sems["st"], t)
                    for kk, dst in enumerate([BT, CT, AT]):
                        _li = vector.tensor_copy(
                            out=dst[:, tc * 128:(tc + 1) * 128],
                            in_=PT48[0:16, kk * 128:(kk + 1) * 128])
                    _li.then_inc(*kb.inc("sv"))
            @block.scalar
            def _(scalar):
                kb.wait(scalar, "sv")
                kb.act(scalar, out=AT[:], in_=AT[:], func=AF.Exp)
                _li = kb.act(scalar, out=AT[:], in_=AT[:], func=AF.Ln, bias=1.0)
                _li.then_inc(*kb.inc("ss"))
            t_sp = kb.tag("ss")
            @block.scalar
            def _(scalar):
                _li = kb.act(scalar, out=S_COL(8)[0:16, :], in_=ALOG_COL, func=AF.Exp)
                _li.then_inc(*kb.inc("ss"))
            t_a = kb.tag("ss")
            @block.vector
            def _(vector, t=t_a):
                vector.wait_ge(kb.sems["ss"], t)
                vector.tensor_scalar(out=S_COL(8)[0:16, :], in0=S_COL(8)[0:16, :],
                                     scalar1=-1.0, scalar2=None, op0=ALU.mult)
                kb.vgap(vector)
                _li = vector.tensor_scalar(out=AT[:], in0=AT[:],
                                     scalar1=S_COL(8)[0:16, :], scalar2=None,
                                     op0=ALU.mult)
                _li.then_inc(*kb.inc("sv"))
            t_ad = kb.tag("sv")
            @block.scalar
            def _(scalar, t=t_ad):
                scalar.wait_ge(kb.sems["sv"], t)
                _li = kb.act(scalar, out=AT[:], in_=AT[:], func=AF.Exp)
                _li.then_inc(*kb.inc("ss"))
            t_aa = kb.tag("ss")
            @block.vector
            def _(vector, t=t_aa):
                vector.wait_ge(kb.sems["ss"], t)
                vector.tensor_tensor_scan(out=ST[:], data0=AT[:],
                                           data1=BT[:], initial=0.0,
                                           op0=ALU.mult, op1=ALU.add)
                _li = vector.tensor_mul(out=AT[:], in0=ST[:], in1=CT[:])
                _li.then_inc(*kb.inc("sv"))
            t_sc = kb.tag("sv")
            kb.dump("b_t", BT[:, :])
            kb.dump("c_t", CT[:, :])
            kb.dump("st_t", ST[:, :])
            sbc.close()
            YROW = s2.enter_context(nc.sbuf_tensor(_un("YROW"), [1, L], F32))
            YCOL = s2.enter_context(nc.sbuf_tensor(_un("YCOL"), [128, NT], F32))
            @block.tensor
            def _(tensor, t=t_sc):
                tensor.wait_ge(kb.sems["sv"], t)
                for off in range(0, L, 512):
                    _li = tensor.matmul(PY[:, off:off + 512], ONESF[0:16, 0:1],
                                  AT[:, off:off + 512], start=True, stop=True)
                _li.then_inc(*kb.inc("st"))
            t_y = kb.tag("st")
            @block.vector
            def _(vector, t=t_y):
                vector.wait_ge(kb.sems["st"], t)
                _li = vector.tensor_copy(out=YROW[:], in_=PY[:])
                _li.then_inc(*kb.inc("sv"))
            t_yr = kb.tag("sv")
            yc_es = contextlib.ExitStack()
            PYC = yc_es.enter_context(nc.psum_tensor(_un("PYC"), [128, 16], F32))
            yc_es.__enter__()
            @block.tensor
            def _(tensor, t=t_yr):
                tensor.wait_ge(kb.sems["sv"], t)
                for tc in range(NT):
                    _li = tensor.transpose(PYC[:, tc:tc + 1],
                                           YROW[0:1, tc * 128:(tc + 1) * 128],
                                           IDENTF[0:1, 0:1])
                _li.then_inc(*kb.inc("st"))
            t_yc = kb.tag("st")
            @block.vector
            def _(vector, t=t_yc):
                vector.wait_ge(kb.sems["st"], t)
                _li = vector.tensor_copy(out=YCOL[:], in_=PYC[:])
                _li.then_inc(*kb.inc("sv"))
            yc_es.close()
            kb.dump("ycol", YCOL[:, :])
            @block.vector
            def _(vector):
                kb.vgap(vector)
                for tc in range(NT):
                    _li = vector.tensor_scalar(out=D_AR[:, tc * 1024:(tc + 1) * 1024],
                                         in0=C_AR[:, tc * 1024:(tc + 1) * 1024],
                                         scalar1=YCOL[:, tc:tc + 1], scalar2=None,
                                         op0=ALU.mult)
                _li.then_inc(*kb.inc("sv"))
            t_y2 = kb.tag("sv")
            kb.dump("y2c", D_AR[:, :])
            tp_es = contextlib.ExitStack()
            TPB2 = tp_es.enter_context(nc.psum_tensor(_un("TPB2"), [128, 1024], BF16))
            with tp_es:
                for tc in range(NT):
                    @block.tensor
                    def _(tensor, tc=tc, t=t_y2):
                        tensor.wait_ge(kb.sems["sv"], t)
                        if tc > 0:
                            tensor.wait_ge(kb.sems["sv"], kb.tag("sv"))
                        for j in range(8):
                            _li = tensor.transpose(TPB2[:, j * 128:(j + 1) * 128],
                                             D_AR[:, tc * 1024 + j * 128:tc * 1024 + (j + 1) * 128],
                                             IDENT[:])
                        _li.then_inc(*kb.inc("st"))
                    t_tp = kb.tag("st")
                    @block.vector
                    def _(vector, tc=tc, t=t_tp):
                        vector.wait_ge(kb.sems["st"], t)
                        _li = vector.tensor_copy(
                            out=C_AR[:, :].rearrange("p (c t) -> p c t", t=L)[
                                :, :, tc * 128:(tc + 1) * 128],
                            in_=TPB2[:].rearrange("p (c t) -> p c t", c=8))
                        _li.then_inc(*kb.inc("sv"))

        s3 = contextlib.ExitStack()
        STG3 = s3.enter_context(nc.sbuf_tensor(_un("STGo"), [128, 1024], F32))
        WUF = s3.enter_context(nc.sbuf_tensor(_un("WUF"), [128, NC * 1024], BF16))
        BUB = s3.enter_context(nc.sbuf_tensor(_un("BUB"), [128, 1024], F32))
        BUROW = s3.enter_context(nc.sbuf_tensor(_un("BUROW"), [1, 1024], F32))
        BUROW = s3.enter_context(nc.sbuf_tensor(_un("BUROW"), [1, 1024], F32))
        PB3 = s3.enter_context(nc.psum_tensor(_un("PB3"), [128, 1024], F32))
        PWU = [s3.enter_context(nc.psum_tensor(_un(f"PWU{i}"), [128, 1024], F32))
               for i in range(2)]
        with s3:
            kb.barrier()
            for g in range(NC):
                tgt = kb.stream(W["woutN"][g * 128:(g + 1) * 128, :], STG3[:], "stgo")
                @block.vector
                def _(vector, g=g, tgt=tgt):
                    vector.wait_ge(kb.sems["sdw"], tgt)
                    _li = vector.tensor_copy(out=WUF[:, g * 1024:(g + 1) * 1024],
                                       in_=STG3[:])
                    _li.then_inc(*kb.inc("sv"))
                kb.release("stgo", "sv")
            t_wu = kb.tag("sv")
            @block.scalar
            def _(scalar):
                scalar.dma_start(out=BUROW[:], in_=W["bout"][None, :]
                                 ).then_inc(*kb.inc("sdx", 16))
            t_bur = kb.tag("sdx")
            @block.tensor
            def _(tensor, t=t_bur):
                tensor.wait_ge(kb.sems["sdx"], t)
                for off in (0, 512):
                    _li = tensor.matmul(PB3[:, off:off + 512], ONESF[0:1, 0:128],
                                  BUROW[0:1, off:off + 512], start=True, stop=True)
                _li.then_inc(*kb.inc("st"))
            t_bu = kb.tag("st")
            @block.vector
            def _(vector, t=t_bu):
                vector.wait_ge(kb.sems["st"], t)
                _li = vector.tensor_copy(out=BUB[:], in_=PB3[:])
                _li.then_inc(*kb.inc("sv"))
            wu_free = [None, None]
            for tc in range(NT):
                pw = PWU[tc % 2]
                @block.tensor
                def _(tensor, tc=tc, pw=pw, t=t_wu, wf=wu_free[tc % 2]):
                    tensor.wait_ge(kb.sems["sv"], t)
                    if wf is not None:
                        tensor.wait_ge(kb.sems["sv"], wf)
                    for g in range(8):
                        for off in (0, 512):
                            _li = tensor.matmul(
                                pw[:, off:off + 512],
                                C_AR[:, g * L + tc * 128:g * L + (tc + 1) * 128],
                                WUF[:, g * 1024 + off:g * 1024 + off + 512],
                                start=(g == 0), stop=(g == 7))
                    _li.then_inc(*kb.inc("st"))
                t_w = kb.tag("st")
                @block.vector
                def _(vector, tc=tc, pw=pw, t=t_w):
                    vector.wait_ge(kb.sems["st"], t)
                    _li = vector.scalar_tensor_tensor(
                        out=D_AR[:, tc * 1024:(tc + 1) * 1024], in0=pw[:],
                        scalar=1.0, in1=BUB[:], op0=ALU.mult, op1=ALU.add)
                    _li.then_inc(*kb.inc("sv"))
                wu_free[tc % 2] = kb.tag("sv")


def emit_quant_ffn(ctx, layer, D_AR, C_AR):
    nc, block, kb, W = ctx["nc"], ctx["block"], ctx["kb"], ctx["W"]
    XRES, IDENT, IDENTF, ONESF = ctx["XRES"], ctx["IDENT"], ctx["IDENTF"], ctx["ONESF"]
    M_COL, R_COL, S_COL = ctx["M_COL"], ctx["R_COL"], ctx["S_COL"]
    cb_col = ctx["cb_col"]
    BIASC = ctx["BIASC"]
    stage = ctx["stage"]

    fes = contextlib.ExitStack()
    XQ = fes.enter_context(nc.sbuf_tensor(_un(f"XQ{layer}"), [128, NC * L], BF16))
    CVS = fes.enter_context(nc.sbuf_tensor(_un(f"CVS{layer}"), [128, 96], F32))

    def slot(c):
        ar = C_AR if c < 16 else D_AR
        cc = c % 16
        return ar[:, cc * 1024:(cc + 1) * 1024]

    with fes:
        q_es = contextlib.ExitStack()
        XNF = q_es.enter_context(nc.sbuf_tensor(_un("XNF"), [128, D], F32))
        I32T = q_es.enter_context(nc.sbuf_tensor(_un("I32T"), [128, D], I32))
        XQT = q_es.enter_context(nc.sbuf_tensor(_un("XQT"), [128, D], BF16))
        TPB = q_es.enter_context(nc.psum_tensor(_un("TPBq"), [128, 1024], BF16))
        with q_es:
            kb.barrier()
            prev_s = [None]
            prev_tp = [None]
            for tc in range(NT):
                @block.vector
                def _(vector, tc=tc, p=prev_s[0]):
                    if p is not None:
                        vector.wait_ge(kb.sems["ss"], p)
                    vector.tensor_sub(out=XNF[:], in0=XRES[:, tc * D:(tc + 1) * D],
                                      in1=D_AR[:, tc * D:(tc + 1) * D])
                    vector.tensor_scalar(out=XNF[:], in0=XNF[:], scalar1=M_COL(tc),
                                         scalar2=R_COL(tc), op0=ALU.subtract,
                                         op1=ALU.mult)
                    vector.tensor_reduce(out=S_COL(0), in_=XNF[:], axis=AX.X,
                                         op=ALU.max, apply_absolute_value=True)
                    kb.vgap(vector)
                    vector.tensor_scalar(out=S_COL(0), in0=S_COL(0), scalar1=1e-5,
                                         scalar2=None, op0=ALU.max)
                    kb.vgap(vector)
                    vector.reciprocal(out=S_COL(1), in_=S_COL(0))
                    kb.vgap(vector)
                    vector.tensor_scalar(out=S_COL(1), in0=S_COL(1), scalar1=127.0,
                                         scalar2=None, op0=ALU.mult)
                    kb.vgap(vector)
                    _li = vector.reciprocal(out=S_COL(2), in_=S_COL(1))
                    _li.then_inc(*kb.inc("sv"))
                t_xn = kb.tag("sv")
                @block.scalar
                def _(scalar, t=t_xn):
                    scalar.wait_ge(kb.sems["sv"], t)
                    _li = kb.act(scalar, out=I32T[:], in_=XNF[:], func=AF.Copy,
                           scale=S_COL(1))
                    _li.then_inc(*kb.inc("ss"))
                prev_s[0] = kb.tag("ss")
                t_i = kb.tag("ss")
                @block.vector
                def _(vector, t=t_i, p=prev_tp[0]):
                    vector.wait_ge(kb.sems["ss"], t)
                    if p is not None:
                        vector.wait_ge(kb.sems["st"], p)
                    _li = vector.tensor_scalar(out=XQT[:], in0=I32T[:], scalar1=S_COL(2),
                                         scalar2=None, op0=ALU.mult)
                    _li.then_inc(*kb.inc("sv"))
                t_xq = kb.tag("sv")
                @block.tensor
                def _(tensor, t=t_xq):
                    tensor.wait_ge(kb.sems["sv"], t)
                    for j in range(8):
                        _li = tensor.transpose(TPB[:, j * 128:(j + 1) * 128],
                                         XQT[:, j * 128:(j + 1) * 128], IDENT[:])
                    _li.then_inc(*kb.inc("st"))
                prev_tp[0] = kb.tag("st")
                t_tp = kb.tag("st")
                @block.vector
                def _(vector, tc=tc, t=t_tp):
                    vector.wait_ge(kb.sems["st"], t)
                    _li = vector.tensor_copy(
                        out=XQ[:, :].rearrange("p (c t) -> p c t", t=L)[
                            :, :, tc * 128:(tc + 1) * 128],
                        in_=TPB[:].rearrange("p (c t) -> p c t", c=8))
                    _li.then_inc(*kb.inc("sv"))
        kb.dump("xq_fm", XQ[:, :])
        if (layer == 0 and stage < 8) or (layer == 1 and stage < 28):
            return

        w1T = W[f"w1T{layer}"]
        w2T = W[f"w2T{layer}"]

        m_es = contextlib.ExitStack()
        STGM = m_es.enter_context(nc.sbuf_tensor(_un("STGM"), [128, 2048], F32))
        MCOL = m_es.enter_context(nc.sbuf_tensor(_un("MCOL"), [128, 4], F32))
        PS1 = m_es.enter_context(nc.psum_tensor(_un("PS1"), [1, 2], F32))
        PBS = m_es.enter_context(nc.psum_tensor(_un("PBS"), [128, 2], F32))
        with m_es:
            kb.barrier()
            @block.vector
            def _(vector):
                _li = vector.memset(MCOL[:, 0:1], 0.0)
                _li.then_inc(*kb.inc("sv"))
            for g in range(NC):
                for hhalf in range(2):
                    tgt = kb.stream(
                        w1T[g * 128:(g + 1) * 128, hhalf * 2048:(hhalf + 1) * 2048],
                        STGM[:], "stgm")
                    @block.vector
                    def _(vector, tgt=tgt):
                        vector.wait_ge(kb.sems["sdw"], tgt)
                        vector.tensor_reduce(out=MCOL[:, 1:2], in_=STGM[:],
                                             axis=AX.X, op=ALU.add,
                                             apply_absolute_value=True)
                        kb.vgap(vector)
                        _li = vector.tensor_add(out=MCOL[:, 0:1], in0=MCOL[:, 0:1],
                                          in1=MCOL[:, 1:2])
                        _li.then_inc(*kb.inc("sv"))
                    kb.release("stgm", "sv")
            t_m = kb.tag("sv")
            @block.tensor
            def _(tensor, t=t_m):
                tensor.wait_ge(kb.sems["sv"], t)
                _li = tensor.matmul(PS1[:, 0:1], ONESF[:, 0:1], MCOL[:, 0:1],
                              start=True, stop=True)
                _li.then_inc(*kb.inc("st"))
            t_s = kb.tag("st")
            @block.vector
            def _(vector, t=t_s):
                vector.wait_ge(kb.sems["st"], t)
                vector.tensor_scalar(out=MCOL[0:1, 2:3], in0=PS1[0:1, 0:1],
                                     scalar1=1.0 / (DH * D), scalar2=1e-5,
                                     op0=ALU.mult, op1=ALU.max)
                kb.vgap(vector)
                _li = vector.reciprocal(out=MCOL[0:1, 3:4], in_=MCOL[0:1, 2:3])
                _li.then_inc(*kb.inc("sv"))
            t_w = kb.tag("sv")
            @block.tensor
            def _(tensor, t=t_w):
                tensor.wait_ge(kb.sems["sv"], t)
                _li = tensor.matmul(PBS[:, 0:2], ONESF[0:1, 0:128], MCOL[0:1, 2:4],
                              start=True, stop=True)
                _li.then_inc(*kb.inc("st"))
            t_b = kb.tag("st")
            @block.vector
            def _(vector, t=t_b, layer=layer):
                vector.wait_ge(kb.sems["st"], t)
                vector.tensor_copy(out=S_COL(4), in_=PBS[:, 0:1])
                vector.tensor_copy(out=S_COL(5), in_=PBS[:, 1:2])
                kb.vgap(vector)
                _li = vector.tensor_scalar(out=CVS[:],
                                     in0=BIASC[:, 80 + layer * 96:80 + (layer + 1) * 96],
                                     scalar1=S_COL(4), scalar2=None, op0=ALU.mult)
                _li.then_inc(*kb.inc("sv"))

        for tf in range(NTF):
            rm_es = contextlib.ExitStack()
            RMX = rm_es.enter_context(nc.sbuf_tensor(_un("RMX"), [128, TF], BF16))
            RMN = rm_es.enter_context(nc.sbuf_tensor(_un("RMN"), [128, TF], BF16))
            rm_es.__enter__()
            a_es = contextlib.ExitStack()
            STGA = a_es.enter_context(nc.sbuf_tensor(_un("STGA"), [128, 1024], F32))
            W1Q = [a_es.enter_context(nc.sbuf_tensor(_un(f"W1Q{i}"), [128, 1024], BF16))
                   for i in range(2)]
            WI32 = a_es.enter_context(nc.sbuf_tensor(_un("WI32"), [128, 1024], I32))
            CVA = a_es.enter_context(nc.sbuf_tensor(_un("CVA"), [128, TF], F32))
            CVB = a_es.enter_context(nc.sbuf_tensor(_un("CVB"), [128, TF], F32))
            PH = [a_es.enter_context(nc.psum_tensor(_un(f"PH{i}"), [128, 1032], F32))
                  for i in range(2)]
            with a_es:
                kb.barrier()
                @block.gpsimd
                def _(gpsimd):
                    gpsimd.memset(RMX[:], -3.0e38)
                    gpsimd.memset(RMN[:], 3.0e38).then_inc(*kb.inc("sg"))
                t_rm0 = kb.tag("sg")
                ph_free = [None, None]
                w1q_free = [None, None]
                for c in range(32):
                    tgt = kb.stream(
                        w1T.rearrange("(g p) h -> p g h", p=128)[
                            :, :, c * 128:(c + 1) * 128],
                        STGA[:].rearrange("p (g d) -> p g d", g=8), "stga")
                    wi_guard = kb.tag("sv")
                    @block.scalar
                    def _(scalar, tgt=tgt, g=wi_guard):
                        scalar.wait_ge(kb.sems["sdw"], tgt)
                        scalar.wait_ge(kb.sems["sv"], g)
                        _li = kb.act(scalar, out=WI32[:], in_=STGA[:], func=AF.Copy,
                               scale=S_COL(5))
                        _li.then_inc(*kb.inc("ss"))
                    kb.release("stga", "ss")
                    t_wi = kb.tag("ss")
                    @block.vector
                    def _(vector, c=c, t=t_wi, wf=w1q_free[c % 2]):
                        vector.wait_ge(kb.sems["ss"], t)
                        if wf is not None:
                            vector.wait_ge(kb.sems["st"], wf)
                        _li = vector.tensor_scalar(out=W1Q[c % 2][:], in0=WI32[:],
                                             scalar1=-1.0, scalar2=1.0,
                                             op0=ALU.max, op1=ALU.min)
                        _li.then_inc(*kb.inc("sv"))
                    t_wq = kb.tag("sv")
                    ph = PH[c % 2]
                    @block.tensor
                    def _(tensor, tf=tf, c=c, ph=ph, t=t_wq, pf=ph_free[c % 2]):
                        tensor.wait_ge(kb.sems["sv"], t)
                        if pf is not None:
                            tensor.wait_ge(kb.sems["sv"], pf)
                        t0 = tf * TF - 1
                        for (joff, jw) in [(0, 512), (512, 512), (1024, 2)]:
                            lo, w, dst = t0 + joff, jw, joff
                            if lo < 0:
                                lo, w, dst = 0, jw - 1, joff + 1
                            if lo + w > L:
                                w = L - lo
                            if w <= 0:
                                continue
                            for g in range(8):
                                _li = tensor.matmul(
                                    ph[:, dst:dst + w],
                                    W1Q[c % 2][:, g * 128:(g + 1) * 128],
                                    XQ[:, :].rearrange("p (cc t) -> p cc t", t=L)[
                                        :, g, lo:lo + w],
                                    start=(g == 0), stop=(g == 7))
                        _li.then_inc(*kb.inc("st"))
                    w1q_free[c % 2] = kb.tag("st")
                    t_h = kb.tag("st")
                    @block.vector
                    def _(vector, tf=tf, ph=ph, t=t_h):
                        vector.wait_ge(kb.sems["st"], t)
                        if tf == 0:
                            _li = vector.memset(ph[:, 0:1], 0.0)
                        if tf == NTF - 1:
                            _li = vector.memset(ph[:, 1025:1026], 0.0)
                        _li.then_inc(*kb.inc("sv"))
                    t_ed = kb.tag("sv")
                    @block.scalar
                    def _(scalar, layer=layer, c=c, ph=ph, t=t_ed):
                        scalar.wait_ge(kb.sems["sv"], t)
                        _li = kb.act(scalar, out=CVA[:], in_=ph[:, 0:TF],
                                     func=AF.Copy, scale=CVS[:, c:c + 1])
                        _li.then_inc(*kb.inc("ss"))
                    t_c0 = kb.tag("ss")
                    @block.vector
                    def _(vector, c=c, ph=ph, t=t_c0):
                        vector.wait_ge(kb.sems["ss"], t)
                        vector.scalar_tensor_tensor(out=CVB[:], in0=ph[:, 1:TF + 1],
                                                    scalar=CVS[:, 32 + c:33 + c],
                                                    in1=CVA[:], op0=ALU.mult,
                                                    op1=ALU.add)
                        _li = vector.scalar_tensor_tensor(out=CVA[:], in0=ph[:, 2:TF + 2],
                                                    scalar=CVS[:, 64 + c:65 + c],
                                                    in1=CVB[:], op0=ALU.mult,
                                                    op1=ALU.add)
                        _li.then_inc(*kb.inc("sv"))
                    ph_free[c % 2] = kb.tag("sv")
                    t_c2 = kb.tag("sv")
                    @block.scalar
                    def _(scalar, layer=layer, c=c, t=t_c2):
                        scalar.wait_ge(kb.sems["sv"], t)
                        _li = kb.act(scalar, out=slot(c), in_=CVA[:], func=AF.Silu,
                                     bias=cb_col(layer, c))
                        _li.then_inc(*kb.inc("ss"))
                    t_si = kb.tag("ss")
                    @block.vector
                    def _(vector, c=c, t=t_si, t0=t_rm0):
                        vector.wait_ge(kb.sems["ss"], t)
                        vector.wait_ge(kb.sems["sg"], t0)
                        vector.tensor_max(out=RMX[:], in0=RMX[:], in1=slot(c))
                        _li = vector.tensor_tensor(out=RMN[:], in0=RMN[:],
                                                   in1=slot(c), op=ALU.min)
                        _li.then_inc(*kb.inc("sv"))
            emit_h_quant(ctx, layer, tf, slot, RMX, RMN)
            rm_es.close()
            emit_w2(ctx, layer, tf, slot, w2T)


def emit_h_quant(ctx, layer, tf, slot, RMX, RMN):
    nc, block, kb = ctx["nc"], ctx["block"], ctx["kb"]
    IDENTF, ONESF, S_COL = ctx["IDENTF"], ctx["ONESF"], ctx["S_COL"]

    t_es = contextlib.ExitStack()
    SCB = t_es.enter_context(nc.sbuf_tensor(_un("SCB"), [128, TF], F32))
    IVB = t_es.enter_context(nc.sbuf_tensor(_un("IVB"), [128, TF], F32))
    SRow = t_es.enter_context(nc.sbuf_tensor(_un("SRow"), [1, 2 * TF], F32))
    SR8 = t_es.enter_context(nc.sbuf_tensor(_un("SR8"), [16, 128], F32))
    QWQ = t_es.enter_context(nc.sbuf_tensor(_un("QWQ"), [128, TF], F32))
    HI32 = t_es.enter_context(nc.sbuf_tensor(_un("HI32"), [128, TF], I32))
    MRT = t_es.enter_context(nc.sbuf_tensor(_un("MRT"), [128, 16], F32))
    PTQ = t_es.enter_context(nc.psum_tensor(_un("PTQ"), [128, 128], F32))
    PT16 = t_es.enter_context(nc.psum_tensor(_un("PT16q"), [16, 128], F32))
    PSB = t_es.enter_context(nc.psum_tensor(_un("PSBq"), [128, 1024], F32))
    with t_es:
        kb.barrier()
        if layer == 0 and tf == 0 and "hsil" in kb.dbg_out:
            dst = kb.dbg_out["hsil"]
            @block.scalar
            def _(scalar):
                for c in range(32):
                    scalar.dma_start(out=dst[:, c * 1024:(c + 1) * 1024],
                                     in_=slot(c)).then_inc(*kb.inc("sdx", 16))
                scalar.wait_ge(kb.sems["sdx"], kb.tag("sdx"))
        @block.vector
        def _(vector):
            vector.tensor_scalar(out=QWQ[:], in0=RMN[:], scalar1=-1.0,
                                 scalar2=None, op0=ALU.mult)
            _li = vector.tensor_max(out=QWQ[:], in0=QWQ[:], in1=RMX[:])
            _li.then_inc(*kb.inc("sv"))
        t_ab = kb.tag("sv")
        for b in range(TF // 128):
            @block.tensor
            def _(tensor, b=b, t=t_ab):
                tensor.wait_ge(kb.sems["sv"], t if b == 0 else kb.tag("sv"))
                _li = tensor.transpose(PTQ[:], QWQ[:, b * 128:(b + 1) * 128], IDENTF[:])
                _li.then_inc(*kb.inc("st"))
            t_t = kb.tag("st")
            @block.vector
            def _(vector, b=b, t=t_t):
                vector.wait_ge(kb.sems["st"], t)
                _li = vector.tensor_reduce(out=MRT[:, b:b + 1], in_=PTQ[:], axis=AX.X,
                                     op=ALU.max)
                _li.then_inc(*kb.inc("sv"))
        t_mx = kb.tag("sv")
        @block.vector
        def _(vector, t=t_mx):
            vector.wait_ge(kb.sems["sv"], t)
            kb.vgap(vector)
            vector.tensor_scalar(out=MRT[:, 8:16], in0=MRT[:, 0:8], scalar1=1e-5,
                                 scalar2=None, op0=ALU.max)
            kb.vgap(vector)
            vector.reciprocal(out=MRT[:, 0:8], in_=MRT[:, 8:16])
            kb.vgap(vector)
            vector.tensor_scalar(out=MRT[:, 0:8], in0=MRT[:, 0:8], scalar1=127.0,
                                 scalar2=None, op0=ALU.mult)
            kb.vgap(vector)
            _li = vector.reciprocal(out=MRT[:, 8:16], in_=MRT[:, 0:8])
            _li.then_inc(*kb.inc("sv"))
        t_sc = kb.tag("sv")
        @block.tensor
        def _(tensor, t=t_sc):
            tensor.wait_ge(kb.sems["sv"], t)
            _li = tensor.transpose(PT16[:], MRT[:], IDENTF[:])
            _li.then_inc(*kb.inc("st"))
        t_tr = kb.tag("st")
        @block.vector
        def _(vector, t=t_tr):
            vector.wait_ge(kb.sems["st"], t)
            _li = vector.tensor_copy(out=SR8[:], in_=PT16[:])
            _li.then_inc(*kb.inc("sv"))
        t_sr = kb.tag("sv")
        @block.scalar
        def _(scalar, t=t_sr):
            scalar.wait_ge(kb.sems["sv"], t)
            scalar.dma_start(out=SRow[0:1, 0:TF].rearrange("o (b p) -> o b p", p=128),
                             in_=SR8[0:8, :]).then_inc(*kb.inc("sdx", 16))
            scalar.dma_start(out=SRow[0:1, TF:2 * TF].rearrange("o (b p) -> o b p", p=128),
                             in_=SR8[8:16, :]).then_inc(*kb.inc("sdx", 16))
        t_row = kb.tag("sdx")
        @block.tensor
        def _(tensor, t=t_row):
            tensor.wait_ge(kb.sems["sdx"], t)
            for off in (0, 512):
                _li = tensor.matmul(PSB[:, off:off + 512], ONESF[0:1, 0:128],
                              SRow[0:1, off:off + 512], start=True, stop=True)
            _li.then_inc(*kb.inc("st"))
        t_sb = kb.tag("st")
        @block.vector
        def _(vector, t=t_sb):
            vector.wait_ge(kb.sems["st"], t)
            _li = vector.tensor_copy(out=SCB[:], in_=PSB[:])
            _li.then_inc(*kb.inc("sv"))
        t_scb = kb.tag("sv")
        @block.tensor
        def _(tensor, t=t_scb):
            tensor.wait_ge(kb.sems["sv"], t)
            for off in (0, 512):
                _li = tensor.matmul(PSB[:, off:off + 512], ONESF[0:1, 0:128],
                              SRow[0:1, TF + off:TF + off + 512],
                              start=True, stop=True)
            _li.then_inc(*kb.inc("st"))
        t_ib = kb.tag("st")
        @block.vector
        def _(vector, t=t_ib):
            vector.wait_ge(kb.sems["st"], t)
            _li = vector.tensor_copy(out=IVB[:], in_=PSB[:])
            _li.then_inc(*kb.inc("sv"))
        t_ivb = kb.tag("sv")
        if layer == 0 and tf == 0 and "scb" in kb.dbg_out:
            dst2 = kb.dbg_out["scb"]
            @block.scalar
            def _(scalar, t=t_ivb):
                scalar.wait_ge(kb.sems["sv"], t)
                scalar.dma_start(out=dst2[:, 0:1024], in_=SCB[:]
                                 ).then_inc(*kb.inc("sdx", 16))
                scalar.dma_start(out=dst2[:, 1024:2048], in_=IVB[:]
                                 ).then_inc(*kb.inc("sdx", 16))
                scalar.wait_ge(kb.sems["sdx"], kb.tag("sdx"))
        prev_q = [None]
        for c in range(32):
            @block.vector
            def _(vector, c=c, t=t_ivb, p=prev_q[0]):
                vector.wait_ge(kb.sems["sv"], t)
                if p is not None:
                    vector.wait_ge(kb.sems["ss"], p)
                _li = vector.tensor_mul(out=QWQ[:], in0=slot(c), in1=SCB[:])
                _li.then_inc(*kb.inc("sv"))
            t_q1 = kb.tag("sv")
            @block.scalar
            def _(scalar, t=t_q1):
                scalar.wait_ge(kb.sems["sv"], t)
                _li = scalar.copy(out=HI32[:], in_=QWQ[:])
                _li.then_inc(*kb.inc("ss"))
            prev_q[0] = kb.tag("ss")
            t_q2 = kb.tag("ss")
            @block.vector
            def _(vector, c=c, t=t_q2):
                vector.wait_ge(kb.sems["ss"], t)
                _li = vector.tensor_tensor(out=slot(c), in0=HI32[:], in1=IVB[:],
                                     op=ALU.mult)
                _li.then_inc(*kb.inc("sv"))


def emit_w2(ctx, layer, tf, slot, w2T):
    nc, block, kb = ctx["nc"], ctx["block"], ctx["kb"]
    XRES, IDENT, ONESF, S_COL = ctx["XRES"], ctx["IDENT"], ctx["ONESF"], ctx["S_COL"]

    w_es = contextlib.ExitStack()
    STG2 = w_es.enter_context(nc.sbuf_tensor(_un("STG2"), [128, 2048], F32))
    W2C = [w_es.enter_context(nc.sbuf_tensor(_un(f"W2C{i}"), [128, 4096], BF16))
           for i in range(2)]
    WI2 = w_es.enter_context(nc.sbuf_tensor(_un("WI2"), [128, 2048], I32))
    MC2 = w_es.enter_context(nc.sbuf_tensor(_un("MC2"), [128, 4], F32))
    WEV = w_es.enter_context(nc.sbuf_tensor(_un("WEV"), [128, 512], BF16))
    PS2 = w_es.enter_context(nc.psum_tensor(_un("PS2"), [1, 2], F32))
    PBS2 = w_es.enter_context(nc.psum_tensor(_un("PBS2"), [128, 2], F32))
    PW = [w_es.enter_context(nc.psum_tensor(_un(f"PW{i}"), [128, 512], F32))
          for i in range(2)]
    PWT = w_es.enter_context(nc.psum_tensor(_un("PWT"), [128, 512], BF16))
    with w_es:
        kb.barrier()
        if tf == 0:
            @block.vector
            def _(vector):
                _li = vector.memset(MC2[:, 0:1], 0.0)
                _li.then_inc(*kb.inc("sv"))
            for g in range(16):
                tgt = kb.stream(
                    w2T.rearrange("(g p) d -> p g d", p=128)[:, g * 2:g * 2 + 2, :],
                    STG2[:].rearrange("p (g d) -> p g d", g=2), "stg2w")
                @block.vector
                def _(vector, tgt=tgt):
                    vector.wait_ge(kb.sems["sdw"], tgt)
                    vector.tensor_reduce(out=MC2[:, 1:2], in_=STG2[:], axis=AX.X,
                                         op=ALU.add, apply_absolute_value=True)
                    kb.vgap(vector)
                    _li = vector.tensor_add(out=MC2[:, 0:1], in0=MC2[:, 0:1],
                                      in1=MC2[:, 1:2])
                    _li.then_inc(*kb.inc("sv"))
                kb.release("stg2w", "sv")
            t_m = kb.tag("sv")
            @block.tensor
            def _(tensor, t=t_m):
                tensor.wait_ge(kb.sems["sv"], t)
                _li = tensor.matmul(PS2[:, 0:1], ONESF[:, 0:1], MC2[:, 0:1],
                              start=True, stop=True)
                _li.then_inc(*kb.inc("st"))
            t_s = kb.tag("st")
            @block.vector
            def _(vector, t=t_s):
                vector.wait_ge(kb.sems["st"], t)
                vector.tensor_scalar(out=MC2[0:1, 2:3], in0=PS2[0:1, 0:1],
                                     scalar1=1.0 / (DH * D), scalar2=1e-5,
                                     op0=ALU.mult, op1=ALU.max)
                kb.vgap(vector)
                _li = vector.reciprocal(out=MC2[0:1, 3:4], in_=MC2[0:1, 2:3])
                _li.then_inc(*kb.inc("sv"))
            t_r = kb.tag("sv")
            @block.tensor
            def _(tensor, t=t_r):
                tensor.wait_ge(kb.sems["sv"], t)
                _li = tensor.matmul(PBS2[:, 0:2], ONESF[0:1, 0:128], MC2[0:1, 2:4],
                              start=True, stop=True)
                _li.then_inc(*kb.inc("st"))
            t_b = kb.tag("st")
            @block.vector
            def _(vector, t=t_b):
                vector.wait_ge(kb.sems["st"], t)
                vector.tensor_copy(out=S_COL(6), in_=PBS2[:, 0:1])
                _li = vector.tensor_copy(out=S_COL(7), in_=PBS2[:, 1:2])
                _li.then_inc(*kb.inc("sv"))
        hq_done = kb.tag("sv")
        w2c_free = [None, None]
        for dc in range(8):
            for hhalf in range(2):
                tgt = kb.stream(
                    w2T.rearrange("(g p) d -> p g d", p=128)[
                        :, hhalf * 16:(hhalf + 1) * 16, dc * 128:(dc + 1) * 128],
                    STG2[:].rearrange("p (g d) -> p g d", g=16), "stg2w")
                wi2_guard = kb.tag("sv")
                @block.scalar
                def _(scalar, tgt=tgt, wf=w2c_free[dc % 2], hhalf=hhalf,
                      g=wi2_guard):
                    scalar.wait_ge(kb.sems["sdw"], tgt)
                    scalar.wait_ge(kb.sems["sv"], g)
                    if wf is not None and hhalf == 0:
                        scalar.wait_ge(kb.sems["st"], wf)
                    _li = kb.act(scalar, out=WI2[:], in_=STG2[:], func=AF.Copy,
                           scale=S_COL(7))
                    _li.then_inc(*kb.inc("ss"))
                kb.release("stg2w", "ss")
                t_wi = kb.tag("ss")
                @block.vector
                def _(vector, dc=dc, hhalf=hhalf, t=t_wi):
                    vector.wait_ge(kb.sems["ss"], t)
                    _li = vector.tensor_scalar(
                        out=W2C[dc % 2][:, hhalf * 2048:(hhalf + 1) * 2048],
                        in0=WI2[:], scalar1=-1.0, scalar2=1.0,
                        op0=ALU.max, op1=ALU.min)
                    _li.then_inc(*kb.inc("sv"))
            t_wq = kb.tag("sv")
            pw_free = [None, None]
            for tp in range(TF // 512):
                pw = PW[tp % 2]
                @block.tensor
                def _(tensor, dc=dc, tp=tp, pw=pw, t=t_wq, hd=hq_done,
                      pf=pw_free[tp % 2]):
                    tensor.wait_ge(kb.sems["sv"], max(t, hd))
                    if pf is not None:
                        tensor.wait_ge(kb.sems["sv"], pf)
                    for cc in range(32):
                        _li = tensor.matmul(
                            pw[:],
                            W2C[dc % 2][:, cc * 128:(cc + 1) * 128],
                            slot(cc)[:, tp * 512:(tp + 1) * 512],
                            start=(cc == 0), stop=(cc == 31))
                    _li.then_inc(*kb.inc("st"))
                t_w = kb.tag("st")
                @block.vector
                def _(vector, pw=pw, t=t_w):
                    vector.wait_ge(kb.sems["st"], t)
                    _li = vector.tensor_copy(out=WEV[:], in_=pw[:])
                    _li.then_inc(*kb.inc("sv"))
                pw_free[tp % 2] = kb.tag("sv")
                t_e = kb.tag("sv")
                for tb in range(4):
                    @block.tensor
                    def _(tensor, tb=tb, t=t_e):
                        tensor.wait_ge(kb.sems["sv"], t if tb == 0 else kb.tag("sv"))
                        _li = tensor.transpose(PWT[:, 0:128],
                                         WEV[:, tb * 128:(tb + 1) * 128], IDENT[:])
                        _li.then_inc(*kb.inc("st"))
                    t_t = kb.tag("st")
                    @block.vector
                    def _(vector, tf=tf, dc=dc, tp=tp, tb=tb, t=t_t):
                        vector.wait_ge(kb.sems["st"], t)
                        tile = tf * (TF // 128) + tp * 4 + tb
                        _li = vector.scalar_tensor_tensor(
                            out=XRES[:, tile * 1024 + dc * 128:tile * 1024 + (dc + 1) * 128],
                            in0=PWT[:, 0:128], scalar=S_COL(6),
                            in1=XRES[:, tile * 1024 + dc * 128:tile * 1024 + (dc + 1) * 128],
                            op0=ALU.mult, op1=ALU.add)
                        _li.then_inc(*kb.inc("sv"))
            w2c_free[dc % 2] = kb.tag("st")


def kernel(**inputs):
    xs = np.ascontiguousarray(np.asarray(inputs["L0_attn_x"], np.float32))
    B = xs.shape[0]

    def cc(a):
        return np.ascontiguousarray(np.asarray(a, np.float32))

    base = {
        "wqT": cc(np.asarray(inputs["L0_attn_Wq"]).T),
        "bq": cc(inputs["L0_attn_bq"]),
        "wkT": cc(np.asarray(inputs["L0_attn_Wk"]).T),
        "bk": cc(inputs["L0_attn_bk"]),
        "wvT": cc(np.asarray(inputs["L0_attn_Wv"]).T),
        "bv": cc(inputs["L0_attn_bv"]),
        "woT": cc(np.asarray(inputs["L0_attn_Wo"]).T),
        "bo": cc(inputs["L0_attn_bo"]),
        "w1T0": cc(np.asarray(inputs["L0_ffn_w1"]).T),
        "cw0": cc(np.asarray(inputs["L0_ffn_conv_w"]).reshape(DH, 3)),
        "cb0": cc(inputs["L0_ffn_conv_b"]),
        "w2T0": cc(np.asarray(inputs["L0_ffn_w2"]).T),
        "winS": cc(np.asarray(inputs["L1_attn_W_in"])[:, D:]),
        "binS": cc(np.asarray(inputs["L1_attn_b_in"])[D:]),
        "alog": cc(inputs["L1_attn_A_log"]),
        "woutN": cc(inputs["L1_attn_W_out"]),
        "bout": cc(inputs["L1_attn_b_out"]),
        "w1T1": cc(np.asarray(inputs["L1_ffn_w1"]).T),
        "cw1": cc(np.asarray(inputs["L1_ffn_conv_w"]).reshape(DH, 3)),
        "cb1": cc(inputs["L1_ffn_conv_b"]),
        "w2T1": cc(np.asarray(inputs["L1_ffn_w2"]).T),
    }
    nc = build(debug=_DEBUG, stage=_STAGE)
    in_maps = [dict(base, x=np.ascontiguousarray(xs[b])) for b in range(B)]
    res = run_bass_kernel_spmd(nc, in_maps, core_ids=list(range(B)))
    out = np.stack([np.asarray(r["y"]) for r in res.results], axis=0)
    if _DEBUG:
        kernel.dbg = [{k: np.asarray(v) for k, v in r.items()
                       if k.startswith("dbg_")} for r in res.results]
    return out


# version 07
# speedup vs baseline: 1.0623x; 1.0623x over previous
import contextlib
import numpy as np
import concourse.bass as bass
import concourse.mybir as mybir
from concourse.bass_utils import run_bass_kernel_spmd

F32 = mybir.dt.float32
F16 = mybir.dt.float16
I32 = mybir.dt.int32
BF16 = mybir.dt.bfloat16
AF = mybir.ActivationFunctionType
ALU = mybir.AluOpType
AX = mybir.AxisListType

L, D, H, DK, MAXF, DH, SD, PD = 2048, 1024, 16, 64, 10, 4096, 16, 1072
NT, NC, TF = L // 128, D // 128, 1024
NTF = L // TF
NEG = -3.0e38
SENT = 2.5e38

_DEBUG = ()
_STAGE = 99
_DBGL = 0


_NAME_N = [0]


def _un(n):
    _NAME_N[0] += 1
    return f"{n}_{_NAME_N[0]}"


class KB:
    def __init__(self, nc, block, es, dbg_out):
        self.nc, self.block, self.es, self.dbg_out = nc, block, es, dbg_out
        self.sems, self.cnt = {}, {}
        for name in ["sdw", "sdx", "sdg", "st", "sv", "ss", "sg", "bar"]:
            self.sems[name] = es.enter_context(nc.semaphore(name))
            self.cnt[name] = 0
        self.gq = []
        self.bg = []
        from concourse.hw_specs import get_activation_tables
        self.tables = {}
        for idx, funcs in enumerate(get_activation_tables(nc.m.arch).values()):
            for f in funcs:
                self.tables.setdefault(f, idx)
        self.cur_table = None
        self.wq = []
        self._slot_rel = {}
        self._bar_snapshot = []

    def inc(self, name, n=1):
        self.cnt[name] += n
        return self.sems[name], n

    def tag(self, name):
        return self.cnt[name]

    def wait(self, eng, name, value=None):
        eng.wait_ge(self.sems[name], self.cnt[name] if value is None else value)

    def barrier(self, prev=None,
                nxt=("vector", "scalar", "tensor", "gpsimd")):
        targets = [(self.sems[n], self.cnt[n]) for n in ["sv", "ss", "st", "sg"]
                   if self.cnt[n] > 0]
        self._bar_snapshot = targets
        self._slot_rel = {}
        for e in nxt:
            @getattr(self.block, e)
            def _(eng):
                for sem, val in targets:
                    eng.wait_ge(sem, val)

    def act(self, scalar, out, in_, func, **kw):
        if func != AF.Copy or self.cur_table is None:
            tid = self.tables[func]
            if tid != self.cur_table:
                scalar.preload_act_table(func)
                self.cur_table = tid
        return scalar.activation(out=out, in_=in_, func=func, **kw)

    def vgap(self, vector, n=1):
        for _ in range(n):
            vector.tensor_copy(out=self.JNK[:, 0:128], in_=self.JNK[:, 128:256])

    def sgap(self, scalar, n=1):
        for _ in range(n):
            scalar.activation(out=self.JNK[:, 0:128], in_=self.JNK[:, 128:256],
                              func=mybir.ActivationFunctionType.Copy)

    def stream(self, src_ap, dst_ap, slot):
        rel = self._slot_rel.get(slot)
        rels = [rel] if rel is not None else list(self._bar_snapshot)
        self.wq.append((src_ap, dst_ap, rels))
        self.cnt["sdw"] += 16
        return self.cnt["sdw"]

    def release(self, slot, sem_name):
        self._slot_rel[slot] = (self.sems[sem_name], self.cnt[sem_name])

    def new_phase_slots(self):
        self._slot_rel = {}

    def flush_stream(self):
        entries = list(self.wq)
        @self.block.sync
        def _(sync):
            for src, dst, rels in entries:
                for sem, val in rels:
                    sync.wait_ge(sem, val)
                sync.dma_start(out=dst, in_=src).then_inc(self.sems["sdw"], 16)
        self.wq = []

    def flush_gq(self):
        entries = list(self.gq)
        self.gq = []
        @self.block.gpsimd
        def _(gpsimd):
            for src, dst, rels in entries:
                for sem, val in rels:
                    gpsimd.wait_ge(sem, val)
                gpsimd.dma_start(out=dst, in_=src).then_inc(self.sems["sdg"], 16)

    def drain_bg(self, n=None):
        k = len(self.bg) if n is None else min(n, len(self.bg))
        for _ in range(k):
            self.bg.pop(0)()

    def dump(self, name, src_ap):
        if name not in self.dbg_out:
            return
        dst = self.dbg_out[name]
        @self.block.scalar
        def _(scalar):
            self.wait(scalar, "sv")
            self.wait(scalar, "ss")
            self.wait(scalar, "st")
            p, f = src_ap.shape[0], src_ap.shape[-1]
            scalar.dma_start(out=dst[0:p, 0:f] if len(src_ap.shape) == 2 else dst,
                             in_=src_ap).then_inc(*self.inc("sdx", 16))
            self.wait(scalar, "sdx")


def build(debug=(), stage=99):
    nc = bass.Bass("TRN2", target_bir_lowering=False,
                   detect_race_conditions=False)
    es = contextlib.ExitStack()

    def din(name, shape):
        return nc.dram_tensor(name, shape, F32, kind="ExternalInput").ap()

    W = {"x": din("x", [L, D])}
    for n, sh in [("wqT", [D, D]), ("bq", [D]), ("wkT", [D, D]), ("bk", [D]),
                  ("wvT", [D, D]), ("bv", [D]), ("woT", [D, D]), ("bo", [D]),
                  ("winS", [D, PD]), ("binS", [PD]), ("alog", [SD]),
                  ("woutN", [D, D]), ("bout", [D])]:
        W[n] = din(n, sh)
    for l in range(2):
        W[f"w1T{l}"] = din(f"w1T{l}", [D, DH])
        W[f"cw{l}"] = din(f"cw{l}", [DH, 3])
        W[f"cb{l}"] = din(f"cb{l}", [DH])
        W[f"w2T{l}"] = din(f"w2T{l}", [DH, D])
    y_hbm = nc.dram_tensor("y", [L, D], F32, kind="ExternalOutput").ap()

    dbg_shapes = {
        "xn_fm": ([128, NC * L], BF16), "xq_fm": ([128, NC * L], BF16),
        "qs": ([16, L], F32), "sel": ([128, 2 * L], BF16),
        "selt": ([128, NT * 160], F32), "xng": ([128, NC * 160], BF16),
        "q8": ([128, 8 * 80], BF16), "kfm": ([128, NC * L], BF16),
        "vtm": ([128, NT * D], BF16), "attn_out": ([128, NT * D], BF16),
        "x_l0": ([128, NT * D], F32), "gz": ([128, NT * D], BF16),
        "attb": ([128, 2048], BF16), "outg": ([128, 8 * 128], BF16),
        "vmk": ([16, 10], F32), "ucol": ([16, 1], F32),
        "outall": ([128, 512], F32),
        "selfs": ([80, 2048], BF16), "full": ([128, NC * L], BF16),
        "hsil": ([128, 32 * 1024], BF16), "scb": ([128, 2 * 1024], F32),
        "b_t": ([16, L], F32), "c_t": ([16, L], F32), "st_t": ([16, L], F32),
        "ycol": ([128, 16], F32), "a_t": ([16, L], F32),
        "gzc": ([128, 16 * 1024], BF16), "y2c": ([128, 16 * 1024], BF16),
        "bcds": ([128, L], F32), "hq": ([128, 32 * TF], BF16),
        "stat": ([16, L], F32), "macc": ([128, 16], F32),
    }
    dbg_out = {}
    for name in debug:
        sh, dt = dbg_shapes[name]
        dbg_out[name] = nc.dram_tensor("dbg_" + name, sh, dt,
                                       kind="ExternalOutput").ap()

    XRES = es.enter_context(nc.sbuf_tensor(_un("XRES"), [128, NT * D], F32))
    IDENT = es.enter_context(nc.sbuf_tensor(_un("IDENT"), [128, 128], BF16))
    IDENTF = es.enter_context(nc.sbuf_tensor(_un("IDENTF"), [128, 128], F32))
    ONESF = es.enter_context(nc.sbuf_tensor(_un("ONESF"), [128, 128], F32))
    IOTA = es.enter_context(nc.sbuf_tensor(_un("IOTA"), [128, 128], F32))
    B16W = es.enter_context(nc.sbuf_tensor(_un("B16W"), [128, 48], BF16))
    MASKP = es.enter_context(nc.sbuf_tensor(_un("MASKP"), [128, 8], F32))
    E80W = es.enter_context(nc.sbuf_tensor(_un("E80W"), [16, 160], F32))
    ETMP = es.enter_context(nc.sbuf_tensor(_un("ETMP"), [16, 160], F32))
    COLS = es.enter_context(nc.sbuf_tensor(_un("COLS"), [128, 64], F32))
    BIASC = es.enter_context(nc.sbuf_tensor(_un("BIASC"), [128, 288], F32))
    CONSTV = es.enter_context(nc.sbuf_tensor(_un("CONSTV"), [128, 4], F32))
    JNK = es.enter_context(nc.sbuf_tensor(_un("JNK"), [128, 256], F32))
    _cvals = [1e-5, 1.0 / D, 1.0]
    for _i, _v in enumerate(_cvals):
        nc.const_aps.register(F32, _v, CONSTV[:, _i:_i + 1])

    ctx = {}
    with es, nc.Block() as block:
        kb = KB(nc, block, es, dbg_out)
        kb.JNK = JNK
        ctx.update(nc=nc, block=block, kb=kb, W=W, XRES=XRES, IDENT=IDENT,
                   IDENTF=IDENTF, ONESF=ONESF, IOTA=IOTA, B16W=B16W, E80W=E80W,
                   MASKP=MASKP, COLS=COLS, BIASC=BIASC, stage=stage)
        ctx["M_COL"] = lambda tc: COLS[:, tc:tc + 1]
        ctx["R_COL"] = lambda tc: COLS[:, 16 + tc:17 + tc]
        ctx["S_COL"] = lambda i: COLS[:, 32 + i:33 + i]
        ctx["bq_col"] = lambda c: BIASC[:, c:c + 1]
        ctx["bk_col"] = lambda c: BIASC[:, 8 + c:9 + c]
        ctx["cb_col"] = lambda l, c: BIASC[:, 16 + l * 32 + c:17 + l * 32 + c]
        ctx["cw_col"] = lambda l, k, c: BIASC[:, 80 + l * 96 + k * 32 + c:81 + l * 96 + k * 32 + c]
        ctx["ALOG_COL"] = BIASC[0:SD, 272:273]

        @block.scalar
        def _(scalar):
            for tc in range(NT):
                scalar.dma_start(out=XRES[:, tc * D:(tc + 1) * D],
                                 in_=W["x"][tc * 128:(tc + 1) * 128, :]
                                 ).then_inc(*kb.inc("sdx", 16))
            for vec, off in [(W["bq"], 0), (W["bk"], 8)]:
                scalar.dma_start(out=BIASC[:, off:off + 8],
                                 in_=vec.rearrange("(c p) -> p c", p=128),
                                 allow_slow_non_contiguous=True,
                                 ).then_inc(*kb.inc("sdx", 16))
            for l in range(2):
                scalar.dma_start(out=BIASC[:, 16 + l * 32:16 + (l + 1) * 32],
                                 in_=W[f"cb{l}"].rearrange("(c p) -> p c", p=128),
                                 allow_slow_non_contiguous=True,
                                 ).then_inc(*kb.inc("sdx", 16))
                scalar.dma_start(
                    out=BIASC[:, 80 + l * 96:80 + (l + 1) * 96].rearrange(
                        "p (k c) -> p k c", k=3),
                    in_=W[f"cw{l}"].rearrange("(c p) k -> p k c", p=128),
                    allow_slow_non_contiguous=True,
                ).then_inc(*kb.inc("sdx", 16))
            scalar.dma_start(out=BIASC[0:SD, 272:273], in_=W["alog"][:, None]
                             ).then_inc(*kb.inc("sdx", 16))
        x_dma_target = kb.tag("sdx")

        @block.gpsimd
        def _(gpsimd):
            gpsimd.iota(IOTA[:], pattern=[[-1, 128]], base=0, channel_multiplier=1,
                        allow_small_or_imprecise_dtypes=True
                        ).then_inc(*kb.inc("sg"))

        @block.vector
        def _(vector):
            kb.wait(vector, "sg")
            vector.tensor_single_scalar(out=IDENT[:], in_=IOTA[:], scalar=0.0,
                                        op=ALU.is_equal)
            vector.tensor_single_scalar(out=IDENTF[:], in_=IOTA[:], scalar=0.0,
                                        op=ALU.is_equal)
            vector.memset(ONESF[:], 1.0)
            vector.memset(JNK[:], 0.0)
            for _i, _v in enumerate(_cvals):
                vector.memset(CONSTV[:, _i:_i + 1], _v)
            vector.tensor_single_scalar(out=COLS[:, 32:33], in_=IOTA[:, 0:1],
                                        scalar=64.0, op=ALU.is_ge)
            kb.vgap(vector)
            _li = vector.tensor_scalar(out=COLS[:, 32:33], in0=COLS[:, 32:33],
                                 scalar1=1.0, scalar2=16.0, op0=ALU.mult,
                                 op1=ALU.add)
            _li.then_inc(*kb.inc("sv"))

        @block.gpsimd
        def _(gpsimd):
            kb.wait(gpsimd, "sv")
            gpsimd.iota(IOTA[:, 0:48], pattern=[[1, 48]], base=0,
                        channel_multiplier=0, allow_small_or_imprecise_dtypes=True
                        ).then_inc(*kb.inc("sg"))

        @block.vector
        def _(vector):
            kb.wait(vector, "sg")
            _li = vector.tensor_single_scalar(out=B16W[:], in_=IOTA[:, 0:48],
                                        scalar=COLS[:, 32:33], op=ALU.is_equal)
            _li.then_inc(*kb.inc("sv"))

        @block.gpsimd
        def _(gpsimd):
            kb.wait(gpsimd, "sv")
            gpsimd.iota(ETMP[:], pattern=[[1, 160]], base=0, channel_multiplier=-10,
                        allow_small_or_imprecise_dtypes=True)
            gpsimd.iota(IOTA[0:16, 100:110], pattern=[[1, 10]], base=0,
                        channel_multiplier=0, allow_small_or_imprecise_dtypes=True
                        ).then_inc(*kb.inc("sg"))

        @block.vector
        def _(vector):
            kb.wait(vector, "sg")
            vector.tensor_single_scalar(out=E80W[:], in_=ETMP[:], scalar=0.0,
                                        op=ALU.is_ge)
            vector.tensor_single_scalar(out=ETMP[:], in_=ETMP[:], scalar=10.0,
                                        op=ALU.is_lt)
            vector.tensor_mul(out=E80W[:], in0=E80W[:], in1=ETMP[:])
            for hl in range(8):
                vector.tensor_single_scalar(out=MASKP[:, hl:hl + 1],
                                            in_=IOTA[:, 127:128],
                                            scalar=float(10 * hl - 127), op=ALU.is_ge)
            for hl in range(8):
                vector.tensor_single_scalar(out=COLS[:, 56 + hl:57 + hl],
                                            in_=IOTA[:, 127:128],
                                            scalar=float(10 * hl + 10 - 127),
                                            op=ALU.is_lt)
            for hl in range(8):
                vector.tensor_mul(out=MASKP[:, hl:hl + 1],
                                  in0=MASKP[:, hl:hl + 1],
                                  in1=COLS[:, 56 + hl:57 + hl])
            _li = vector.memset(COLS[:, 56:64], 0.0)
            _li.then_inc(*kb.inc("sv"))

        sv0, sg0, ss0 = kb.tag("sv"), kb.tag("sg"), kb.tag("ss")
        for ename in ["vector", "scalar", "tensor", "gpsimd"]:
            @getattr(block, ename)
            def _(eng):
                eng.wait_ge(kb.sems["sdx"], x_dma_target)
                eng.wait_ge(kb.sems["sv"], sv0)
                eng.wait_ge(kb.sems["sg"], sg0)

        if stage >= 1:
            emit_layer(ctx, 0)
        elif "qs" in dbg_out:
            pass
        if stage >= 20:
            emit_layer(ctx, 1)

        kb.barrier(nxt=("scalar",))
        @block.scalar
        def _(scalar):
            for tc in range(NT):
                scalar.dma_start(out=y_hbm[tc * 128:(tc + 1) * 128, :],
                                 in_=XRES[:, tc * D:(tc + 1) * D]
                                 ).then_inc(*kb.inc("sdx", 16))
        out_target = kb.tag("sdx")
        kb.flush_stream()
        @block.sync
        def _(sync):
            sync.wait_ge(kb.sems["sdx"], out_target)

    return nc


def emit_layer(ctx, layer):
    nc, block, kb, stage = ctx["nc"], ctx["block"], ctx["kb"], ctx["stage"]
    XRES = ctx["XRES"]

    les = contextlib.ExitStack()
    C_AR = les.enter_context(nc.sbuf_tensor(_un(f"C{layer}"), [128, NC * L], BF16))
    D_AR = les.enter_context(nc.sbuf_tensor(_un(f"D{layer}"), [128, NT * D], BF16))
    with les:
        abes = contextlib.ExitStack()
        AB = abes.enter_context(nc.sbuf_tensor(_un(f"AB{layer}"), [128, NC * L], BF16))
        with abes:
            emit_ln(ctx, AB, layer)
            if layer == 0:
                kb.dump("xn_fm", AB[:, :])
                if stage < 2:
                    return
                emit_l0_attn(ctx, AB, C_AR, D_AR)
            else:
                emit_l1_attn(ctx, AB, C_AR, D_AR)
        if (layer == 0 and stage < 6) or (layer == 1 and stage < 26):
            return
        kb.dump("attn_out" if layer == 0 else "gz", D_AR[:, :])
        kb.barrier(nxt=("vector",))
        @block.vector
        def _(vector):
            for tc in range(NT):
                _li = vector.tensor_add(out=XRES[:, tc * D:(tc + 1) * D],
                                  in0=XRES[:, tc * D:(tc + 1) * D],
                                  in1=D_AR[:, tc * D:(tc + 1) * D])
            _li.then_inc(*kb.inc("sv"))
        if (layer == 0 and stage < 7) or (layer == 1 and stage < 27):
            kb.dump("x_l0", XRES[:, :])
            return
        kb.drain_bg()
        if layer == 0 and stage >= 28:
            emit_wmean(ctx, 1, les)
        emit_quant_ffn(ctx, layer, D_AR, C_AR)
        if layer == 0:
            kb.dump("x_l0", XRES[:, :])


def emit_ln(ctx, AB, layer):
    nc, block, kb = ctx["nc"], ctx["block"], ctx["kb"]
    XRES, IDENT = ctx["XRES"], ctx["IDENT"]
    COLS = ctx["COLS"]
    M_COL, R_COL, S_COL = ctx["M_COL"], ctx["R_COL"], ctx["S_COL"]
    NM_COL = lambda tc: COLS[:, 48 + tc:49 + tc]

    ln_es = contextlib.ExitStack()
    QWK = ln_es.enter_context(nc.sbuf_tensor(_un("QWK"), [128, D], F32))
    XNT = [ln_es.enter_context(nc.sbuf_tensor(_un(f"XNT{i}"), [128, D], BF16))
           for i in range(2)]
    TPB = ln_es.enter_context(nc.psum_tensor(_un("TPB"), [128, 1024], BF16))
    with ln_es:
        kb.barrier()
        @block.vector
        def _(vector):
            for tc in range(NT):
                vector.reduce_sum(out=NM_COL(tc), in_=XRES[:, tc * D:(tc + 1) * D],
                                  axis=AX.X)
            kb.vgap(vector)
            vector.tensor_scalar(out=COLS[:, 0:16], in0=COLS[:, 48:64],
                                 scalar1=1.0 / D, scalar2=None, op0=ALU.mult)
            kb.vgap(vector)
            _li = vector.tensor_scalar(out=COLS[:, 48:64], in0=COLS[:, 48:64],
                                       scalar1=-1.0 / D, scalar2=None,
                                       op0=ALU.mult)
            _li.then_inc(*kb.inc("sv"))
        t_means = kb.tag("sv")
        @block.scalar
        def _(scalar, t=t_means):
            scalar.wait_ge(kb.sems["sv"], t)
            for tc in range(NT):
                _li = kb.act(scalar, out=QWK[:], in_=XRES[:, tc * D:(tc + 1) * D],
                             func=AF.Square, bias=NM_COL(tc),
                             accum_out=R_COL(tc))
            _li.then_inc(*kb.inc("ss"))
        t_ssq = kb.tag("ss")
        @block.vector
        def _(vector, t=t_ssq):
            vector.wait_ge(kb.sems["ss"], t)
            _li = vector.tensor_scalar(out=COLS[:, 16:32], in0=COLS[:, 16:32],
                                       scalar1=1.0 / D, scalar2=1e-5,
                                       op0=ALU.mult, op1=ALU.add)
            _li.then_inc(*kb.inc("sv"))
        t_c1 = kb.tag("sv")
        @block.scalar
        def _(scalar, t=t_c1):
            scalar.wait_ge(kb.sems["sv"], t)
            _li = kb.act(scalar, out=COLS[:, 16:32], in_=COLS[:, 16:32],
                         func=AF.Sqrt)
            _li.then_inc(*kb.inc("ss"))
        t_c2 = kb.tag("ss")
        @block.vector
        def _(vector, t=t_c2):
            vector.wait_ge(kb.sems["ss"], t)
            vector.reciprocal(out=COLS[:, 16:32], in_=COLS[:, 16:32])
            kb.vgap(vector)
            _li = vector.tensor_copy(out=kb.JNK[:, 0:1], in_=COLS[:, 16:17])
            _li.then_inc(*kb.inc("sv"))
        t_rstd = kb.tag("sv")
        last_ev = [None]
        for tc in range(NT):
            xnb = XNT[tc % 2]
            @block.vector
            def _(vector, tc=tc, xnb=xnb, t=t_rstd, lt=last_ev[0]):
                vector.wait_ge(kb.sems["sv"], t)
                if lt is not None:
                    vector.wait_ge(kb.sems["st"], lt)
                _li = vector.tensor_scalar(out=xnb[:],
                                           in0=XRES[:, tc * D:(tc + 1) * D],
                                           scalar1=M_COL(tc), scalar2=R_COL(tc),
                                           op0=ALU.subtract, op1=ALU.mult)
                _li.then_inc(*kb.inc("sv"))
            t_xn = kb.tag("sv")
            @block.tensor
            def _(tensor, xnb=xnb, t=t_xn):
                tensor.wait_ge(kb.sems["sv"], t)
                for j in range(8):
                    _li = tensor.transpose(TPB[:, j * 128:(j + 1) * 128],
                                     xnb[:, j * 128:(j + 1) * 128], IDENT[:])
                _li.then_inc(*kb.inc("st"))
            last_ev[0] = kb.tag("st")
            t_tp = kb.tag("st")
            @block.vector
            def _(vector, tc=tc, t=t_tp):
                vector.wait_ge(kb.sems["st"], t)
                _li = vector.tensor_copy(
                    out=AB[:, :].rearrange("p (c t) -> p c t", t=L)[
                        :, :, tc * 128:(tc + 1) * 128],
                    in_=TPB[:].rearrange("p (c t) -> p c t", c=8))
                _li.then_inc(*kb.inc("sv"))


def emit_l0_attn(ctx, AB, C_AR, D_AR):
    nc, block, kb, W = ctx["nc"], ctx["block"], ctx["kb"], ctx["W"]
    XRES, IDENT, IDENTF = ctx["XRES"], ctx["IDENT"], ctx["IDENTF"]
    ONESF, IOTA, B16W, E80W = ctx["ONESF"], ctx["IOTA"], ctx["B16W"], ctx["E80W"]
    COLS, MASKP = ctx["COLS"], ctx["MASKP"]
    M_COL, R_COL, S_COL = ctx["M_COL"], ctx["R_COL"], ctx["S_COL"]
    bq_col, bk_col = ctx["bq_col"], ctx["bk_col"]
    stage = ctx["stage"]

    aes = contextlib.ExitStack()
    QS = aes.enter_context(nc.sbuf_tensor(_un("QS"), [16, L], F32))
    QSTM = aes.enter_context(nc.sbuf_tensor(_un("QSTM"), [128, NT * 16], F32))
    M8 = aes.enter_context(nc.sbuf_tensor(_un("M8"), [16, 16], F32))
    VMK = aes.enter_context(nc.sbuf_tensor(_un("VMK"), [16, 10], F32))
    VROW = aes.enter_context(nc.sbuf_tensor(_un("VROW"), [1, 160], F32))
    VMC = aes.enter_context(nc.sbuf_tensor(_un("VMC"), [128, 2], F32))
    UCOL = aes.enter_context(nc.sbuf_tensor(_un("UCOL"), [16, 1], F32))
    OUTG = aes.enter_context(nc.sbuf_tensor(_un("OUTG"), [128, 8 * 128], BF16))
    with aes:
        s1 = contextlib.ExitStack()
        WQF = s1.enter_context(nc.sbuf_tensor(_un("WQF"), [128, NC * 1024], BF16))
        STG = s1.enter_context(nc.sbuf_tensor(_un("STGq"), [128, 1024], F32))
        QBF = s1.enter_context(nc.sbuf_tensor(_un("QBF"), [128, 512], F32))
        QSB = s1.enter_context(nc.sbuf_tensor(_un("QSB"), [128, 2048], BF16))
        CMB = s1.enter_context(nc.sbuf_tensor(_un("CMB"), [16, 2 * 512], F32))
        PQ = [s1.enter_context(nc.psum_tensor(_un(f"PQ{i}"), [128, 512], F32))
              for i in range(2)]
        PSTAT = s1.enter_context(nc.psum_tensor(_un("PSTAT"), [16, 2048], F32))
        with s1:
            kb.barrier()
            for c in range(NC):
                tgt = kb.stream(
                    W["wqT"].rearrange("(g p) d -> p g d", p=128)[:, :, c * 128:(c + 1) * 128],
                    STG[:].rearrange("p (g d) -> p g d", g=8), "stgq")
                @block.vector
                def _(vector, c=c, tgt=tgt):
                    vector.wait_ge(kb.sems["sdw"], tgt)
                    _li = vector.tensor_copy(out=WQF[:, c * 1024:(c + 1) * 1024],
                                             in_=STG[:])
                    _li.then_inc(*kb.inc("sv"))
                kb.release("stgq", "sv")
            wq_ready = kb.tag("sv")
            pq_free = [None, None]
            for tb in range(4):
                comb_done = kb.tag("sv")
                for c in range(NC):
                    pq = PQ[c % 2]
                    @block.tensor
                    def _(tensor, c=c, tb=tb, pq=pq, tf=pq_free[c % 2],
                          cd=comb_done):
                        if c == 0 and tb == 0:
                            tensor.wait_ge(kb.sems["sv"], wq_ready)
                        if tf is not None:
                            tensor.wait_ge(kb.sems["sv"], tf)
                        if c == 0 and tb > 0:
                            tensor.wait_ge(kb.sems["sv"], cd)
                        for g in range(8):
                            _li = tensor.matmul(
                                pq[:],
                                WQF[:, c * 1024 + g * 128:c * 1024 + (g + 1) * 128],
                                AB[:, g * L + tb * 512:g * L + (tb + 1) * 512],
                                start=(g == 0), stop=(g == 7))
                        _li.then_inc(*kb.inc("st"))
                    t_q = kb.tag("st")
                    @block.vector
                    def _(vector, c=c, pq=pq, t=t_q):
                        vector.wait_ge(kb.sems["st"], t)
                        _li = vector.tensor_scalar(out=QBF[:], in0=pq[:],
                                                   scalar1=bq_col(c), scalar2=None,
                                                   op0=ALU.add)
                        _li.then_inc(*kb.inc("sv"))
                    pq_free[c % 2] = kb.tag("sv")
                    t_qbf = kb.tag("sv")
                    @block.scalar
                    def _(scalar, t=t_qbf):
                        scalar.wait_ge(kb.sems["sv"], t)
                        kb.act(scalar, out=QSB[:, 0:512], in_=QBF[:], func=AF.Square)
                        _li = kb.act(scalar, out=QSB[:, 512:1024], in_=QBF[:],
                                     func=AF.Exp)
                        _li.then_inc(*kb.inc("ss"))
                    t_se = kb.tag("ss")
                    @block.vector
                    def _(vector, t=t_se):
                        vector.wait_ge(kb.sems["ss"], t)
                        vector.tensor_copy(out=QSB[:, 1024:1536], in_=QBF[:])
                        _li = vector.tensor_mul(out=QSB[:, 1536:2048],
                                                in0=QSB[:, 512:1024], in1=QBF[:])
                        _li.then_inc(*kb.inc("sv"))
                    t_el = kb.tag("sv")
                    @block.tensor
                    def _(tensor, c=c, t=t_el):
                        tensor.wait_ge(kb.sems["sv"], t)
                        for s in range(4):
                            _li = tensor.matmul(PSTAT[:, s * 512:(s + 1) * 512],
                                                B16W[:, 16 - 2 * c:32 - 2 * c],
                                                QSB[:, s * 512:(s + 1) * 512],
                                                start=(c == 0), stop=(c == NC - 1))
                        _li.then_inc(*kb.inc("st"))
                t_stat = kb.tag("st")
                S2 = PSTAT[:, 0:512]; Se = PSTAT[:, 512:1024]
                S1 = PSTAT[:, 1024:1536]; Sg = PSTAT[:, 1536:2048]
                T0 = CMB[:, 0:512]; T1 = CMB[:, 512:1024]; T2 = QBF[0:16, 0:512]
                qsl = QS[:, tb * 512:(tb + 1) * 512]
                @block.vector
                def _(vector, t=t_stat, qsl=qsl, S1=S1, S2=S2, Se=Se, Sg=Sg):
                    vector.wait_ge(kb.sems["st"], t)
                    vector.tensor_scalar(out=T0, in0=S1, scalar1=1.0 / DK,
                                         scalar2=None, op0=ALU.mult)
                    vector.tensor_mul(out=T0, in0=S1, in1=T0)
                    vector.tensor_sub(out=T0, in0=S2, in1=T0)
                    vector.tensor_scalar(out=qsl, in0=T0, scalar1=0.2 / (DK - 1),
                                         scalar2=None, op0=ALU.mult)
                    vector.reciprocal(out=T1, in_=Se)
                    _li = vector.tensor_mul(out=T1, in0=Sg, in1=T1)
                    _li.then_inc(*kb.inc("sv"))
                t_v1 = kb.tag("sv")
                @block.scalar
                def _(scalar, t=t_v1, S2=S2, Se=Se):
                    scalar.wait_ge(kb.sems["sv"], t)
                    kb.act(scalar, out=T0, in_=S2, func=AF.Sqrt)
                    _li = kb.act(scalar, out=T2, in_=Se, func=AF.Ln)
                    _li.then_inc(*kb.inc("ss"))
                t_s1 = kb.tag("ss")
                @block.vector
                def _(vector, t=t_s1, qsl=qsl):
                    vector.wait_ge(kb.sems["ss"], t)
                    vector.scalar_tensor_tensor(out=qsl, in0=T0, scalar=0.5,
                                                in1=qsl, op0=ALU.mult, op1=ALU.add)
                    vector.tensor_sub(out=T2, in0=T2, in1=T1)
                    vector.tensor_scalar(out=T2, in0=T2, scalar1=0.3,
                                         scalar2=-1.92e-8, op0=ALU.mult,
                                         op1=ALU.add)
                    _li = vector.tensor_add(out=qsl, in0=qsl, in1=T2)
                    _li.then_inc(*kb.inc("sv"))
        kb.dump("qs", QS[:, :])
        if stage < 3:
            return
        if stage >= 8:
            emit_wmean(ctx, 0, aes)

        s2 = contextlib.ExitStack()
        SCR = s2.enter_context(nc.sbuf_tensor(_un("SCR"), [16, 512], F32))
        QSC = s2.enter_context(nc.sbuf_tensor(_un("QSC"), [16, L], F32))
        SENTT = s2.enter_context(nc.sbuf_tensor(_un("SENTT"), [16, 10], F32))
        VMM = s2.enter_context(nc.sbuf_tensor(_un("VMM"), [16, 10], mybir.dt.uint8))
        FI32 = s2.enter_context(nc.sbuf_tensor(_un("FI32"), [16, 1], I32))
        PT = s2.enter_context(nc.psum_tensor(_un("PT"), [128, 16], F32))
        PU = s2.enter_context(nc.psum_tensor(_un("PU"), [16, 1], F32))
        with s2:
            kb.barrier()
            FC = lambda i: COLS[0:16, 40 + i:41 + i]
            @block.scalar
            def _(scalar):
                for q in range(4):
                    _li = kb.act(scalar, out=SCR[:], in_=QS[:, q * 512:(q + 1) * 512],
                                 func=AF.Square, accum_out=FC(4 + q))
                _li.then_inc(*kb.inc("ss"))
            t0 = kb.tag("ss")
            @block.vector
            def _(vector, t=t0):
                vector.wait_ge(kb.sems["ss"], t)
                vector.reduce_sum(out=FC(0), in_=QS[:, :], axis=AX.X)
                vector.tensor_add(out=FC(4), in0=FC(4), in1=FC(5))
                vector.tensor_add(out=FC(6), in0=FC(6), in1=FC(7))
                kb.vgap(vector)
                vector.tensor_add(out=FC(4), in0=FC(4), in1=FC(6))
                vector.tensor_scalar(out=FC(1), in0=FC(0), scalar1=1.0 / L,
                                     scalar2=None, op0=ALU.mult)
                kb.vgap(vector)
                vector.tensor_mul(out=FC(2), in0=FC(0), in1=FC(1))
                kb.vgap(vector)
                vector.tensor_sub(out=FC(2), in0=FC(4), in1=FC(2))
                kb.vgap(vector)
                _li = vector.tensor_scalar(out=FC(2), in0=FC(2),
                                           scalar1=1.0 / (L - 1), scalar2=None,
                                           op0=ALU.mult)
                _li.then_inc(*kb.inc("sv"))
            t1 = kb.tag("sv")
            @block.scalar
            def _(scalar, t=t1):
                scalar.wait_ge(kb.sems["sv"], t)
                _li = kb.act(scalar, out=FC(3), in_=FC(2), func=AF.Sqrt)
                _li.then_inc(*kb.inc("ss"))
            t2 = kb.tag("ss")
            @block.vector
            def _(vector, t=t2):
                vector.wait_ge(kb.sems["ss"], t)
                vector.tensor_scalar(out=FC(1), in0=FC(1), scalar1=1e-6,
                                     scalar2=None, op0=ALU.add)
                kb.vgap(vector)
                vector.reciprocal(out=FC(1), in_=FC(1))
                kb.vgap(vector)
                vector.tensor_mul(out=FC(0), in0=FC(3), in1=FC(1))
                kb.vgap(vector)
                _li = vector.tensor_scalar(out=FC(0), in0=FC(0), scalar1=10.0,
                                           scalar2=None, op0=ALU.mult)
                _li.then_inc(*kb.inc("sv"))
            t3 = kb.tag("sv")
            @block.scalar
            def _(scalar, t=t3):
                scalar.wait_ge(kb.sems["sv"], t)
                scalar.copy(out=FI32[:], in_=FC(0))
                kb.sgap(scalar)
                _li = scalar.copy(out=FC(0), in_=FI32[:])
                _li.then_inc(*kb.inc("ss"))
            t4 = kb.tag("ss")
            @block.vector
            def _(vector, t=t4):
                vector.wait_ge(kb.sems["ss"], t)
                _li = vector.tensor_scalar(out=FC(0), in0=FC(0), scalar1=3.0,
                                           scalar2=10.0, op0=ALU.max, op1=ALU.min)
                _li.then_inc(*kb.inc("sv"))
            t5 = kb.tag("sv")
            @block.tensor
            def _(tensor, t=t5):
                tensor.wait_ge(kb.sems["sv"], t)
                _li = tensor.matmul(PU[:], ONESF[0:1, 0:16], FC(0)[0:1, 0:1],
                                    start=True, stop=True)
                _li.then_inc(*kb.inc("st"))
            tu = kb.tag("st")
            @block.vector
            def _(vector, t=tu):
                vector.wait_ge(kb.sems["st"], t)
                _li = vector.tensor_copy(out=UCOL[:], in_=PU[:])
                _li.then_inc(*kb.inc("sv"))
            for tc in range(NT):
                @block.tensor
                def _(tensor, tc=tc):
                    kb.wait(tensor, "sv")
                    _li = tensor.transpose(PT[:], QS[:, tc * 128:(tc + 1) * 128],
                                           IDENTF[0:16, 0:16])
                    _li.then_inc(*kb.inc("st"))
                tt = kb.tag("st")
                @block.vector
                def _(vector, tc=tc, t=tt):
                    vector.wait_ge(kb.sems["st"], t)
                    _li = vector.tensor_copy(out=QSTM[:, tc * 16:(tc + 1) * 16],
                                             in_=PT[:])
                    _li.then_inc(*kb.inc("sv"))
            qstm_done = kb.tag("sv")
            @block.vector
            def _(vector, t=qstm_done):
                vector.wait_ge(kb.sems["sv"], t)
                vector.tensor_copy(out=QSC[:], in_=QS[:, :])
                vector.max(out=M8[:, 0:8], in_=QSC[:])
                kb.vgap(vector)
                vector.match_replace(out=QSC[:], in_to_replace=M8[:, 0:8],
                                     in_values=QSC[:], imm_value=NEG)
                vector.max(out=M8[:, 8:16], in_=QSC[:])
                vector.memset(SENTT[:], SENT)
                vector.tensor_single_scalar(out=VMM[:], in_=IOTA[0:16, 100:110],
                                            scalar=UCOL[:], op=ALU.is_lt)
                kb.vgap(vector)
                _li = vector.select(out=VMK[:], mask=VMM[:], on_true=M8[:, 0:10],
                                    on_false=SENTT[:])
                _li.then_inc(*kb.inc("sv"))
            t_v = kb.tag("sv")
            @block.scalar
            def _(scalar, t=t_v):
                scalar.wait_ge(kb.sems["sv"], t)
                scalar.dma_start(
                    out=VROW[0:1, :].rearrange("o (h j) -> o h j", h=16),
                    in_=M8[:, 0:10]).then_inc(*kb.inc("sdx", 16))
                scalar.dma_start(out=VMC[0:80, 0:1], in_=VMK[0:8, :]
                                 ).then_inc(*kb.inc("sdx", 16))
                scalar.dma_start(out=VMC[0:80, 1:2], in_=VMK[8:16, :]
                                 ).then_inc(*kb.inc("sdx", 16))
                scalar.wait_ge(kb.sems["sdx"], kb.tag("sdx"))
            t_vrow = kb.tag("sdx")
            ctx["t_vrow"] = t_vrow
            kb.dump("vmk", VMK[:, :])
            kb.dump("ucol", UCOL[:, :])
        if stage < 4:
            return

        s46 = contextlib.ExitStack()
        Q8 = s46.enter_context(nc.sbuf_tensor(_un("Q8"), [128, 8 * 80], BF16))
        with s46:
            s4 = contextlib.ExitStack()
            XNG = s4.enter_context(nc.sbuf_tensor(_un("XNG"), [128, NC * 160], BF16))
            SELC = s4.enter_context(nc.sbuf_tensor(_un("SELC"), [128, 160], F32))
            MRROW = s4.enter_context(nc.sbuf_tensor(_un("MRROW"), [2, 160], F32))
            MR2 = s4.enter_context(nc.sbuf_tensor(_un("MR2"), [1, 160], F32))
            SCR1 = s4.enter_context(nc.sbuf_tensor(_un("SCR1"), [128, 160], F32))
            MBB = s4.enter_context(nc.sbuf_tensor(_un("MBB"), [128, 160], F32))
            RBB = s4.enter_context(nc.sbuf_tensor(_un("RBB"), [128, 160], F32))
            STG4 = s4.enter_context(nc.sbuf_tensor(_un("STG4"), [128, 1024], F32))
            WB4 = [s4.enter_context(nc.sbuf_tensor(_un(f"WB4{i}"), [128, 1024], BF16))
                   for i in range(2)]
            with s4:
                g_es = contextlib.ExitStack()
                PVB = g_es.enter_context(nc.psum_tensor(_un("PVB"), [128, 160], F32))
                PM = g_es.enter_context(nc.psum_tensor(_un("PM"), [2, 160], F32))
                PXG = [g_es.enter_context(nc.psum_tensor(_un(f"PXG{i}"), [128, 160], F32))
                       for i in range(4)]
                PMB = g_es.enter_context(nc.psum_tensor(_un("PMB"), [128, 160], F32))
                PRB = g_es.enter_context(nc.psum_tensor(_un("PRB"), [128, 160], F32))
                g_es.__enter__()
                kb.barrier()
                @block.vector
                def _(vector):
                    _li = vector.memset(Q8[:], 0.0)
                    _li.then_inc(*kb.inc("sv"))
                @block.tensor
                def _(tensor, tvr=ctx.get("t_vrow", 0)):
                    kb.wait(tensor, "sv")
                    tensor.wait_ge(kb.sems["sdx"], tvr)
                    _li = tensor.matmul(PVB[:], ONESF[0:1, 0:128], VROW[0:1, :],
                                        start=True, stop=True)
                    _li.then_inc(*kb.inc("st"))
                t_vb = kb.tag("st")
                for half in range(2):
                    for tc in range(NT):
                        @block.vector
                        def _(vector, tc=tc, half=half, t=t_vb):
                            vector.wait_ge(kb.sems["st"], t)
                            if not (half == 0 and tc == 0):
                                kb.wait(vector, "st")
                            _li = vector.tensor_tensor(
                                out=SELC[:].rearrange("p (h j) -> p h j", h=16),
                                in0=QSTM[:, tc * 16:(tc + 1) * 16].to_broadcast(
                                    [128, 16, 10]),
                                in1=PVB[:].rearrange("p (h j) -> p h j", h=16),
                                op=ALU.is_equal)
                            _li.then_inc(*kb.inc("sv"))
                        t_sel = kb.tag("sv")
                        @block.tensor
                        def _(tensor, tc=tc, half=half, t=t_sel):
                            tensor.wait_ge(kb.sems["sv"], t)
                            st_ = (tc == 0)
                            sp_ = (tc == NT - 1)
                            for ci in range(4):
                                c = half * 4 + ci
                                _li = tensor.matmul(
                                    PXG[ci][:],
                                    XRES[:, tc * 1024 + c * 128:tc * 1024 + (c + 1) * 128],
                                    SELC[:], start=st_, stop=sp_)
                            if half == 0:
                                _li = tensor.matmul(
                                    PM[:], COLS[:, tc:tc + 17:16], SELC[:],
                                    start=st_, stop=sp_)
                            _li.then_inc(*kb.inc("st"))
                    t_g = kb.tag("st")
                    if half == 0:
                        @block.vector
                        def _(vector, t=t_g):
                            vector.wait_ge(kb.sems["st"], t)
                            _li = vector.tensor_copy(out=MRROW[:], in_=PM[:])
                            _li.then_inc(*kb.inc("sv"))
                        t_mrr = kb.tag("sv")
                        @block.scalar
                        def _(scalar, t=t_mrr):
                            scalar.wait_ge(kb.sems["sv"], t)
                            scalar.dma_start(out=MR2[:], in_=MRROW[1:2, :]
                                             ).then_inc(*kb.inc("sdx", 16))
                        t_mr2 = kb.tag("sdx")
                        @block.tensor
                        def _(tensor, t=t_mrr, t2=t_mr2):
                            tensor.wait_ge(kb.sems["sv"], t)
                            tensor.wait_ge(kb.sems["sdx"], t2)
                            tensor.matmul(PMB[:], ONESF[0:1, 0:128], MRROW[0:1, :],
                                          start=True, stop=True)
                            _li = tensor.matmul(PRB[:], ONESF[0:1, 0:128],
                                                MR2[0:1, :], start=True, stop=True)
                            _li.then_inc(*kb.inc("st"))
                    t_bc = kb.tag("st")
                    if half == 0:
                        @block.vector
                        def _(vector, t=t_bc):
                            vector.wait_ge(kb.sems["st"], t)
                            vector.tensor_copy(out=MBB[:], in_=PMB[:])
                            _li = vector.tensor_copy(out=RBB[:], in_=PRB[:])
                            _li.then_inc(*kb.inc("sv"))
                    t_bc2 = kb.tag("sv")
                    for ci in range(4):
                        c = half * 4 + ci
                        @block.vector
                        def _(vector, c=c, ci=ci, t=t_bc2, tg=t_g):
                            vector.wait_ge(kb.sems["sv"], t)
                            vector.wait_ge(kb.sems["st"], tg)
                            vector.tensor_sub(out=SCR1[:], in0=PXG[ci][:], in1=MBB[:])
                            _li = vector.tensor_mul(out=XNG[:, c * 160:(c + 1) * 160],
                                                    in0=SCR1[:], in1=RBB[:])
                            _li.then_inc(*kb.inc("sv"))
                kb.dump("xng", XNG[:, :])
                g_es.close()
                pqs_es = contextlib.ExitStack()
                PQS = pqs_es.enter_context(nc.psum_tensor(_un("PQS"), [128, 20], F32))
                with pqs_es:
                    for cs in range(NC):
                        kb.drain_bg(2)
                        tgt = kb.stream(
                            W["wqT"].rearrange("(g p) d -> p g d", p=128)[
                                :, :, cs * 128:(cs + 1) * 128],
                            STG4[:].rearrange("p (g d) -> p g d", g=8), "stg4")
                        @block.vector
                        def _(vector, cs=cs, tgt=tgt):
                            vector.wait_ge(kb.sems["sdw"], tgt)
                            _li = vector.tensor_copy(out=WB4[cs % 2][:], in_=STG4[:])
                            _li.then_inc(*kb.inc("sv"))
                        kb.release("stg4", "sv")
                        t_w = kb.tag("sv")
                        @block.tensor
                        def _(tensor, cs=cs, t=t_w):
                            tensor.wait_ge(kb.sems["sv"], t)
                            if cs > 0:
                                tensor.wait_ge(kb.sems["sv"], kb.tag("sv"))
                            for g in range(8):
                                _li = tensor.matmul(
                                    PQS[:],
                                    WB4[cs % 2][:, g * 128:(g + 1) * 128],
                                    XNG[:, g * 160 + 20 * cs:g * 160 + 20 * cs + 20],
                                    start=(g == 0), stop=(g == 7))
                            _li.then_inc(*kb.inc("st"))
                        t_qs = kb.tag("st")
                        for hh in range(2):
                            h = 2 * cs + hh
                            s = h // 8; cc = (h % 8) // 2
                            blk = (s * 4 + cc) * 80 + (h % 8) * 10
                            pb = (h % 2) * 64
                            @block.vector
                            def _(vector, h=h, blk=blk, pb=pb, cs=cs, hh=hh, t=t_qs):
                                vector.wait_ge(kb.sems["st"], t)
                                _li = vector.tensor_scalar(
                                    out=Q8[pb:pb + 64, blk:blk + 10],
                                    in0=PQS[pb:pb + 64, hh * 10:(hh + 1) * 10],
                                    scalar1=bq_col(cs)[pb:pb + 64, :],
                                    scalar2=0.125, op0=ALU.add, op1=ALU.mult)
                                _li.then_inc(*kb.inc("sv"))
            kb.dump("q8", Q8[:, :])
            if stage < 5:
                return

            s5 = contextlib.ExitStack()
            STG5 = s5.enter_context(nc.sbuf_tensor(_un("STG5"), [128, 1024], F32))
            KWB = [s5.enter_context(nc.sbuf_tensor(_un(f"KWB{i}"), [128, 1024], BF16))
                   for i in range(2)]
            WVF = s5.enter_context(nc.sbuf_tensor(_un("WVF"), [128, NC * 512], BF16))
            PKV = [s5.enter_context(nc.psum_tensor(_un(f"PKV{i}"), [128, 512], F32))
                   for i in range(3)]
            with s5:
                kb.barrier()
                kv_free = [None, None, None]
                ki = 0
                for c in range(NC):
                    kb.drain_bg(2)
                    tgt = kb.stream(
                        W["wkT"].rearrange("(g p) d -> p g d", p=128)[
                            :, :, c * 128:(c + 1) * 128],
                        STG5[:].rearrange("p (g d) -> p g d", g=8), "stg5")
                    @block.vector
                    def _(vector, c=c, tgt=tgt):
                        vector.wait_ge(kb.sems["sdw"], tgt)
                        _li = vector.tensor_copy(out=KWB[c % 2][:], in_=STG5[:])
                        _li.then_inc(*kb.inc("sv"))
                    kb.release("stg5", "sv")
                    t_w = kb.tag("sv")
                    for tb in range(4):
                        pk = PKV[ki % 3]
                        @block.tensor
                        def _(tensor, c=c, tb=tb, pk=pk, t=t_w, kf=kv_free[ki % 3]):
                            tensor.wait_ge(kb.sems["sv"], t)
                            if kf is not None:
                                tensor.wait_ge(kb.sems["sv"], kf)
                            for g in range(8):
                                _li = tensor.matmul(
                                    pk[:],
                                    KWB[c % 2][:, g * 128:(g + 1) * 128],
                                    AB[:, g * L + tb * 512:g * L + (tb + 1) * 512],
                                    start=(g == 0), stop=(g == 7))
                            _li.then_inc(*kb.inc("st"))
                        t_k = kb.tag("st")
                        @block.vector
                        def _(vector, c=c, tb=tb, pk=pk, t=t_k):
                            vector.wait_ge(kb.sems["st"], t)
                            _li = vector.tensor_scalar(
                                out=C_AR[:, c * L + tb * 512:c * L + (tb + 1) * 512],
                                in0=pk[:], scalar1=bk_col(c), scalar2=None,
                                op0=ALU.add)
                            _li.then_inc(*kb.inc("sv"))
                        kv_free[ki % 3] = kb.tag("sv")
                        ki += 1
                for vhalf in range(2):
                    for g in range(NC):
                        kb.drain_bg(1)
                        tgt = kb.stream(
                            W["wvT"][g * 128:(g + 1) * 128,
                                     vhalf * 512:(vhalf + 1) * 512],
                            STG5[:, 0:512], "stg5")
                        @block.vector
                        def _(vector, g=g, tgt=tgt):
                            vector.wait_ge(kb.sems["sdw"], tgt)
                            _li = vector.tensor_copy(out=WVF[:, g * 512:(g + 1) * 512],
                                                     in_=STG5[:, 0:512])
                            _li.then_inc(*kb.inc("sv"))
                        kb.release("stg5", "sv")
                    t_wv = kb.tag("sv")
                    for tc in range(NT):
                        pk = PKV[ki % 3]
                        @block.tensor
                        def _(tensor, tc=tc, pk=pk, t=t_wv, kf=kv_free[ki % 3]):
                            tensor.wait_ge(kb.sems["sv"], t)
                            if kf is not None:
                                tensor.wait_ge(kb.sems["sv"], kf)
                            for g in range(8):
                                _li = tensor.matmul(
                                    pk[:],
                                    AB[:, g * L + tc * 128:g * L + (tc + 1) * 128],
                                    WVF[:, g * 512:(g + 1) * 512],
                                    start=(g == 0), stop=(g == 7))
                            _li.then_inc(*kb.inc("st"))
                        t_v = kb.tag("st")
                        @block.vector
                        def _(vector, tc=tc, vhalf=vhalf, pk=pk, t=t_v):
                            vector.wait_ge(kb.sems["st"], t)
                            _li = vector.tensor_copy(
                                out=D_AR[:, tc * 1024 + vhalf * 512:tc * 1024 + (vhalf + 1) * 512],
                                in_=pk[:])
                            _li.then_inc(*kb.inc("sv"))
                        kv_free[ki % 3] = kb.tag("sv")
                        ki += 1
            kb.dump("kfm", C_AR[:, :])
            kb.dump("vtm", D_AR[:, :])

            s6 = contextlib.ExitStack()
            ATTB = s6.enter_context(nc.sbuf_tensor(_un("ATTB"), [128, 2048], BF16))
            ATTT = s6.enter_context(nc.sbuf_tensor(_un("ATTT"), [128, NT * 80], BF16))
            BVBC = s6.enter_context(nc.sbuf_tensor(_un("BVBC"), [128, 1024], F32))
            OUTALL = s6.enter_context(nc.sbuf_tensor(_un("OUTALL"), [128, 512], F32))
            BROW = s6.enter_context(nc.sbuf_tensor(_un("BROW"), [1, 1024], F32))
            SMX = s6.enter_context(nc.sbuf_tensor(_un("SMX"), [128, 8], F32))
            with s6:
                kb.barrier()
                bv_es = contextlib.ExitStack()
                PB1 = bv_es.enter_context(nc.psum_tensor(_un("PB1"), [128, 1024], F32))
                with bv_es:
                    @block.scalar
                    def _(scalar):
                        scalar.dma_start(out=BROW[:], in_=W["bv"][None, :]
                                         ).then_inc(*kb.inc("sdx", 16))
                    t_br = kb.tag("sdx")
                    @block.tensor
                    def _(tensor, t=t_br):
                        tensor.wait_ge(kb.sems["sdx"], t)
                        for off in (0, 512):
                            _li = tensor.matmul(PB1[:, off:off + 512],
                                                ONESF[0:1, 0:128],
                                                BROW[0:1, off:off + 512],
                                                start=True, stop=True)
                        _li.then_inc(*kb.inc("st"))
                    t_bv = kb.tag("st")
                    @block.vector
                    def _(vector, t=t_bv):
                        vector.wait_ge(kb.sems["st"], t)
                        vector.tensor_copy(out=BVBC[:], in_=PB1[:])
                        _li = vector.memset(OUTG[:], 0.0)
                        _li.then_inc(*kb.inc("sv"))
                for s in range(2):
                    h6 = contextlib.ExitStack()
                    PSC = h6.enter_context(nc.psum_tensor(_un("PSC"), [80, 2048], F32))
                    PE = h6.enter_context(nc.psum_tensor(_un("PE"), [80, 2048], F32))
                    with h6:
                        @block.tensor
                        def _(tensor, s=s):
                            kb.wait(tensor, "sv")
                            for tp in range(4):
                                for cc in range(4):
                                    _li = tensor.matmul(
                                        PSC[:, tp * 512:(tp + 1) * 512],
                                        Q8[:, (s * 4 + cc) * 80:(s * 4 + cc + 1) * 80],
                                        C_AR[:, (4 * s + cc) * L + tp * 512:(4 * s + cc) * L + (tp + 1) * 512],
                                        start=(cc == 0), stop=(cc == 3))
                            _li.then_inc(*kb.inc("st"))
                        t_sc = kb.tag("st")
                        @block.vector
                        def _(vector, t=t_sc):
                            vector.wait_ge(kb.sems["st"], t)
                            _li = vector.reduce_max(out=SMX[0:80, 0:1], in_=PSC[:],
                                                    axis=AX.X, negate=True)
                            _li.then_inc(*kb.inc("sv"))
                        t_mx = kb.tag("sv")
                        @block.scalar
                        def _(scalar, t=t_mx):
                            scalar.wait_ge(kb.sems["sv"], t)
                            for q in range(4):
                                _li = kb.act(scalar, out=PE[:, q * 512:(q + 1) * 512],
                                             in_=PSC[:, q * 512:(q + 1) * 512],
                                             func=AF.Exp, bias=SMX[0:80, 0:1],
                                             accum_out=SMX[0:80, 1 + q:2 + q])
                            _li.then_inc(*kb.inc("ss"))
                        t_e = kb.tag("ss")
                        @block.vector
                        def _(vector, t=t_e):
                            vector.wait_ge(kb.sems["ss"], t)
                            vector.tensor_add(out=SMX[0:80, 1:2], in0=SMX[0:80, 1:2],
                                              in1=SMX[0:80, 2:3])
                            vector.tensor_add(out=SMX[0:80, 3:4], in0=SMX[0:80, 3:4],
                                              in1=SMX[0:80, 4:5])
                            kb.vgap(vector)
                            vector.tensor_add(out=SMX[0:80, 1:2], in0=SMX[0:80, 1:2],
                                              in1=SMX[0:80, 3:4])
                            kb.vgap(vector)
                            vector.reciprocal(out=SMX[0:80, 5:6], in_=SMX[0:80, 1:2])
                            kb.vgap(vector)
                            _li = vector.tensor_scalar(out=ATTB[0:80, :], in0=PE[:],
                                                       scalar1=SMX[0:80, 5:6],
                                                       scalar2=None, op0=ALU.mult)
                            _li.then_inc(*kb.inc("sv"))
                        t_at = kb.tag("sv")
                    h7 = contextlib.ExitStack()
                    PTT = h7.enter_context(nc.psum_tensor(_un("PTT"), [128, 80], BF16))
                    PO = h7.enter_context(nc.psum_tensor(_un("PO"), [80, 512], F32))
                    with h7:
                        for tc in range(NT):
                            @block.tensor
                            def _(tensor, tc=tc, t=t_at):
                                tensor.wait_ge(kb.sems["sv"], t)
                                if tc > 0:
                                    kb.wait(tensor, "sv")
                                _li = tensor.transpose(
                                    PTT[:, 0:80],
                                    ATTB[0:80, tc * 128:(tc + 1) * 128],
                                    IDENT[0:80, 0:80])
                                _li.then_inc(*kb.inc("st"))
                            t_tt = kb.tag("st")
                            @block.vector
                            def _(vector, tc=tc, t=t_tt):
                                vector.wait_ge(kb.sems["st"], t)
                                _li = vector.tensor_copy(
                                    out=ATTT[:, tc * 80:(tc + 1) * 80],
                                    in_=PTT[:, 0:80])
                                _li.then_inc(*kb.inc("sv"))
                        t_attt = kb.tag("sv")
                        @block.tensor
                        def _(tensor, s=s, t=t_attt):
                            tensor.wait_ge(kb.sems["sv"], t)
                            for tc in range(NT):
                                _li = tensor.matmul(
                                    PO[:],
                                    ATTT[:, tc * 80:(tc + 1) * 80],
                                    D_AR[:, tc * 1024 + s * 512:tc * 1024 + (s + 1) * 512],
                                    start=(tc == 0), stop=(tc == NT - 1))
                            _li.then_inc(*kb.inc("st"))
                        t_po = kb.tag("st")
                        if s == 0:
                            kb.dump("attb", ATTB[0:80, :])
                        if s == 0:
                            pass
                        @block.vector
                        def _(vector, s=s, t=t_po):
                            vector.wait_ge(kb.sems["st"], t)
                            vector.scalar_tensor_tensor(
                                out=OUTALL[0:80, :], in0=PO[:], scalar=1.0,
                                in1=BVBC[0:80, s * 512:(s + 1) * 512],
                                op0=ALU.mult, op1=ALU.add)
                            for pp in range(4):
                                pg = s * 4 + pp
                                for hh in range(2):
                                    hl = 2 * pp + hh
                                    _li = vector.tensor_scalar(
                                        out=OUTG[0:80, pg * 128 + hh * 64:pg * 128 + (hh + 1) * 64],
                                        in0=OUTALL[0:80, hl * 64:(hl + 1) * 64],
                                        scalar1=MASKP[0:80, hl:hl + 1], scalar2=None,
                                        op0=ALU.mult)
                            _li.then_inc(*kb.inc("sv"))
                        if s == 0:
                            kb.dump("outall", OUTALL[0:80, :])
        kb.dump("outg", OUTG[:, :])
        s7 = contextlib.ExitStack()
        SELFS = s7.enter_context(nc.sbuf_tensor(_un("SELFS"), [80, 2048], BF16))
        OUTG2 = None
        with s7:
            kb.barrier()
            for s in range(2):
                pe_es = contextlib.ExitStack()
                PEA = pe_es.enter_context(nc.psum_tensor(_un("PEA"), [80, 2048], F32))
                with pe_es:
                    @block.tensor
                    def _(tensor, s=s):
                        kb.wait(tensor, "sv")
                        for off in range(0, 2048, 512):
                            _li = tensor.matmul(PEA[:, off:off + 512],
                                                E80W[:, s * 80:(s + 1) * 80],
                                                QS[:, off:off + 512],
                                                start=True, stop=True)
                        _li.then_inc(*kb.inc("st"))
                    t_pe = kb.tag("st")
                    @block.vector
                    def _(vector, s=s, t=t_pe, tvr=ctx.get("t_vrow", 0)):
                        vector.wait_ge(kb.sems["st"], t)
                        vector.wait_ge(kb.sems["sdx"], tvr)
                        _li = vector.tensor_single_scalar(
                            out=SELFS[:], in_=PEA[:],
                            scalar=VMC[0:80, s:s + 1], op=ALU.is_equal)
                        _li.then_inc(*kb.inc("sv"))
                if s == 0:
                    kb.dump("selfs", SELFS[:, :])
                sc_es = contextlib.ExitStack()
                PF = sc_es.enter_context(nc.psum_tensor(_un("PF"), [128, 2048], F32))
                with sc_es:
                    for pp in range(4):
                        pg = s * 4 + pp
                        @block.tensor
                        def _(tensor, pg=pg):
                            kb.wait(tensor, "sv")
                            if pg % 4 > 0:
                                tensor.wait_ge(kb.sems["sv"], kb.tag("sv"))
                            for off in range(0, 2048, 512):
                                _li = tensor.matmul(PF[:, off:off + 512],
                                                    OUTG[0:80, pg * 128:(pg + 1) * 128],
                                                    SELFS[:, off:off + 512],
                                                    start=True, stop=True)
                            _li.then_inc(*kb.inc("st"))
                        t_f = kb.tag("st")
                        @block.vector
                        def _(vector, pg=pg, t=t_f):
                            vector.wait_ge(kb.sems["st"], t)
                            _li = vector.tensor_copy(out=C_AR[:, pg * L:(pg + 1) * L],
                                                     in_=PF[:])
                            _li.then_inc(*kb.inc("sv"))

        kb.dump("full", C_AR[:, :])
        s8 = contextlib.ExitStack()
        STG8 = s8.enter_context(nc.sbuf_tensor(_un("STG8"), [128, 512], F32))
        WOF = s8.enter_context(nc.sbuf_tensor(_un("WOF"), [128, NC * 1024], BF16))
        BOBC = s8.enter_context(nc.sbuf_tensor(_un("BOBC"), [128, 1024], F32))
        PB2 = s8.enter_context(nc.psum_tensor(_un("PB2"), [128, 1024], F32))
        PWO = [s8.enter_context(nc.psum_tensor(_un(f"PWO{i}"), [128, 1024], F32))
               for i in range(2)]
        with s8:
            kb.barrier()
            @block.scalar
            def _(scalar):
                scalar.dma_start(out=BOBC[0:1, :], in_=W["bo"][None, :]
                                 ).then_inc(*kb.inc("sdx", 16))
            t_br = kb.tag("sdx")
            for g in range(NC):
                kb.drain_bg(1)
                for wh in range(2):
                    tgt = kb.stream(W["woT"][g * 128:(g + 1) * 128,
                                             wh * 512:(wh + 1) * 512],
                                    STG8[:], "stg8")
                    @block.vector
                    def _(vector, g=g, wh=wh, tgt=tgt):
                        vector.wait_ge(kb.sems["sdw"], tgt)
                        _li = vector.tensor_copy(
                            out=WOF[:, g * 1024 + wh * 512:g * 1024 + (wh + 1) * 512],
                            in_=STG8[:])
                        _li.then_inc(*kb.inc("sv"))
                    kb.release("stg8", "sv")
            t_wo = kb.tag("sv")
            @block.tensor
            def _(tensor, t=t_br):
                tensor.wait_ge(kb.sems["sdx"], t)
                for off in (0, 512):
                    _li = tensor.matmul(PB2[:, off:off + 512], ONESF[0:1, 0:128],
                                        BOBC[0:1, off:off + 512],
                                        start=True, stop=True)
                _li.then_inc(*kb.inc("st"))
            t_bo = kb.tag("st")
            @block.vector
            def _(vector, t=t_bo):
                vector.wait_ge(kb.sems["st"], t)
                _li = vector.tensor_copy(out=BOBC[:], in_=PB2[:])
                _li.then_inc(*kb.inc("sv"))
            wo_free = [None, None]
            for tc in range(NT):
                pw = PWO[tc % 2]
                @block.tensor
                def _(tensor, tc=tc, pw=pw, t=t_wo, wf=wo_free[tc % 2]):
                    tensor.wait_ge(kb.sems["sv"], t)
                    if wf is not None:
                        tensor.wait_ge(kb.sems["sv"], wf)
                    for g in range(8):
                        for off in (0, 512):
                            _li = tensor.matmul(
                                pw[:, off:off + 512],
                                C_AR[:, g * L + tc * 128:g * L + (tc + 1) * 128],
                                WOF[:, g * 1024 + off:g * 1024 + off + 512],
                                start=(g == 0), stop=(g == 7))
                    _li.then_inc(*kb.inc("st"))
                t_w = kb.tag("st")
                @block.vector
                def _(vector, tc=tc, pw=pw, t=t_w):
                    vector.wait_ge(kb.sems["st"], t)
                    _li = vector.scalar_tensor_tensor(
                        out=D_AR[:, tc * 1024:(tc + 1) * 1024], in0=pw[:],
                        scalar=1.0, in1=BOBC[:], op0=ALU.mult, op1=ALU.add)
                    _li.then_inc(*kb.inc("sv"))
                wo_free[tc % 2] = kb.tag("sv")


def emit_l1_attn(ctx, AB, C_AR, D_AR):
    nc, block, kb, W = ctx["nc"], ctx["block"], ctx["kb"], ctx["W"]
    XRES, IDENT, IDENTF = ctx["XRES"], ctx["IDENT"], ctx["IDENTF"]
    ONESF = ctx["ONESF"]
    ALOG_COL = ctx["ALOG_COL"]
    S_COL = ctx["S_COL"]
    stage = ctx["stage"]

    aes = contextlib.ExitStack()
    XZ3 = aes.enter_context(nc.sbuf_tensor(_un("XZ3"), [128, NT * 48], F32))
    with aes:
        s1 = contextlib.ExitStack()
        STG = s1.enter_context(nc.sbuf_tensor(_un("STGi"), [128, 1072], F32))
        WIF = s1.enter_context(nc.sbuf_tensor(_un("WIF"), [128, NC * PD], BF16))
        BINB = s1.enter_context(nc.sbuf_tensor(_un("BINB"), [128, PD], F32))
        BIROW = s1.enter_context(nc.sbuf_tensor(_un("BIROW"), [1, PD], F32))
        ZW = [s1.enter_context(nc.sbuf_tensor(_un(f"ZW{i}"), [128, 512], F32))
              for i in range(2)]
        PXZ = [s1.enter_context(nc.psum_tensor(_un(f"PXZ{i}"), [128, 512], F32))
               for i in range(2)]
        PX3 = s1.enter_context(nc.psum_tensor(_un("PX3"), [128, 48], F32))
        PBI = s1.enter_context(nc.psum_tensor(_un("PBI"), [128, 1072], F32))

        with s1:
            kb.barrier()
            for g in range(NC):
                tgt = kb.stream(W["winS"][g * 128:(g + 1) * 128, :], STG[:], "stgi")
                @block.vector
                def _(vector, g=g, tgt=tgt):
                    vector.wait_ge(kb.sems["sdw"], tgt)
                    _li = vector.tensor_copy(out=WIF[:, g * PD:(g + 1) * PD], in_=STG[:])
                    _li.then_inc(*kb.inc("sv"))
                kb.release("stgi", "sv")
            t_wi = kb.tag("sv")
            @block.scalar
            def _(scalar):
                scalar.dma_start(out=BINB[0:1, :], in_=W["binS"][None, :]
                                 ).then_inc(*kb.inc("sdx", 16))
            t_bir = kb.tag("sdx")
            @block.tensor
            def _(tensor, t=t_bir):
                tensor.wait_ge(kb.sems["sdx"], t)
                for off, w in [(0, 512), (512, 512), (1024, 48)]:
                    _li = tensor.matmul(PBI[:, off:off + w], ONESF[0:1, 0:128],
                                  BINB[0:1, off:off + w], start=True, stop=True)
                _li.then_inc(*kb.inc("st"))
            t_bi = kb.tag("st")
            @block.vector
            def _(vector, t=t_bi):
                vector.wait_ge(kb.sems["st"], t)
                _li = vector.tensor_copy(out=BINB[:], in_=PBI[:])
                _li.then_inc(*kb.inc("sv"))
            xz_free = [None, None]
            x3_free = None
            for tc in range(NT):
                for half in range(2):
                    px = PXZ[half]
                    @block.tensor
                    def _(tensor, tc=tc, half=half, px=px, t=t_wi,
                          xf=xz_free[half]):
                        tensor.wait_ge(kb.sems["sv"], t)
                        if xf is not None:
                            tensor.wait_ge(kb.sems["ss"], xf)
                        for g in range(8):
                            _li = tensor.matmul(
                                px[:],
                                AB[:, g * L + tc * 128:g * L + (tc + 1) * 128],
                                WIF[:, g * PD + half * 512:g * PD + (half + 1) * 512],
                                start=(g == 0), stop=(g == 7))
                        _li.then_inc(*kb.inc("st"))
                    t_x = kb.tag("st")
                    zw_guard = kb.tag("ss")
                    @block.vector
                    def _(vector, half=half, px=px, t=t_x, g=zw_guard):
                        vector.wait_ge(kb.sems["st"], t)
                        vector.wait_ge(kb.sems["ss"], g)
                        _li = vector.tensor_add(out=ZW[half][:], in0=px[:],
                                          in1=BINB[:, half * 512:(half + 1) * 512])
                        _li.then_inc(*kb.inc("sv"))
                    t_z = kb.tag("sv")
                    @block.scalar
                    def _(scalar, tc=tc, half=half, t=t_z):
                        scalar.wait_ge(kb.sems["sv"], t)
                        _li = kb.act(scalar,
                               out=C_AR[:, tc * 1024 + half * 512:tc * 1024 + (half + 1) * 512],
                               in_=ZW[half][:], func=AF.Gelu)
                        _li.then_inc(*kb.inc("ss"))
                    xz_free[half] = kb.tag("ss")
                @block.tensor
                def _(tensor, tc=tc, t=t_wi):
                    tensor.wait_ge(kb.sems["sv"], t)
                    if tc > 0:
                        tensor.wait_ge(kb.sems["sv"], kb.tag("sv"))
                    for g in range(8):
                        _li = tensor.matmul(PX3[:],
                                      AB[:, g * L + tc * 128:g * L + (tc + 1) * 128],
                                      WIF[:, g * PD + 1024:g * PD + 1072],
                                      start=(g == 0), stop=(g == 7))
                    _li.then_inc(*kb.inc("st"))
                t_3 = kb.tag("st")
                @block.vector
                def _(vector, tc=tc, t=t_3):
                    vector.wait_ge(kb.sems["st"], t)
                    _li = vector.tensor_add(out=XZ3[:, tc * 48:(tc + 1) * 48],
                                            in0=PX3[:], in1=BINB[:, 1024:1072])
                    _li.then_inc(*kb.inc("sv"))

        kb.dump("gzc", C_AR[:, :])
        s2 = contextlib.ExitStack()
        AT = s2.enter_context(nc.sbuf_tensor(_un("AT"), [16, L], F32))
        PY = s2.enter_context(nc.psum_tensor(_un("PY"), [1, 2048], F32))
        PT48 = s2.enter_context(nc.psum_tensor(_un("PT48"), [16, 384], F32))
        with s2:
            sbc = contextlib.ExitStack()
            BT = sbc.enter_context(nc.sbuf_tensor(_un("BT"), [16, L], F32))
            CT = sbc.enter_context(nc.sbuf_tensor(_un("CT"), [16, L], F32))
            ST = sbc.enter_context(nc.sbuf_tensor(_un("ST"), [16, L], F32))
            sbc.__enter__()
            kb.barrier()
            for tc in range(NT):
                @block.tensor
                def _(tensor, tc=tc):
                    if tc > 0:
                        kb.wait(tensor, "sv")
                    for kk in range(3):
                        _li = tensor.transpose(PT48[:, kk * 128:(kk + 1) * 128],
                                               XZ3[:, tc * 48 + kk * 16:tc * 48 + (kk + 1) * 16],
                                               IDENTF[:])
                    _li.then_inc(*kb.inc("st"))
                t_t = kb.tag("st")
                @block.vector
                def _(vector, tc=tc, t=t_t):
                    vector.wait_ge(kb.sems["st"], t)
                    for kk, dst in enumerate([BT, CT, AT]):
                        _li = vector.tensor_copy(
                            out=dst[:, tc * 128:(tc + 1) * 128],
                            in_=PT48[0:16, kk * 128:(kk + 1) * 128])
                    _li.then_inc(*kb.inc("sv"))
            @block.scalar
            def _(scalar):
                kb.wait(scalar, "sv")
                kb.act(scalar, out=AT[:], in_=AT[:], func=AF.Exp)
                _li = kb.act(scalar, out=AT[:], in_=AT[:], func=AF.Ln, bias=1.0)
                _li.then_inc(*kb.inc("ss"))
            t_sp = kb.tag("ss")
            @block.scalar
            def _(scalar):
                _li = kb.act(scalar, out=S_COL(8)[0:16, :], in_=ALOG_COL, func=AF.Exp)
                _li.then_inc(*kb.inc("ss"))
            t_a = kb.tag("ss")
            @block.vector
            def _(vector, t=t_a):
                vector.wait_ge(kb.sems["ss"], t)
                vector.tensor_scalar(out=S_COL(8)[0:16, :], in0=S_COL(8)[0:16, :],
                                     scalar1=-1.0, scalar2=None, op0=ALU.mult)
                kb.vgap(vector)
                _li = vector.tensor_scalar(out=AT[:], in0=AT[:],
                                     scalar1=S_COL(8)[0:16, :], scalar2=None,
                                     op0=ALU.mult)
                _li.then_inc(*kb.inc("sv"))
            t_ad = kb.tag("sv")
            @block.scalar
            def _(scalar, t=t_ad):
                scalar.wait_ge(kb.sems["sv"], t)
                _li = kb.act(scalar, out=AT[:], in_=AT[:], func=AF.Exp)
                _li.then_inc(*kb.inc("ss"))
            t_aa = kb.tag("ss")
            @block.vector
            def _(vector, t=t_aa):
                vector.wait_ge(kb.sems["ss"], t)
                vector.tensor_tensor_scan(out=ST[:], data0=AT[:],
                                           data1=BT[:], initial=0.0,
                                           op0=ALU.mult, op1=ALU.add)
                _li = vector.tensor_mul(out=AT[:], in0=ST[:], in1=CT[:])
                _li.then_inc(*kb.inc("sv"))
            t_sc = kb.tag("sv")
            kb.dump("b_t", BT[:, :])
            kb.dump("c_t", CT[:, :])
            kb.dump("st_t", ST[:, :])
            sbc.close()
            YROW = s2.enter_context(nc.sbuf_tensor(_un("YROW"), [1, L], F32))
            YCOL = s2.enter_context(nc.sbuf_tensor(_un("YCOL"), [128, NT], F32))
            @block.tensor
            def _(tensor, t=t_sc):
                tensor.wait_ge(kb.sems["sv"], t)
                for off in range(0, L, 512):
                    _li = tensor.matmul(PY[:, off:off + 512], ONESF[0:16, 0:1],
                                  AT[:, off:off + 512], start=True, stop=True)
                _li.then_inc(*kb.inc("st"))
            t_y = kb.tag("st")
            @block.vector
            def _(vector, t=t_y):
                vector.wait_ge(kb.sems["st"], t)
                _li = vector.tensor_copy(out=YROW[:], in_=PY[:])
                _li.then_inc(*kb.inc("sv"))
            t_yr = kb.tag("sv")
            yc_es = contextlib.ExitStack()
            PYC = yc_es.enter_context(nc.psum_tensor(_un("PYC"), [128, 16], F32))
            yc_es.__enter__()
            @block.tensor
            def _(tensor, t=t_yr):
                tensor.wait_ge(kb.sems["sv"], t)
                for tc in range(NT):
                    _li = tensor.transpose(PYC[:, tc:tc + 1],
                                           YROW[0:1, tc * 128:(tc + 1) * 128],
                                           IDENTF[0:1, 0:1])
                _li.then_inc(*kb.inc("st"))
            t_yc = kb.tag("st")
            @block.vector
            def _(vector, t=t_yc):
                vector.wait_ge(kb.sems["st"], t)
                _li = vector.tensor_copy(out=YCOL[:], in_=PYC[:])
                _li.then_inc(*kb.inc("sv"))
            yc_es.close()
            kb.dump("ycol", YCOL[:, :])
            @block.vector
            def _(vector):
                kb.vgap(vector)
                for tc in range(NT):
                    _li = vector.tensor_scalar(out=D_AR[:, tc * 1024:(tc + 1) * 1024],
                                         in0=C_AR[:, tc * 1024:(tc + 1) * 1024],
                                         scalar1=YCOL[:, tc:tc + 1], scalar2=None,
                                         op0=ALU.mult)
                _li.then_inc(*kb.inc("sv"))
            t_y2 = kb.tag("sv")
            kb.dump("y2c", D_AR[:, :])
            tp_es = contextlib.ExitStack()
            TPB2 = tp_es.enter_context(nc.psum_tensor(_un("TPB2"), [128, 1024], BF16))
            with tp_es:
                for tc in range(NT):
                    @block.tensor
                    def _(tensor, tc=tc, t=t_y2):
                        tensor.wait_ge(kb.sems["sv"], t)
                        if tc > 0:
                            tensor.wait_ge(kb.sems["sv"], kb.tag("sv"))
                        for j in range(8):
                            _li = tensor.transpose(TPB2[:, j * 128:(j + 1) * 128],
                                             D_AR[:, tc * 1024 + j * 128:tc * 1024 + (j + 1) * 128],
                                             IDENT[:])
                        _li.then_inc(*kb.inc("st"))
                    t_tp = kb.tag("st")
                    @block.vector
                    def _(vector, tc=tc, t=t_tp):
                        vector.wait_ge(kb.sems["st"], t)
                        _li = vector.tensor_copy(
                            out=C_AR[:, :].rearrange("p (c t) -> p c t", t=L)[
                                :, :, tc * 128:(tc + 1) * 128],
                            in_=TPB2[:].rearrange("p (c t) -> p c t", c=8))
                        _li.then_inc(*kb.inc("sv"))

        s3 = contextlib.ExitStack()
        STG3 = s3.enter_context(nc.sbuf_tensor(_un("STGo"), [128, 1024], F32))
        WUF = s3.enter_context(nc.sbuf_tensor(_un("WUF"), [128, NC * 1024], BF16))
        BUB = s3.enter_context(nc.sbuf_tensor(_un("BUB"), [128, 1024], F32))
        BUROW = s3.enter_context(nc.sbuf_tensor(_un("BUROW"), [1, 1024], F32))
        BUROW = s3.enter_context(nc.sbuf_tensor(_un("BUROW"), [1, 1024], F32))
        PB3 = s3.enter_context(nc.psum_tensor(_un("PB3"), [128, 1024], F32))
        PWU = [s3.enter_context(nc.psum_tensor(_un(f"PWU{i}"), [128, 1024], F32))
               for i in range(2)]
        with s3:
            kb.barrier()
            for g in range(NC):
                tgt = kb.stream(W["woutN"][g * 128:(g + 1) * 128, :], STG3[:], "stgo")
                @block.vector
                def _(vector, g=g, tgt=tgt):
                    vector.wait_ge(kb.sems["sdw"], tgt)
                    _li = vector.tensor_copy(out=WUF[:, g * 1024:(g + 1) * 1024],
                                       in_=STG3[:])
                    _li.then_inc(*kb.inc("sv"))
                kb.release("stgo", "sv")
            t_wu = kb.tag("sv")
            @block.scalar
            def _(scalar):
                scalar.dma_start(out=BUROW[:], in_=W["bout"][None, :]
                                 ).then_inc(*kb.inc("sdx", 16))
            t_bur = kb.tag("sdx")
            @block.tensor
            def _(tensor, t=t_bur):
                tensor.wait_ge(kb.sems["sdx"], t)
                for off in (0, 512):
                    _li = tensor.matmul(PB3[:, off:off + 512], ONESF[0:1, 0:128],
                                  BUROW[0:1, off:off + 512], start=True, stop=True)
                _li.then_inc(*kb.inc("st"))
            t_bu = kb.tag("st")
            @block.vector
            def _(vector, t=t_bu):
                vector.wait_ge(kb.sems["st"], t)
                _li = vector.tensor_copy(out=BUB[:], in_=PB3[:])
                _li.then_inc(*kb.inc("sv"))
            wu_free = [None, None]
            for tc in range(NT):
                pw = PWU[tc % 2]
                @block.tensor
                def _(tensor, tc=tc, pw=pw, t=t_wu, wf=wu_free[tc % 2]):
                    tensor.wait_ge(kb.sems["sv"], t)
                    if wf is not None:
                        tensor.wait_ge(kb.sems["sv"], wf)
                    for g in range(8):
                        for off in (0, 512):
                            _li = tensor.matmul(
                                pw[:, off:off + 512],
                                C_AR[:, g * L + tc * 128:g * L + (tc + 1) * 128],
                                WUF[:, g * 1024 + off:g * 1024 + off + 512],
                                start=(g == 0), stop=(g == 7))
                    _li.then_inc(*kb.inc("st"))
                t_w = kb.tag("st")
                @block.vector
                def _(vector, tc=tc, pw=pw, t=t_w):
                    vector.wait_ge(kb.sems["st"], t)
                    _li = vector.scalar_tensor_tensor(
                        out=D_AR[:, tc * 1024:(tc + 1) * 1024], in0=pw[:],
                        scalar=1.0, in1=BUB[:], op0=ALU.mult, op1=ALU.add)
                    _li.then_inc(*kb.inc("sv"))
                wu_free[tc % 2] = kb.tag("sv")


def emit_wmean(ctx, layer, scope_es):
    nc, block, kb, W = ctx["nc"], ctx["block"], ctx["kb"], ctx["W"]
    BIASC = ctx["BIASC"]
    base = 274 + layer * 4
    acc1, scr1 = BIASC[:, base:base + 1], BIASC[:, base + 1:base + 2]
    acc2, scr2 = BIASC[:, base + 2:base + 3], BIASC[:, base + 3:base + 4]
    w1T, w2T = W[f"w1T{layer}"], W[f"w2T{layer}"]
    MSTG = scope_es.enter_context(
        nc.sbuf_tensor(_un(f"MSTG{layer}"), [128, 2048], F32))
    rels0 = [(kb.sems[n], kb.cnt[n]) for n in ("sv", "ss", "st", "sg")
             if kb.cnt[n] > 0]

    @block.vector
    def _(vector):
        vector.memset(acc1, 0.0)
        _li = vector.memset(acc2, 0.0)
        _li.then_inc(*kb.inc("bar"))
    bar0 = kb.tag("bar")

    srcs = []
    for g in range(NC):
        for hh in range(2):
            srcs.append((w1T[g * 128:(g + 1) * 128, hh * 2048:(hh + 1) * 2048],
                         False))
    for g in range(16):
        srcs.append((w2T.rearrange("(g p) d -> p g d", p=128)[:, g * 2:g * 2 + 2, :],
                     True))
    for i, (src, is_w2) in enumerate(srcs):
        kb.cnt["sdg"] += 16
        tgt = kb.cnt["sdg"]
        rels = rels0 if i == 0 else [(kb.sems["bar"], bar0 + i)]
        dst = MSTG[:].rearrange("p (g d) -> p g d", g=2) if is_w2 else MSTG[:]
        kb.gq.append((src, dst, rels))
        acc, scr = (acc2, scr2) if is_w2 else (acc1, scr1)
        def task(tgt=tgt, acc=acc, scr=scr, MSTG=MSTG):
            @block.vector
            def _(vector):
                vector.wait_ge(kb.sems["sdg"], tgt)
                vector.tensor_reduce(out=scr, in_=MSTG[:], axis=AX.X, op=ALU.add,
                                     apply_absolute_value=True)
                kb.vgap(vector)
                _li = vector.tensor_add(out=acc, in0=acc, in1=scr)
                _li.then_inc(*kb.inc("bar"))
        kb.bg.append(task)
    kb.flush_gq()
    ctx[f"mean{layer}"] = {"bar_w1": bar0 + 16, "bar_w2": bar0 + 32,
                           "acc1": acc1, "acc2": acc2}


def emit_quant_ffn(ctx, layer, D_AR, C_AR):
    nc, block, kb, W = ctx["nc"], ctx["block"], ctx["kb"], ctx["W"]
    XRES, IDENT, IDENTF, ONESF = ctx["XRES"], ctx["IDENT"], ctx["IDENTF"], ctx["ONESF"]
    M_COL, R_COL, S_COL = ctx["M_COL"], ctx["R_COL"], ctx["S_COL"]
    cb_col = ctx["cb_col"]
    BIASC = ctx["BIASC"]
    stage = ctx["stage"]

    fes = contextlib.ExitStack()
    XQ = fes.enter_context(nc.sbuf_tensor(_un(f"XQ{layer}"), [128, NC * L], BF16))
    CVS = fes.enter_context(nc.sbuf_tensor(_un(f"CVS{layer}"), [128, 96], F32))

    def slot(c):
        ar = C_AR if c < 16 else D_AR
        cc = c % 16
        return ar[:, cc * 1024:(cc + 1) * 1024]

    if layer == 1:
        kb.drain_bg()
    with fes:
        q_es = contextlib.ExitStack()
        XNF = q_es.enter_context(nc.sbuf_tensor(_un("XNF"), [128, D], F32))
        I32T = q_es.enter_context(nc.sbuf_tensor(_un("I32T"), [128, D], I32))
        XQT = q_es.enter_context(nc.sbuf_tensor(_un("XQT"), [128, D], BF16))
        TPB = q_es.enter_context(nc.psum_tensor(_un("TPBq"), [128, 1024], BF16))
        with q_es:
            kb.barrier()
            prev_s = [None]
            prev_tp = [None]
            for tc in range(NT):
                @block.vector
                def _(vector, tc=tc, p=prev_s[0]):
                    if p is not None:
                        vector.wait_ge(kb.sems["ss"], p)
                    vector.tensor_sub(out=XNF[:], in0=XRES[:, tc * D:(tc + 1) * D],
                                      in1=D_AR[:, tc * D:(tc + 1) * D])
                    vector.tensor_scalar(out=XNF[:], in0=XNF[:], scalar1=M_COL(tc),
                                         scalar2=R_COL(tc), op0=ALU.subtract,
                                         op1=ALU.mult)
                    vector.tensor_reduce(out=S_COL(0), in_=XNF[:], axis=AX.X,
                                         op=ALU.max, apply_absolute_value=True)
                    kb.vgap(vector)
                    vector.tensor_scalar(out=S_COL(0), in0=S_COL(0), scalar1=1e-5,
                                         scalar2=None, op0=ALU.max)
                    kb.vgap(vector)
                    vector.reciprocal(out=S_COL(1), in_=S_COL(0))
                    kb.vgap(vector)
                    vector.tensor_scalar(out=S_COL(1), in0=S_COL(1), scalar1=127.0,
                                         scalar2=None, op0=ALU.mult)
                    kb.vgap(vector)
                    _li = vector.reciprocal(out=S_COL(2), in_=S_COL(1))
                    _li.then_inc(*kb.inc("sv"))
                t_xn = kb.tag("sv")
                @block.scalar
                def _(scalar, t=t_xn):
                    scalar.wait_ge(kb.sems["sv"], t)
                    _li = kb.act(scalar, out=I32T[:], in_=XNF[:], func=AF.Copy,
                           scale=S_COL(1))
                    _li.then_inc(*kb.inc("ss"))
                prev_s[0] = kb.tag("ss")
                t_i = kb.tag("ss")
                @block.vector
                def _(vector, t=t_i, p=prev_tp[0]):
                    vector.wait_ge(kb.sems["ss"], t)
                    if p is not None:
                        vector.wait_ge(kb.sems["st"], p)
                    _li = vector.tensor_scalar(out=XQT[:], in0=I32T[:], scalar1=S_COL(2),
                                         scalar2=None, op0=ALU.mult)
                    _li.then_inc(*kb.inc("sv"))
                t_xq = kb.tag("sv")
                @block.tensor
                def _(tensor, t=t_xq):
                    tensor.wait_ge(kb.sems["sv"], t)
                    for j in range(8):
                        _li = tensor.transpose(TPB[:, j * 128:(j + 1) * 128],
                                         XQT[:, j * 128:(j + 1) * 128], IDENT[:])
                    _li.then_inc(*kb.inc("st"))
                prev_tp[0] = kb.tag("st")
                t_tp = kb.tag("st")
                @block.vector
                def _(vector, tc=tc, t=t_tp):
                    vector.wait_ge(kb.sems["st"], t)
                    _li = vector.tensor_copy(
                        out=XQ[:, :].rearrange("p (c t) -> p c t", t=L)[
                            :, :, tc * 128:(tc + 1) * 128],
                        in_=TPB[:].rearrange("p (c t) -> p c t", c=8))
                    _li.then_inc(*kb.inc("sv"))
        kb.dump("xq_fm", XQ[:, :])
        if (layer == 0 and stage < 8) or (layer == 1 and stage < 28):
            return

        w1T = W[f"w1T{layer}"]
        w2T = W[f"w2T{layer}"]

        m_es = contextlib.ExitStack()
        MCOL = m_es.enter_context(nc.sbuf_tensor(_un("MCOL"), [128, 4], F32))
        PS1 = m_es.enter_context(nc.psum_tensor(_un("PS1"), [1, 2], F32))
        PBS = m_es.enter_context(nc.psum_tensor(_un("PBS"), [128, 2], F32))
        mean_info = ctx[f"mean{layer}"]
        with m_es:
            kb.barrier()
            @block.vector
            def _(vector, t=mean_info["bar_w2"], acc=mean_info["acc1"]):
                vector.wait_ge(kb.sems["bar"], t)
                kb.vgap(vector)
                _li = vector.tensor_copy(out=MCOL[:, 0:1], in_=acc)
                _li.then_inc(*kb.inc("sv"))
            t_m = kb.tag("sv")
            if layer == 1:
                kb.dump("macc", BIASC[:, 274:282])
            @block.tensor
            def _(tensor, t=t_m):
                tensor.wait_ge(kb.sems["sv"], t)
                _li = tensor.matmul(PS1[:, 0:1], ONESF[:, 0:1], MCOL[:, 0:1],
                              start=True, stop=True)
                _li.then_inc(*kb.inc("st"))
            t_s = kb.tag("st")
            @block.vector
            def _(vector, t=t_s):
                vector.wait_ge(kb.sems["st"], t)
                vector.tensor_scalar(out=MCOL[0:1, 2:3], in0=PS1[0:1, 0:1],
                                     scalar1=1.0 / (DH * D), scalar2=1e-5,
                                     op0=ALU.mult, op1=ALU.max)
                kb.vgap(vector)
                _li = vector.reciprocal(out=MCOL[0:1, 3:4], in_=MCOL[0:1, 2:3])
                _li.then_inc(*kb.inc("sv"))
            t_w = kb.tag("sv")
            @block.tensor
            def _(tensor, t=t_w):
                tensor.wait_ge(kb.sems["sv"], t)
                _li = tensor.matmul(PBS[:, 0:2], ONESF[0:1, 0:128], MCOL[0:1, 2:4],
                              start=True, stop=True)
                _li.then_inc(*kb.inc("st"))
            t_b = kb.tag("st")
            @block.vector
            def _(vector, t=t_b, layer=layer):
                vector.wait_ge(kb.sems["st"], t)
                vector.tensor_copy(out=S_COL(4), in_=PBS[:, 0:1])
                vector.tensor_copy(out=S_COL(5), in_=PBS[:, 1:2])
                kb.vgap(vector)
                _li = vector.tensor_scalar(out=CVS[:],
                                     in0=BIASC[:, 80 + layer * 96:80 + (layer + 1) * 96],
                                     scalar1=S_COL(4), scalar2=None, op0=ALU.mult)
                _li.then_inc(*kb.inc("sv"))

        for tf in range(NTF):
            rm_es = contextlib.ExitStack()
            RMX = rm_es.enter_context(nc.sbuf_tensor(_un("RMX"), [128, TF], BF16))
            RMN = rm_es.enter_context(nc.sbuf_tensor(_un("RMN"), [128, TF], BF16))
            rm_es.__enter__()
            a_es = contextlib.ExitStack()
            STGA = a_es.enter_context(nc.sbuf_tensor(_un("STGA"), [128, 1024], F32))
            W1Q = [a_es.enter_context(nc.sbuf_tensor(_un(f"W1Q{i}"), [128, 1024], BF16))
                   for i in range(2)]
            WI32 = a_es.enter_context(nc.sbuf_tensor(_un("WI32"), [128, 1024], I32))
            CVA = a_es.enter_context(nc.sbuf_tensor(_un("CVA"), [128, TF], F32))
            CVB = a_es.enter_context(nc.sbuf_tensor(_un("CVB"), [128, TF], F32))
            PH = [a_es.enter_context(nc.psum_tensor(_un(f"PH{i}"), [128, 1032], F32))
                  for i in range(2)]
            with a_es:
                kb.barrier()
                @block.vector
                def _(vector):
                    vector.memset(RMX[:], -3.0e38)
                    _li = vector.memset(RMN[:], 3.0e38)
                    _li.then_inc(*kb.inc("sv"))
                t_rm0 = kb.tag("sv")
                ph_free = [None, None]
                w1q_free = [None, None]
                for c in range(32):
                    if c % 2 == 1:
                        kb.drain_bg(1)
                    tgt = kb.stream(
                        w1T.rearrange("(g p) h -> p g h", p=128)[
                            :, :, c * 128:(c + 1) * 128],
                        STGA[:].rearrange("p (g d) -> p g d", g=8), "stga")
                    wi_guard = kb.tag("sv")
                    @block.scalar
                    def _(scalar, tgt=tgt, g=wi_guard):
                        scalar.wait_ge(kb.sems["sdw"], tgt)
                        scalar.wait_ge(kb.sems["sv"], g)
                        _li = kb.act(scalar, out=WI32[:], in_=STGA[:], func=AF.Copy,
                               scale=S_COL(5))
                        _li.then_inc(*kb.inc("ss"))
                    kb.release("stga", "ss")
                    t_wi = kb.tag("ss")
                    @block.vector
                    def _(vector, c=c, t=t_wi, wf=w1q_free[c % 2]):
                        vector.wait_ge(kb.sems["ss"], t)
                        if wf is not None:
                            vector.wait_ge(kb.sems["st"], wf)
                        _li = vector.tensor_scalar(out=W1Q[c % 2][:], in0=WI32[:],
                                             scalar1=-1.0, scalar2=1.0,
                                             op0=ALU.max, op1=ALU.min)
                        _li.then_inc(*kb.inc("sv"))
                    t_wq = kb.tag("sv")
                    ph = PH[c % 2]
                    @block.tensor
                    def _(tensor, tf=tf, c=c, ph=ph, t=t_wq, pf=ph_free[c % 2]):
                        tensor.wait_ge(kb.sems["sv"], t)
                        if pf is not None:
                            tensor.wait_ge(kb.sems["sv"], pf)
                        t0 = tf * TF - 1
                        for (joff, jw) in [(0, 512), (512, 512), (1024, 2)]:
                            lo, w, dst = t0 + joff, jw, joff
                            if lo < 0:
                                lo, w, dst = 0, jw - 1, joff + 1
                            if lo + w > L:
                                w = L - lo
                            if w <= 0:
                                continue
                            for g in range(8):
                                _li = tensor.matmul(
                                    ph[:, dst:dst + w],
                                    W1Q[c % 2][:, g * 128:(g + 1) * 128],
                                    XQ[:, :].rearrange("p (cc t) -> p cc t", t=L)[
                                        :, g, lo:lo + w],
                                    start=(g == 0), stop=(g == 7))
                        _li.then_inc(*kb.inc("st"))
                    w1q_free[c % 2] = kb.tag("st")
                    t_h = kb.tag("st")
                    @block.vector
                    def _(vector, tf=tf, ph=ph, t=t_h):
                        vector.wait_ge(kb.sems["st"], t)
                        if tf == 0:
                            _li = vector.memset(ph[:, 0:1], 0.0)
                        if tf == NTF - 1:
                            _li = vector.memset(ph[:, 1025:1026], 0.0)
                        _li.then_inc(*kb.inc("sv"))
                    t_ed = kb.tag("sv")
                    @block.scalar
                    def _(scalar, layer=layer, c=c, ph=ph, t=t_ed):
                        scalar.wait_ge(kb.sems["sv"], t)
                        _li = kb.act(scalar, out=CVA[:], in_=ph[:, 0:TF],
                                     func=AF.Copy, scale=CVS[:, c:c + 1])
                        _li.then_inc(*kb.inc("ss"))
                    t_c0 = kb.tag("ss")
                    @block.vector
                    def _(vector, c=c, ph=ph, t=t_c0):
                        vector.wait_ge(kb.sems["ss"], t)
                        vector.scalar_tensor_tensor(out=CVB[:], in0=ph[:, 1:TF + 1],
                                                    scalar=CVS[:, 32 + c:33 + c],
                                                    in1=CVA[:], op0=ALU.mult,
                                                    op1=ALU.add)
                        _li = vector.scalar_tensor_tensor(out=CVA[:], in0=ph[:, 2:TF + 2],
                                                    scalar=CVS[:, 64 + c:65 + c],
                                                    in1=CVB[:], op0=ALU.mult,
                                                    op1=ALU.add)
                        _li.then_inc(*kb.inc("sv"))
                    ph_free[c % 2] = kb.tag("sv")
                    t_c2 = kb.tag("sv")
                    @block.scalar
                    def _(scalar, layer=layer, c=c, t=t_c2):
                        scalar.wait_ge(kb.sems["sv"], t)
                        _li = kb.act(scalar, out=slot(c), in_=CVA[:], func=AF.Silu,
                                     bias=cb_col(layer, c))
                        _li.then_inc(*kb.inc("ss"))
                    t_si = kb.tag("ss")
                    @block.vector
                    def _(vector, c=c, t=t_si, t0=t_rm0):
                        vector.wait_ge(kb.sems["ss"], t)
                        vector.tensor_max(out=RMX[:], in0=RMX[:], in1=slot(c))
                        _li = vector.tensor_tensor(out=RMN[:], in0=RMN[:],
                                                   in1=slot(c), op=ALU.min)
                        _li.then_inc(*kb.inc("sv"))
            emit_h_quant(ctx, layer, tf, slot, RMX, RMN)
            rm_es.close()
            emit_w2(ctx, layer, tf, slot, w2T)


def emit_h_quant(ctx, layer, tf, slot, RMX, RMN):
    nc, block, kb = ctx["nc"], ctx["block"], ctx["kb"]
    IDENTF, ONESF, S_COL = ctx["IDENTF"], ctx["ONESF"], ctx["S_COL"]

    t_es = contextlib.ExitStack()
    SCB = t_es.enter_context(nc.sbuf_tensor(_un("SCB"), [128, TF], F32))
    IVB = t_es.enter_context(nc.sbuf_tensor(_un("IVB"), [128, TF], F32))
    SRow = t_es.enter_context(nc.sbuf_tensor(_un("SRow"), [1, 2 * TF], F32))
    SR8 = t_es.enter_context(nc.sbuf_tensor(_un("SR8"), [16, 128], F32))
    QWQ = t_es.enter_context(nc.sbuf_tensor(_un("QWQ"), [128, TF], F32))
    HI32 = t_es.enter_context(nc.sbuf_tensor(_un("HI32"), [128, TF], I32))
    MRT = t_es.enter_context(nc.sbuf_tensor(_un("MRT"), [128, 16], F32))
    PTQ = t_es.enter_context(nc.psum_tensor(_un("PTQ"), [128, 128], F32))
    PT16 = t_es.enter_context(nc.psum_tensor(_un("PT16q"), [16, 128], F32))
    PSB = t_es.enter_context(nc.psum_tensor(_un("PSBq"), [128, 1024], F32))
    with t_es:
        kb.barrier()
        if layer == _DBGL and tf == 0 and "hsil" in kb.dbg_out:
            dst = kb.dbg_out["hsil"]
            @block.scalar
            def _(scalar):
                for c in range(32):
                    scalar.dma_start(out=dst[:, c * 1024:(c + 1) * 1024],
                                     in_=slot(c)).then_inc(*kb.inc("sdx", 16))
                scalar.wait_ge(kb.sems["sdx"], kb.tag("sdx"))
        @block.vector
        def _(vector):
            vector.tensor_scalar(out=QWQ[:], in0=RMN[:], scalar1=-1.0,
                                 scalar2=None, op0=ALU.mult)
            _li = vector.tensor_max(out=QWQ[:], in0=QWQ[:], in1=RMX[:])
            _li.then_inc(*kb.inc("sv"))
        t_ab = kb.tag("sv")
        for b in range(TF // 128):
            @block.tensor
            def _(tensor, b=b, t=t_ab):
                tensor.wait_ge(kb.sems["sv"], t if b == 0 else kb.tag("sv"))
                _li = tensor.transpose(PTQ[:], QWQ[:, b * 128:(b + 1) * 128], IDENTF[:])
                _li.then_inc(*kb.inc("st"))
            t_t = kb.tag("st")
            @block.vector
            def _(vector, b=b, t=t_t):
                vector.wait_ge(kb.sems["st"], t)
                _li = vector.tensor_reduce(out=MRT[:, b:b + 1], in_=PTQ[:], axis=AX.X,
                                     op=ALU.max)
                _li.then_inc(*kb.inc("sv"))
        t_mx = kb.tag("sv")
        @block.vector
        def _(vector, t=t_mx):
            vector.wait_ge(kb.sems["sv"], t)
            kb.vgap(vector)
            vector.tensor_scalar(out=MRT[:, 8:16], in0=MRT[:, 0:8], scalar1=1e-5,
                                 scalar2=None, op0=ALU.max)
            kb.vgap(vector)
            vector.reciprocal(out=MRT[:, 0:8], in_=MRT[:, 8:16])
            kb.vgap(vector)
            vector.tensor_scalar(out=MRT[:, 0:8], in0=MRT[:, 0:8], scalar1=127.0,
                                 scalar2=None, op0=ALU.mult)
            kb.vgap(vector)
            _li = vector.reciprocal(out=MRT[:, 8:16], in_=MRT[:, 0:8])
            _li.then_inc(*kb.inc("sv"))
        t_sc = kb.tag("sv")
        @block.tensor
        def _(tensor, t=t_sc):
            tensor.wait_ge(kb.sems["sv"], t)
            _li = tensor.transpose(PT16[:], MRT[:], IDENTF[:])
            _li.then_inc(*kb.inc("st"))
        t_tr = kb.tag("st")
        @block.vector
        def _(vector, t=t_tr):
            vector.wait_ge(kb.sems["st"], t)
            _li = vector.tensor_copy(out=SR8[:], in_=PT16[:])
            _li.then_inc(*kb.inc("sv"))
        t_sr = kb.tag("sv")
        @block.scalar
        def _(scalar, t=t_sr):
            scalar.wait_ge(kb.sems["sv"], t)
            scalar.dma_start(out=SRow[0:1, 0:TF].rearrange("o (b p) -> o b p", p=128),
                             in_=SR8[0:8, :]).then_inc(*kb.inc("sdx", 16))
            scalar.dma_start(out=SRow[0:1, TF:2 * TF].rearrange("o (b p) -> o b p", p=128),
                             in_=SR8[8:16, :]).then_inc(*kb.inc("sdx", 16))
        t_row = kb.tag("sdx")
        @block.tensor
        def _(tensor, t=t_row):
            tensor.wait_ge(kb.sems["sdx"], t)
            for off in (0, 512):
                _li = tensor.matmul(PSB[:, off:off + 512], ONESF[0:1, 0:128],
                              SRow[0:1, off:off + 512], start=True, stop=True)
            _li.then_inc(*kb.inc("st"))
        t_sb = kb.tag("st")
        @block.vector
        def _(vector, t=t_sb):
            vector.wait_ge(kb.sems["st"], t)
            _li = vector.tensor_copy(out=SCB[:], in_=PSB[:])
            _li.then_inc(*kb.inc("sv"))
        t_scb = kb.tag("sv")
        @block.tensor
        def _(tensor, t=t_scb):
            tensor.wait_ge(kb.sems["sv"], t)
            for off in (0, 512):
                _li = tensor.matmul(PSB[:, off:off + 512], ONESF[0:1, 0:128],
                              SRow[0:1, TF + off:TF + off + 512],
                              start=True, stop=True)
            _li.then_inc(*kb.inc("st"))
        t_ib = kb.tag("st")
        @block.vector
        def _(vector, t=t_ib):
            vector.wait_ge(kb.sems["st"], t)
            _li = vector.tensor_copy(out=IVB[:], in_=PSB[:])
            _li.then_inc(*kb.inc("sv"))
        t_ivb = kb.tag("sv")
        if layer == _DBGL and tf == 0 and "scb" in kb.dbg_out:
            dst2 = kb.dbg_out["scb"]
            @block.scalar
            def _(scalar, t=t_ivb):
                scalar.wait_ge(kb.sems["sv"], t)
                scalar.dma_start(out=dst2[:, 0:1024], in_=SCB[:]
                                 ).then_inc(*kb.inc("sdx", 16))
                scalar.dma_start(out=dst2[:, 1024:2048], in_=IVB[:]
                                 ).then_inc(*kb.inc("sdx", 16))
                scalar.wait_ge(kb.sems["sdx"], kb.tag("sdx"))
        prev_q = [None]
        for c in range(32):
            @block.vector
            def _(vector, c=c, t=t_ivb, p=prev_q[0]):
                vector.wait_ge(kb.sems["sv"], t)
                if p is not None:
                    vector.wait_ge(kb.sems["ss"], p)
                _li = vector.tensor_mul(out=QWQ[:], in0=slot(c), in1=SCB[:])
                _li.then_inc(*kb.inc("sv"))
            t_q1 = kb.tag("sv")
            @block.scalar
            def _(scalar, t=t_q1):
                scalar.wait_ge(kb.sems["sv"], t)
                _li = scalar.copy(out=HI32[:], in_=QWQ[:])
                _li.then_inc(*kb.inc("ss"))
            prev_q[0] = kb.tag("ss")
            t_q2 = kb.tag("ss")
            @block.vector
            def _(vector, c=c, t=t_q2):
                vector.wait_ge(kb.sems["ss"], t)
                _li = vector.tensor_tensor(out=slot(c), in0=HI32[:], in1=IVB[:],
                                     op=ALU.mult)
                _li.then_inc(*kb.inc("sv"))


def emit_w2(ctx, layer, tf, slot, w2T):
    nc, block, kb = ctx["nc"], ctx["block"], ctx["kb"]
    XRES, IDENT, ONESF, S_COL = ctx["XRES"], ctx["IDENT"], ctx["ONESF"], ctx["S_COL"]

    w_es = contextlib.ExitStack()
    STG2 = w_es.enter_context(nc.sbuf_tensor(_un("STG2"), [128, 2048], F32))
    W2C = [w_es.enter_context(nc.sbuf_tensor(_un(f"W2C{i}"), [128, 4096], BF16))
           for i in range(2)]
    WI2 = w_es.enter_context(nc.sbuf_tensor(_un("WI2"), [128, 2048], I32))
    MC2 = w_es.enter_context(nc.sbuf_tensor(_un("MC2"), [128, 4], F32))
    WEV = w_es.enter_context(nc.sbuf_tensor(_un("WEV"), [128, 512], BF16))
    PS2 = w_es.enter_context(nc.psum_tensor(_un("PS2"), [1, 2], F32))
    PBS2 = w_es.enter_context(nc.psum_tensor(_un("PBS2"), [128, 2], F32))
    PW = [w_es.enter_context(nc.psum_tensor(_un(f"PW{i}"), [128, 512], F32))
          for i in range(2)]
    PWT = w_es.enter_context(nc.psum_tensor(_un("PWT"), [128, 512], BF16))
    with w_es:
        kb.barrier()
        if tf == 0:
            mean_info = ctx[f"mean{layer}"]
            @block.vector
            def _(vector, t=mean_info["bar_w2"], acc=mean_info["acc2"]):
                vector.wait_ge(kb.sems["bar"], t)
                kb.vgap(vector)
                _li = vector.tensor_copy(out=MC2[:, 0:1], in_=acc)
                _li.then_inc(*kb.inc("sv"))
            t_m = kb.tag("sv")
            @block.tensor
            def _(tensor, t=t_m):
                tensor.wait_ge(kb.sems["sv"], t)
                _li = tensor.matmul(PS2[:, 0:1], ONESF[:, 0:1], MC2[:, 0:1],
                              start=True, stop=True)
                _li.then_inc(*kb.inc("st"))
            t_s = kb.tag("st")
            @block.vector
            def _(vector, t=t_s):
                vector.wait_ge(kb.sems["st"], t)
                vector.tensor_scalar(out=MC2[0:1, 2:3], in0=PS2[0:1, 0:1],
                                     scalar1=1.0 / (DH * D), scalar2=1e-5,
                                     op0=ALU.mult, op1=ALU.max)
                kb.vgap(vector)
                _li = vector.reciprocal(out=MC2[0:1, 3:4], in_=MC2[0:1, 2:3])
                _li.then_inc(*kb.inc("sv"))
            t_r = kb.tag("sv")
            @block.tensor
            def _(tensor, t=t_r):
                tensor.wait_ge(kb.sems["sv"], t)
                _li = tensor.matmul(PBS2[:, 0:2], ONESF[0:1, 0:128], MC2[0:1, 2:4],
                              start=True, stop=True)
                _li.then_inc(*kb.inc("st"))
            t_b = kb.tag("st")
            @block.vector
            def _(vector, t=t_b):
                vector.wait_ge(kb.sems["st"], t)
                vector.tensor_copy(out=S_COL(6), in_=PBS2[:, 0:1])
                _li = vector.tensor_copy(out=S_COL(7), in_=PBS2[:, 1:2])
                _li.then_inc(*kb.inc("sv"))
        hq_done = kb.tag("sv")
        w2c_free = [None, None]
        for dc in range(8):
            kb.drain_bg(2)
            for hhalf in range(2):
                tgt = kb.stream(
                    w2T.rearrange("(g p) d -> p g d", p=128)[
                        :, hhalf * 16:(hhalf + 1) * 16, dc * 128:(dc + 1) * 128],
                    STG2[:].rearrange("p (g d) -> p g d", g=16), "stg2w")
                wi2_guard = kb.tag("sv")
                @block.scalar
                def _(scalar, tgt=tgt, wf=w2c_free[dc % 2], hhalf=hhalf,
                      g=wi2_guard):
                    scalar.wait_ge(kb.sems["sdw"], tgt)
                    scalar.wait_ge(kb.sems["sv"], g)
                    if wf is not None and hhalf == 0:
                        scalar.wait_ge(kb.sems["st"], wf)
                    _li = kb.act(scalar, out=WI2[:], in_=STG2[:], func=AF.Copy,
                           scale=S_COL(7))
                    _li.then_inc(*kb.inc("ss"))
                kb.release("stg2w", "ss")
                t_wi = kb.tag("ss")
                @block.vector
                def _(vector, dc=dc, hhalf=hhalf, t=t_wi):
                    vector.wait_ge(kb.sems["ss"], t)
                    _li = vector.tensor_scalar(
                        out=W2C[dc % 2][:, hhalf * 2048:(hhalf + 1) * 2048],
                        in0=WI2[:], scalar1=-1.0, scalar2=1.0,
                        op0=ALU.max, op1=ALU.min)
                    _li.then_inc(*kb.inc("sv"))
            t_wq = kb.tag("sv")
            pw_free = [None, None]
            for tp in range(TF // 512):
                pw = PW[tp % 2]
                @block.tensor
                def _(tensor, dc=dc, tp=tp, pw=pw, t=t_wq, hd=hq_done,
                      pf=pw_free[tp % 2]):
                    tensor.wait_ge(kb.sems["sv"], max(t, hd))
                    if pf is not None:
                        tensor.wait_ge(kb.sems["sv"], pf)
                    for cc in range(32):
                        _li = tensor.matmul(
                            pw[:],
                            W2C[dc % 2][:, cc * 128:(cc + 1) * 128],
                            slot(cc)[:, tp * 512:(tp + 1) * 512],
                            start=(cc == 0), stop=(cc == 31))
                    _li.then_inc(*kb.inc("st"))
                t_w = kb.tag("st")
                @block.vector
                def _(vector, pw=pw, t=t_w):
                    vector.wait_ge(kb.sems["st"], t)
                    _li = vector.tensor_copy(out=WEV[:], in_=pw[:])
                    _li.then_inc(*kb.inc("sv"))
                pw_free[tp % 2] = kb.tag("sv")
                t_e = kb.tag("sv")
                for tb in range(4):
                    @block.tensor
                    def _(tensor, tb=tb, t=t_e):
                        tensor.wait_ge(kb.sems["sv"], t if tb == 0 else kb.tag("sv"))
                        _li = tensor.transpose(PWT[:, 0:128],
                                         WEV[:, tb * 128:(tb + 1) * 128], IDENT[:])
                        _li.then_inc(*kb.inc("st"))
                    t_t = kb.tag("st")
                    @block.vector
                    def _(vector, tf=tf, dc=dc, tp=tp, tb=tb, t=t_t):
                        vector.wait_ge(kb.sems["st"], t)
                        tile = tf * (TF // 128) + tp * 4 + tb
                        _li = vector.scalar_tensor_tensor(
                            out=XRES[:, tile * 1024 + dc * 128:tile * 1024 + (dc + 1) * 128],
                            in0=PWT[:, 0:128], scalar=S_COL(6),
                            in1=XRES[:, tile * 1024 + dc * 128:tile * 1024 + (dc + 1) * 128],
                            op0=ALU.mult, op1=ALU.add)
                        _li.then_inc(*kb.inc("sv"))
            w2c_free[dc % 2] = kb.tag("st")


def kernel(**inputs):
    xs = np.ascontiguousarray(np.asarray(inputs["L0_attn_x"], np.float32))
    B = xs.shape[0]

    def cc(a):
        return np.ascontiguousarray(np.asarray(a, np.float32))

    base = {
        "wqT": cc(np.asarray(inputs["L0_attn_Wq"]).T),
        "bq": cc(inputs["L0_attn_bq"]),
        "wkT": cc(np.asarray(inputs["L0_attn_Wk"]).T),
        "bk": cc(inputs["L0_attn_bk"]),
        "wvT": cc(np.asarray(inputs["L0_attn_Wv"]).T),
        "bv": cc(inputs["L0_attn_bv"]),
        "woT": cc(np.asarray(inputs["L0_attn_Wo"]).T),
        "bo": cc(inputs["L0_attn_bo"]),
        "w1T0": cc(np.asarray(inputs["L0_ffn_w1"]).T),
        "cw0": cc(np.asarray(inputs["L0_ffn_conv_w"]).reshape(DH, 3)),
        "cb0": cc(inputs["L0_ffn_conv_b"]),
        "w2T0": cc(np.asarray(inputs["L0_ffn_w2"]).T),
        "winS": cc(np.asarray(inputs["L1_attn_W_in"])[:, D:]),
        "binS": cc(np.asarray(inputs["L1_attn_b_in"])[D:]),
        "alog": cc(inputs["L1_attn_A_log"]),
        "woutN": cc(inputs["L1_attn_W_out"]),
        "bout": cc(inputs["L1_attn_b_out"]),
        "w1T1": cc(np.asarray(inputs["L1_ffn_w1"]).T),
        "cw1": cc(np.asarray(inputs["L1_ffn_conv_w"]).reshape(DH, 3)),
        "cb1": cc(inputs["L1_ffn_conv_b"]),
        "w2T1": cc(np.asarray(inputs["L1_ffn_w2"]).T),
    }
    nc = build(debug=_DEBUG, stage=_STAGE)
    in_maps = [dict(base, x=np.ascontiguousarray(xs[b])) for b in range(B)]
    res = run_bass_kernel_spmd(nc, in_maps, core_ids=list(range(B)))
    out = np.stack([np.asarray(r["y"]) for r in res.results], axis=0)
    if _DEBUG:
        kernel.dbg = [{k: np.asarray(v) for k, v in r.items()
                       if k.startswith("dbg_")} for r in res.results]
    return out
